# version 02
# baseline (speedup 1.0000x reference)
from contextlib import ExitStack

import numpy as np
import ml_dtypes
import concourse.bass as bass
import concourse.mybir as mybir
from concourse.bass_utils import run_bass_kernel_spmd

F32 = mybir.dt.float32
BF16 = mybir.dt.bfloat16
I32 = mybir.dt.int32
AF = mybir.ActivationFunctionType
ALU = mybir.AluOpType
AX = mybir.AxisListType

H = 2048
NH = 16
HD = 128
NB = 4
HT = H // 128
II = 8192
IT = II // 128
EPS = 1e-6
LN_EPS = 1e-5
NCORES = 8


class Sem:
    def __init__(self, nc, name):
        self.h = nc.alloc_semaphore(name=name)
        self.n = 0


def build_nc(S=2048, dbg=False):
    ST = S // 128
    NSUP = S // 512
    NFS = S // 512

    nc = bass.Bass("TRN2", target_bir_lowering=False, debug=False,
                   detect_race_conditions=False)

    x_in = nc.dram_tensor("x", [S, H], F32, kind="ExternalInput").ap()
    wq_in = nc.dram_tensor("wq", [H, H], F32, kind="ExternalInput").ap()
    wk_in = nc.dram_tensor("wk", [H, H], F32, kind="ExternalInput").ap()
    wv_in = nc.dram_tensor("wv", [H, H], F32, kind="ExternalInput").ap()
    wo_in = nc.dram_tensor("wo", [H, H], F32, kind="ExternalInput").ap()
    egw1_in = nc.dram_tensor("egw1", [512, H], F32, kind="ExternalInput").ap()
    egb1_in = nc.dram_tensor("egb1", [512], F32, kind="ExternalInput").ap()
    egw2_in = nc.dram_tensor("egw2", [1, 512], F32, kind="ExternalInput").ap()
    egb2_in = nc.dram_tensor("egb2", [1], F32, kind="ExternalInput").ap()
    mg_in = nc.dram_tensor("mgate", [NH], F32, kind="ExternalInput").ap()
    fm_in = nc.dram_tensor("fmem", [NB, NH, HD, HD], F32, kind="ExternalInput").ap()
    fn_in = nc.dram_tensor("fnorm", [NB, NH, HD], F32, kind="ExternalInput").ap()
    wg_in = nc.dram_tensor("gw", [II, H], I32, kind="ExternalInput").ap()
    sg_in = nc.dram_tensor("gs", [II], F32, kind="ExternalInput").ap()
    wu_in = nc.dram_tensor("uw", [II, H], I32, kind="ExternalInput").ap()
    su_in = nc.dram_tensor("us", [II], F32, kind="ExternalInput").ap()
    wd_in = nc.dram_tensor("dw", [H, II], I32, kind="ExternalInput").ap()
    sd_in = nc.dram_tensor("ds", [H], F32, kind="ExternalInput").ap()
    out = nc.dram_tensor("out", [S, H], F32, kind="ExternalOutput").ap()

    wq_bf = nc.dram_tensor("wq_bf", [H, H], BF16, kind="Internal").ap()
    wk_bf = nc.dram_tensor("wk_bf", [H, H], BF16, kind="Internal").ap()
    wv_bf = nc.dram_tensor("wv_bf", [H, H], BF16, kind="Internal").ap()
    wo_bf = nc.dram_tensor("wo_bf", [H, H], BF16, kind="Internal").ap()
    egw1_bf = nc.dram_tensor("egw1_bf", [512, H], BF16, kind="Internal").ap()
    wg_bf = nc.dram_tensor("wg_bf", [II, H], BF16, kind="Internal").ap()
    wu_bf = nc.dram_tensor("wu_bf", [II, H], BF16, kind="Internal").ap()
    wd_bf = nc.dram_tensor("wd_bf", [H, II], BF16, kind="Internal").ap()
    kscr = "ExternalOutput" if dbg else "Internal"
    xn_scr = nc.dram_tensor("xn_scr", [S, H], BF16, kind=kscr).ap()
    sq_scr = nc.dram_tensor("sq_scr", [H, S], BF16, kind=kscr).ap()
    skT_scr = nc.dram_tensor("skT_scr", [H, S], BF16, kind=kscr).ap()
    oc_scr = nc.dram_tensor("oc_scr", [S, H], BF16, kind=kscr).ap()
    y_scr = nc.dram_tensor("y_scr", [S, H], BF16, kind=kscr).ap()
    x1_scr = nc.dram_tensor("x1_scr", [S, H], F32, kind=kscr).ap()
    xn2_scr = nc.dram_tensor("xn2_scr", [S, H], BF16, kind=kscr).ap()

    ident_c = nc.inline_tensor(np.eye(128, dtype=np.float32), "ident_c").ap()
    maskT_c = nc.inline_tensor(np.triu(np.ones((128, 128), dtype=np.float32)), "maskT_c").ap()
    onc_bf_c = nc.inline_tensor(np.ones((128, 1), dtype=ml_dtypes.bfloat16), "onc_bf_c").ap()
    onc_f_c = nc.inline_tensor(np.ones((128, 1), dtype=np.float32), "onc_f_c").ap()
    onr_f_c = nc.inline_tensor(np.ones((1, 128), dtype=np.float32), "onr_f_c").ap()

    sems = {}
    for nm in ["bar",
               "sXL", "sSQ", "sVAR", "sSQT", "sXN", "sXNst",
               "sSm", "sFM", "sFMV", "sPE0", "sV0", "sS0",
               "sWL", "sWS",
               "sXnT", "sWq", "sQP", "sQX", "sQE", "sCN", "sCV", "sRe", "sSx",
               "sOCst", "sSQstr",
               "sGW", "sCrT", "sGPE", "sGS",
               "sKVw", "sKP", "sKX", "sKE", "sSKst", "sSkp", "sOCld",
               "sPEb", "sVb", "sSb", "sGb", "sYst",
               "sWo", "sYT", "sOP", "sXre", "sRv", "sXN2v", "sSQ2", "sVAR2", "sSQT2",
               "sX1st", "sXN2st",
               "sX2T", "sFW", "sGU", "sSil", "sHA", "sR2", "sDW", "sDP", "sDE",
               "sXR1", "sOut"]:
        sems[nm] = Sem(nc, nm)
    s = sems
    BAR_ENG = 4
    cur_phase = [0]

    def bwait(eng):
        eng.wait_ge(s["bar"].h, BAR_ENG * cur_phase[0])

    def bdone(eng, drain=False):
        if drain:
            eng.drain()
        eng.sem_inc(s["bar"].h, 1)

    def next_phase():
        cur_phase[0] += 1

    top = ExitStack()

    def sb(stack, name, shape, dtype, **kw):
        return stack.enter_context(nc.sbuf_tensor(name, shape, dtype, **kw))

    def psum(stack, name, shape, dtype=F32):
        return stack.enter_context(nc.psum_tensor(name, shape, dtype))

    identf = sb(top, "identf", [128, 128], F32)
    maskT = sb(top, "maskTs", [128, 128], F32)
    onc_bf = sb(top, "onc_bfs", [128, 1], BF16)
    onc_f = sb(top, "onc_fs", [128, 1], F32)
    onr_f = sb(top, "onr_fs", [1, 128], F32)
    sg_col = sb(top, "sg_col", [128, 64], F32)
    su_col = sb(top, "su_col", [128, 64], F32)
    sd_bc = sb(top, "sd_bc", [128, H], F32)
    b2_bc = sb(top, "b2_bc", [128, 1], F32)
    p_col = sb(top, "p_col", [128, ST], F32)

    attn = ExitStack()
    xnT = sb(attn, "xnT", [128, HT * S], BF16, is_transpose_dest=True)
    F_bf = sb(attn, "F_bf", [128, NH * NB * 128], BF16)
    Mp_bf = sb(attn, "Mp_bf", [128, NH * 129], BF16)
    norms_f = sb(attn, "norms_f", [128, 64], F32)
    norms_hm = sb(attn, "norms_hm", [128, 64], BF16)
    z_sb = sb(attn, "z_sb", [128, NH], F32)
    g_bc = sb(attn, "g_bc", [128, NH], F32)
    invg_bc = sb(attn, "invg_bc", [128, NH], F32)
    indz_bc = sb(attn, "indz_bc", [128, 1], F32)
    indb_row = sb(attn, "indb_row", [1, NB], F32)
    bsums = sb(attn, "bsums", [1, NB], F32)
    g_row = sb(attn, "g_row", [1, NH], F32)
    b1_col = sb(attn, "b1_col", [128, 4], F32)
    egw2_bf = sb(attn, "egw2_bf", [128, 4], BF16)
    rcw_all = sb(attn, "rcw_all", [128, NH * ST * NB], F32)

    SMALL_LOADS = 13

    ph = ExitStack()
    xbuf = [sb(ph, f"xb{i}", [128, H], F32) for i in range(2)]
    xnb = [sb(ph, f"xnb{i}", [128, H], BF16) for i in range(2)]
    junk = sb(ph, "junk", [128, H], BF16)
    ssq = [sb(ph, f"ssq{i}", [128, 1], F32) for i in range(2)]
    s1 = [sb(ph, f"s1_{i}", [128, 1], F32) for i in range(2)]
    vtmp = sb(ph, "vtmp", [128, 1], F32)
    mean2 = sb(ph, "mean2", [128, 1], F32)
    var = [sb(ph, f"var{i}", [128, 1], F32) for i in range(2)]
    sqv = [sb(ph, f"sqv{i}", [128, 1], F32) for i in range(2)]
    rstd = [sb(ph, f"rstd{i}", [128, 1], F32) for i in range(2)]
    nbias = [sb(ph, f"nbias{i}", [128, 1], F32) for i in range(2)]
    wstg = [sb(ph, f"wstg{i}", [128, 2048], BF16) for i in range(4)]
    nrm_raw = sb(ph, "nrm_raw", [64, 128], F32)
    sgr = sb(ph, "sgr", [64, 128], F32)
    sur = sb(ph, "sur", [64, 128], F32)
    b1r = sb(ph, "b1r", [4, 128], F32)
    w2r = sb(ph, "w2r", [4, 128], F32)
    mg_row = sb(ph, "mg_row", [1, NH], F32)
    invg_row = sb(ph, "invg_row", [1, NH], F32)
    b2_row = sb(ph, "b2_row", [1, 1], F32)
    sd_row = sb(ph, "sd_row", [1, H], F32)
    indz_f = sb(ph, "indz_f", [1, 1], F32)
    zt = sb(ph, "zt", [1, 1], F32)
    fstage = [sb(ph, f"fstage{i}", [128, 512], F32) for i in range(2)]
    mstage = sb(ph, "mstage", [128, 128], F32)
    egw2f = sb(ph, "egw2f", [128, 4], F32)
    pA = psum(ph, "pA", [128, 64])
    pB = psum(ph, "pB", [128, 512])
    pC = psum(ph, "pC", [128, 16])
    pD = psum(ph, "pD", [1, 16])

    prep_chunks = []
    for r in range(16):
        prep_chunks.append((wq_in[r * 128:(r + 1) * 128, :], wq_bf[r * 128:(r + 1) * 128, :]))
    for r in range(16):
        prep_chunks.append((wk_in[r * 128:(r + 1) * 128, :], wk_bf[r * 128:(r + 1) * 128, :]))
    for r in range(16):
        prep_chunks.append((wv_in[r * 128:(r + 1) * 128, :], wv_bf[r * 128:(r + 1) * 128, :]))
    for r in range(16):
        prep_chunks.append((wo_in[r * 128:(r + 1) * 128, :], wo_bf[r * 128:(r + 1) * 128, :]))
    for r in range(4):
        prep_chunks.append((egw1_in[r * 128:(r + 1) * 128, :], egw1_bf[r * 128:(r + 1) * 128, :]))
    for r in range(64):
        prep_chunks.append((wg_in[r * 128:(r + 1) * 128, :], wg_bf[r * 128:(r + 1) * 128, :]))
    for r in range(64):
        prep_chunks.append((wu_in[r * 128:(r + 1) * 128, :], wu_bf[r * 128:(r + 1) * 128, :]))
    for r in range(16):
        for c in range(4):
            prep_chunks.append((wd_in[r * 128:(r + 1) * 128, c * 2048:(c + 1) * 2048],
                                wd_bf[r * 128:(r + 1) * 128, c * 2048:(c + 1) * 2048]))
    N_PREP = len(prep_chunks)
    PREP_WQ, PREP_WK, PREP_WV, PREP_WO, PREP_EG = 16, 32, 48, 64, 68

    with nc.Block("p0") as blk:
        @blk.gpsimd
        def _(g):
            for ci, (src, dst) in enumerate(prep_chunks):
                buf = wstg[ci % 4]
                if ci >= 4:
                    g.wait_ge(s["sWS"].h, 16 * (ci - 3))
                g.dma_start(buf[:], src).then_inc(s["sWL"].h, 16)
                s["sWL"].n += 16
                g.wait_ge(s["sWL"].h, 16 * (ci + 1))
                g.dma_start(dst, buf[:]).then_inc(s["sWS"].h, 16)
                s["sWS"].n += 16

        @blk.sync
        def _(sy):
            for i in range(ST):
                if i >= 2:
                    sy.wait_ge(s["sXN"].h, i - 1)
                    sy.dma_start(xn_scr[(i - 2) * 128:(i - 1) * 128, :],
                                 xnb[i % 2][:]).then_inc(s["sXNst"].h, 16)
                    s["sXNst"].n += 16
                sy.dma_start(xbuf[i % 2][:], x_in[i * 128:(i + 1) * 128, :]
                             ).then_inc(s["sXL"].h, 16)
                s["sXL"].n += 16
            for i in range(ST - 2, ST):
                sy.wait_ge(s["sXN"].h, i + 1)
                sy.dma_start(xn_scr[i * 128:(i + 1) * 128, :],
                             xnb[i % 2][:]).then_inc(s["sXNst"].h, 16)
                s["sXNst"].n += 16
            bdone(sy)

        @blk.scalar
        def _(sc):
            small = [(ident_c, identf[:]), (maskT_c, maskT[:]),
                     (onc_bf_c, onc_bf[:]), (onc_f_c, onc_f[:]), (onr_f_c, onr_f[:]),
                     (fn_in.rearrange("m h d -> (m h) d"), nrm_raw[:]),
                     (sg_in.rearrange("(a b) -> a b", b=128), sgr[:]),
                     (su_in.rearrange("(a b) -> a b", b=128), sur[:]),
                     (egb1_in.rearrange("(a b) -> a b", b=128), b1r[:]),
                     (egw2_in.rearrange("a (b c) -> (a b) c", c=128), w2r[:]),
                     (mg_in.rearrange("(a b) -> a b", a=1), mg_row[:]),
                     (egb2_in.rearrange("(a b) -> a b", a=1), b2_row[:]),
                     (sd_in.rearrange("(a b) -> a b", a=1), sd_row[:])]
            assert len(small) == SMALL_LOADS
            for src, dst in small:
                sc.dma_start(dst, src).then_inc(s["sSm"].h, 16)
                s["sSm"].n += 16
            sc.wait_ge(s["sSm"].h, 16 * SMALL_LOADS)
            sc.activation(g_row[:], mg_row[:], AF.Sigmoid).then_inc(s["sS0"].h, 1)
            s["sS0"].n += 1
            for h in range(NH):
                if h >= 2:
                    sc.wait_ge(s["sFMV"].h, h - 1)
                for m in range(NB):
                    sc.dma_start(fstage[h % 2][:, m * 128:(m + 1) * 128],
                                 fm_in[m, h]).then_inc(s["sFM"].h, 16)
                    s["sFM"].n += 16
            for i in range(ST):
                sc.wait_ge(s["sXL"].h, 16 * (i + 1))
                if i >= 2:
                    sc.wait_ge(s["sVAR"].h, i - 1)
                sc.activation(junk[:], xbuf[i % 2][:], AF.Square,
                              accum_out=ssq[i % 2][:])
                sc.drain()
                sc.sem_inc(s["sSQ"].h, 1)
                s["sSQ"].n += 1
                sc.wait_ge(s["sVAR"].h, i + 1)
                sc.activation(sqv[i % 2][:], var[i % 2][:], AF.Sqrt
                              ).then_inc(s["sSQT"].h, 1)
                s["sSQT"].n += 1
            bdone(sc, drain=True)

        @blk.tensor
        def _(t):
            t.wait_ge(s["sSm"].h, 16 * SMALL_LOADS)
            t.transpose(pA[:], nrm_raw[:], identf[0:64, 0:64]).then_inc(s["sPE0"].h, 1)
            t.wait_ge(s["sV0"].h, 1)
            t.transpose(pA[:], sgr[:], identf[0:64, 0:64]).then_inc(s["sPE0"].h, 1)
            t.wait_ge(s["sV0"].h, 2)
            t.transpose(pA[:], sur[:], identf[0:64, 0:64]).then_inc(s["sPE0"].h, 1)
            t.wait_ge(s["sV0"].h, 3)
            t.transpose(pA[0:128, 0:4], b1r[:], identf[0:4, 0:4]).then_inc(s["sPE0"].h, 1)
            t.wait_ge(s["sV0"].h, 4)
            t.transpose(pA[0:128, 0:4], w2r[:], identf[0:4, 0:4]).then_inc(s["sPE0"].h, 1)
            for m in range(NB):
                t.wait_ge(s["sV0"].h, 5 + m)
                t.matmul(pD[:], onc_f[:], norms_f[:, m * 16:(m + 1) * 16],
                         start=True, stop=True).then_inc(s["sPE0"].h, 1)
            t.wait_ge(s["sV0"].h, 10)
            t.matmul(pC[:], onr_f[:], g_row[:], start=True, stop=True
                     ).then_inc(s["sPE0"].h, 1)
            t.wait_ge(s["sV0"].h, 11)
            t.matmul(pC[:], onr_f[:], invg_row[:], start=True, stop=True
                     ).then_inc(s["sPE0"].h, 1)
            t.wait_ge(s["sV0"].h, 12)
            t.matmul(pC[0:128, 0:1], onr_f[:], indz_f[:], start=True, stop=True
                     ).then_inc(s["sPE0"].h, 1)
            t.wait_ge(s["sV0"].h, 13)
            t.matmul(pC[0:128, 0:1], onr_f[:], b2_row[:], start=True, stop=True
                     ).then_inc(s["sPE0"].h, 1)
            for b in range(4):
                t.wait_ge(s["sV0"].h, 14 + b)
                t.matmul(pB[:], onr_f[:], sd_row[0:1, b * 512:(b + 1) * 512],
                         start=True, stop=True).then_inc(s["sPE0"].h, 1)
            s["sPE0"].n = 17
            bdone(t)

        @blk.vector
        def _(v):
            v.wait_ge(s["sPE0"].h, 1)
            v.tensor_copy(norms_f[:], pA[:])
            v.tensor_copy(norms_hm.rearrange("p (hh m) -> p hh m", m=NB),
                          pA.rearrange("p (m hh) -> p hh m", hh=NH))
            v.tensor_add(z_sb[:], norms_f[:, 0:16], norms_f[:, 16:32])
            v.drain()
            v.tensor_add(z_sb[:], z_sb[:], norms_f[:, 32:48])
            v.drain()
            v.tensor_add(z_sb[:], z_sb[:], norms_f[:, 48:64]).then_inc(s["sV0"].h, 1)
            v.wait_ge(s["sPE0"].h, 2)
            v.tensor_copy(sg_col[:], pA[:]).then_inc(s["sV0"].h, 1)
            v.wait_ge(s["sPE0"].h, 3)
            v.tensor_copy(su_col[:], pA[:]).then_inc(s["sV0"].h, 1)
            v.wait_ge(s["sPE0"].h, 4)
            v.tensor_copy(b1_col[:], pA[0:128, 0:4]).then_inc(s["sV0"].h, 1)
            v.wait_ge(s["sPE0"].h, 5)
            v.tensor_copy(egw2f[:], pA[0:128, 0:4])
            v.drain()
            v.tensor_copy(egw2_bf[:], egw2f[:]).then_inc(s["sV0"].h, 1)
            for m in range(NB):
                v.wait_ge(s["sPE0"].h, 6 + m)
                v.reduce_sum(bsums[0:1, m:m + 1], pD[:], AX.X).then_inc(s["sV0"].h, 1)
            v.drain()
            v.tensor_scalar(indb_row[:], bsums[:], EPS, None, ALU.is_ge)
            v.reduce_sum(zt[:], bsums[:], AX.X)
            v.drain()
            v.tensor_scalar(indz_f[:], zt[:], EPS, None, ALU.is_ge)
            v.drain()
            v.wait_ge(s["sS0"].h, 1)
            v.tensor_scalar(invg_row[:], g_row[:], -1.0, 1.0, ALU.mult, ALU.add
                            ).then_inc(s["sV0"].h, 1)
            v.wait_ge(s["sPE0"].h, 10)
            v.tensor_copy(g_bc[:], pC[:]).then_inc(s["sV0"].h, 1)
            v.wait_ge(s["sPE0"].h, 11)
            v.tensor_copy(invg_bc[:], pC[:]).then_inc(s["sV0"].h, 1)
            v.wait_ge(s["sPE0"].h, 12)
            v.tensor_copy(indz_bc[:], pC[0:128, 0:1]).then_inc(s["sV0"].h, 1)
            v.wait_ge(s["sPE0"].h, 13)
            v.tensor_copy(b2_bc[:], pC[0:128, 0:1]).then_inc(s["sV0"].h, 1)
            for b in range(4):
                v.wait_ge(s["sPE0"].h, 14 + b)
                v.tensor_copy(sd_bc[:, b * 512:(b + 1) * 512], pB[:]).then_inc(s["sV0"].h, 1)
            s["sV0"].n = 18
            for h in range(NH):
                v.wait_ge(s["sFM"].h, 16 * 4 * (h + 1))
                fs = fstage[h % 2]
                v.tensor_add(mstage[:], fs[:, 0:128], fs[:, 128:256])
                v.tensor_add(mstage[:], mstage[:], fs[:, 256:384])
                v.tensor_add(mstage[:], mstage[:], fs[:, 384:512])
                v.tensor_copy(Mp_bf[:, h * 129:h * 129 + 128], mstage[:])
                v.tensor_copy(Mp_bf[:, h * 129 + 128:h * 129 + 129], z_sb[:, h:h + 1])
                v.tensor_copy(F_bf[:, h * 512:(h + 1) * 512], fs[:]).then_inc(s["sFMV"].h, 1)
                s["sFMV"].n += 1
            for i in range(ST + 1):
                if i < ST:
                    v.wait_ge(s["sXL"].h, 16 * (i + 1))
                    v.reduce_sum(s1[i % 2][:], xbuf[i % 2][:], AX.X)
                if i >= 1:
                    j = i - 1
                    if i == ST:
                        v.drain()
                    v.scalar_tensor_tensor(nbias[j % 2][:], s1[j % 2][:], -1.0 / H,
                                           rstd[j % 2][:], ALU.mult, ALU.mult)
                if i < ST:
                    v.wait_ge(s["sSQ"].h, i + 1)
                    v.tensor_scalar(vtmp[:], ssq[i % 2][:], 1.0 / H, LN_EPS,
                                    ALU.mult, ALU.add)
                    v.scalar_tensor_tensor(mean2[:], s1[i % 2][:], 1.0 / (H * float(H)),
                                           s1[i % 2][:], ALU.mult, ALU.mult)
                if i >= 1:
                    j = i - 1
                    if j >= 2:
                        v.wait_ge(s["sXNst"].h, 16 * (j - 1))
                    if i == ST:
                        v.drain()
                    v.tensor_scalar(xnb[j % 2][:], xbuf[j % 2][:], rstd[j % 2][:, 0:1],
                                    nbias[j % 2][:, 0:1], ALU.mult, ALU.add
                                    ).then_inc(s["sXN"].h, 1)
                    s["sXN"].n += 1
                if i < ST:
                    if i == 0:
                        v.drain()
                    v.tensor_sub(var[i % 2][:], vtmp[:], mean2[:]).then_inc(s["sVAR"].h, 1)
                    s["sVAR"].n += 1
                    v.wait_ge(s["sSQT"].h, i + 1)
                    v.reciprocal(rstd[i % 2][:], sqv[i % 2][:])
            bdone(v, drain=True)
    ph.close()
    next_phase()

    CVH = ST + 2
    CNH = ST + 3

    def cv(h, what, c=0):
        base = h * CVH
        return {"nf": base + 1, "w": base + 2, "oc": base + 2 + c + 1}[what]

    def cn(h, what, c=0):
        base = h * CNH
        return {"nf": base + 1, "rel": base + 2, "wbc": base + 3, "co": base + 3 + c + 1}[what]

    ph = ExitStack()
    wqT = [sb(ph, f"wqT{i}", [128, HT * 512], BF16, is_transpose_dest=True) for i in range(2)]
    sqTg = [sb(ph, f"sqTg{i}", [128, 4 * S], BF16) for i in range(2)]
    ocb = sb(ph, "ocb", [128, S], BF16)
    nf_sb = sb(ph, "nf_sb", [128, NB * ST], F32)
    rc_sb = sb(ph, "rc_sb", [128, NB * ST], F32)
    rel_r = sb(ph, "rel_r", [1, NB], F32)
    rel_m = sb(ph, "rel_m", [1, 1], F32)
    rel_e = sb(ph, "rel_e", [1, NB], F32)
    rel_x = sb(ph, "rel_x", [1, NB], F32)
    rel_s = sb(ph, "rel_s", [1, 1], F32)
    w_row = sb(ph, "w_row", [1, NB], F32)
    w_pre = sb(ph, "w_pre", [1, NB], F32)
    wbc = sb(ph, "wbc", [128, NB], F32)
    ebuf = [sb(ph, f"ebuf{i}", [128, 512], F32) for i in range(2)]
    rbuf = [sb(ph, f"rbuf{i}", [128, 512], F32) for i in range(2)]
    nct = [sb(ph, f"nct{i}", [128, 1], F32) for i in range(2)]
    recc = [sb(ph, f"recc{i}", [128, 1], F32) for i in range(2)]
    pQ = [psum(ph, f"pQ{i}", [128, 512]) for i in range(2)]
    pAC = [psum(ph, f"pAC{i}", [128, 129]) for i in range(2)]
    pNF = psum(ph, "pNF", [128, NB * ST])
    pREL = psum(ph, "pREL", [1, NB])
    pWBC = psum(ph, "pWBC", [128, NB])

    with nc.Block("a2a") as blk:
        @blk.sync
        def _(sy):
            bwait(sy)
            sy.wait_ge(s["sXNst"].h, 16 * ST)
            for ht in range(HT):
                for j in range(NSUP):
                    sy.dma_start(xnT[:, ht * S + j * 512: ht * S + (j + 1) * 512],
                                 xn_scr[j * 512:(j + 1) * 512, ht * 128:(ht + 1) * 128],
                                 transpose=True).then_inc(s["sXnT"].h, 16)
                    s["sXnT"].n += 16
            sy.wait_ge(s["sWS"].h, 16 * PREP_WQ)
            def load_wqT(es):
                for ht in range(HT):
                    sy.dma_start(wqT[es % 2][:, ht * 512:(ht + 1) * 512],
                                 wq_bf[es * 512:(es + 1) * 512, ht * 128:(ht + 1) * 128],
                                 transpose=True).then_inc(s["sWq"].h, 16)
                    s["sWq"].n += 16
            load_wqT(0)
            load_wqT(1)
            for es in range(4):
                for hl in range(4):
                    h = es * 4 + hl
                    sy.wait_ge(s["sCV"].h, cv(h, "oc", ST - 1))
                    sy.dma_start(
                        oc_scr[:, h * 128:(h + 1) * 128].rearrange("(c p) e -> p c e", p=128),
                        ocb.rearrange("p (c e) -> p c e", e=128)).then_inc(s["sOCst"].h, 16)
                    s["sOCst"].n += 16
                sy.wait_ge(s["sQE"].h, (es + 1) * 4 * NSUP)
                sy.dma_start(
                    sq_scr[es * 512:(es + 1) * 512, :].rearrange("(el p) t -> p el t", p=128),
                    sqTg[es % 2].rearrange("p (el t) -> p el t", el=4)).then_inc(s["sSQstr"].h, 16)
                s["sSQstr"].n += 16
                if es + 2 < 4:
                    sy.wait_ge(s["sQP"].h, (es + 1) * 4 * NSUP)
                    load_wqT(es + 2)
            bdone(sy)

        @blk.tensor
        def _(t):
            bwait(t)
            t.wait_ge(s["sXnT"].h, 16 * HT * NSUP)
            for es in range(4):
                t.wait_ge(s["sWq"].h, 16 * HT * (es + 1))
                if es >= 2:
                    t.wait_ge(s["sSQstr"].h, 16 * (es - 1))
                for el in range(4):
                    for j in range(NSUP):
                        gq = es * 4 * NSUP + el * NSUP + j
                        pq = pQ[gq % 2]
                        if gq >= 2:
                            t.wait_ge(s["sQX"].h, gq - 1)
                        mm = None
                        for ht in range(HT):
                            mm = t.matmul(pq[:],
                                          wqT[es % 2][:, ht * 512 + el * 128: ht * 512 + (el + 1) * 128],
                                          xnT[:, ht * S + j * 512: ht * S + (j + 1) * 512],
                                          start=(ht == 0), stop=(ht == HT - 1))
                        mm.then_inc(s["sQP"].h, 1)
                        s["sQP"].n += 1
                for hl in range(4):
                    h = es * 4 + hl
                    t.wait_ge(s["sQE"].h, es * 4 * NSUP + (hl + 1) * NSUP)
                    if h >= 1:
                        t.wait_ge(s["sCV"].h, cv(h - 1, "w"))
                    mm = None
                    for c in range(ST):
                        mm = t.matmul(pNF[:, c * NB:(c + 1) * NB],
                                      sqTg[es % 2][:, hl * S + c * 128:hl * S + (c + 1) * 128],
                                      norms_hm[:, h * NB:(h + 1) * NB],
                                      start=True, stop=True)
                    mm.then_inc(s["sCN"].h, 1)
                    s["sCN"].n += 1
                    t.wait_ge(s["sCV"].h, cv(h, "nf"))
                    mm = None
                    for c in range(ST):
                        mm = t.matmul(pREL[:], onc_f[:], nf_sb[:, c * NB:(c + 1) * NB],
                                      start=(c == 0), stop=(c == ST - 1))
                    mm.then_inc(s["sCN"].h, 1)
                    s["sCN"].n += 1
                    t.wait_ge(s["sCV"].h, cv(h, "w"))
                    t.matmul(pWBC[:], onr_f[:], w_row[:], start=True, stop=True
                             ).then_inc(s["sCN"].h, 1)
                    s["sCN"].n += 1
                    for c in range(ST):
                        k = h * ST + c
                        if k >= 2:
                            h2, c2 = (k - 2) // ST, (k - 2) % ST
                            t.wait_ge(s["sCV"].h, cv(h2, "oc", c2))
                        t.matmul(pAC[c % 2][:],
                                 sqTg[es % 2][:, hl * S + c * 128:hl * S + (c + 1) * 128],
                                 Mp_bf[:, h * 129:(h + 1) * 129],
                                 start=True, stop=True).then_inc(s["sCN"].h, 1)
                        s["sCN"].n += 1
            bdone(t)

        @blk.scalar
        def _(sc):
            bwait(sc)
            for es in range(4):
                for el in range(4):
                    for j in range(NSUP):
                        gq = es * 4 * NSUP + el * NSUP + j
                        pq = pQ[gq % 2]
                        sc.wait_ge(s["sQP"].h, gq + 1)
                        if gq >= 2:
                            sc.wait_ge(s["sQE"].h, gq - 1)
                        sc.activation(ebuf[gq % 2][:], pq[:], AF.Exp)
                        sc.activation(rbuf[gq % 2][:], pq[:], AF.Relu).then_inc(s["sQX"].h, 1)
                        s["sQX"].n += 1
                for hl in range(4):
                    h = es * 4 + hl
                    sc.wait_ge(s["sRe"].h, h + 1)
                    sc.activation(rel_x[:], rel_e[:], AF.Exp).then_inc(s["sSx"].h, 1)
                    s["sSx"].n += 1
            bdone(sc, drain=True)

        @blk.vector
        def _(v):
            bwait(v)
            for es in range(4):
                for el in range(4):
                    for j in range(NSUP):
                        gq = es * 4 * NSUP + el * NSUP + j
                        v.wait_ge(s["sQX"].h, gq + 1)
                        v.scalar_tensor_tensor(
                            sqTg[es % 2][:, (el * S + j * 512):(el * S + (j + 1) * 512)],
                            ebuf[gq % 2][:], 1.0, rbuf[gq % 2][:], ALU.min, ALU.add
                        ).then_inc(s["sQE"].h, 1)
                        s["sQE"].n += 1
                for hl in range(4):
                    h = es * 4 + hl
                    v.wait_ge(s["sCN"].h, cn(h, "nf"))
                    v.tensor_copy(nf_sb[:], pNF[:]).then_inc(s["sCV"].h, 1)
                    s["sCV"].n += 1
                    v.wait_ge(s["sCN"].h, cn(h, "rel"))
                    v.tensor_scalar(rel_r[:], pREL[:], 1.0 / S, None, ALU.mult)
                    v.tensor_scalar(rc_sb[:], nf_sb[:], EPS, None, ALU.max)
                    v.reduce_max(rel_m[:], rel_r[:], AX.X)
                    v.reciprocal(rc_sb[:], rc_sb[:])
                    v.tensor_single_scalar(rel_e[:], rel_r[:], rel_m[0:1, 0:1], ALU.subtract
                                           ).then_inc(s["sRe"].h, 1)
                    s["sRe"].n += 1
                    v.wait_ge(s["sSx"].h, h + 1)
                    v.tensor_mul(w_pre[:], rel_x[:], indb_row[:])
                    v.reduce_sum(rel_s[:], rel_x[:], AX.X)
                    v.drain()
                    v.reciprocal(rel_s[:], rel_s[:])
                    v.drain()
                    v.tensor_single_scalar(w_row[:], w_pre[:], rel_s[0:1, 0:1], ALU.mult
                                           ).then_inc(s["sCV"].h, 1)
                    s["sCV"].n += 1
                    v.wait_ge(s["sCN"].h, cn(h, "wbc"))
                    v.tensor_copy(wbc[:], pWBC[:])
                    v.drain()
                    rc3 = rc_sb.rearrange("p (c m) -> p c m", m=NB)
                    rw3 = rcw_all.rearrange("p (hh c m) -> p hh c m", hh=NH, m=NB)
                    for m in range(NB):
                        v.tensor_scalar(rw3[:, h, :, m], rc3[:, :, m], wbc[:, m:m + 1],
                                        None, ALU.mult)
                    if h >= 1:
                        v.wait_ge(s["sOCst"].h, 16 * h)
                    for c in range(ST + 1):
                        if c < ST:
                            v.wait_ge(s["sCN"].h, cn(h, "co", c))
                            v.tensor_scalar(nct[c % 2][:], pAC[c % 2][:, 128:129], EPS,
                                            None, ALU.max)
                        if c >= 1:
                            if c == ST:
                                v.drain()
                            v.tensor_scalar(ocb[:, (c - 1) * 128:c * 128],
                                            pAC[(c - 1) % 2][:, 0:128],
                                            recc[(c - 1) % 2][:, 0:1], indz_bc[:, 0:1],
                                            ALU.mult, ALU.mult).then_inc(s["sCV"].h, 1)
                            s["sCV"].n += 1
                        if c < ST:
                            if c == 0:
                                v.drain()
                            v.reciprocal(recc[c % 2][:], nct[c % 2][:])
            bdone(v, drain=True)
    ph.close()
    next_phase()

    ph = ExitStack()
    egw1T = sb(ph, "egw1T", [128, 16 * 512], BF16, is_transpose_dest=True)
    crT = [sb(ph, f"crT{i}", [128, 16 * 512], BF16, is_transpose_dest=True) for i in range(2)]
    hid_bf = sb(ph, "hid_bf", [128, 4 * 512], BF16)
    pHh = [psum(ph, f"pHh{i}", [128, 512]) for i in range(2)]
    pGP = psum(ph, "pGP", [128, 4])

    def hid_sgs(k_h):
        return (k_h // 4) * 8 + (k_h % 4) + 1

    with nc.Block("gprob") as blk:
        @blk.sync
        def _(sy):
            bwait(sy)
            sy.wait_ge(s["sWS"].h, 16 * PREP_EG)
            for et in range(HT):
                sy.dma_start(egw1T[:, et * 512:(et + 1) * 512],
                             egw1_bf[0:512, et * 128:(et + 1) * 128],
                             transpose=True).then_inc(s["sGW"].h, 16)
                s["sGW"].n += 16
            sy.wait_ge(s["sOCst"].h, 16 * NH)
            for j in range(NSUP):
                if j >= 2:
                    sy.wait_ge(s["sGPE"].h, (j - 1) * 8)
                for et in range(HT):
                    sy.dma_start(crT[j % 2][:, et * 512:(et + 1) * 512],
                                 oc_scr[j * 512:(j + 1) * 512, et * 128:(et + 1) * 128],
                                 transpose=True).then_inc(s["sCrT"].h, 16)
                    s["sCrT"].n += 16
            bdone(sy)

        @blk.tensor
        def _(t):
            bwait(t)
            t.wait_ge(s["sGW"].h, 16 * HT)
            for j in range(NSUP):
                t.wait_ge(s["sCrT"].h, 16 * HT * (j + 1))
                for ft in range(4):
                    k_h = j * 4 + ft
                    if k_h >= 2:
                        t.wait_ge(s["sGS"].h, hid_sgs(k_h - 2))
                    mm = None
                    for et in range(HT):
                        mm = t.matmul(pHh[ft % 2][:],
                                      egw1T[:, et * 512 + ft * 128: et * 512 + (ft + 1) * 128],
                                      crT[j % 2][:, et * 512:(et + 1) * 512],
                                      start=(et == 0), stop=(et == HT - 1))
                    mm.then_inc(s["sGPE"].h, 1)
                    s["sGPE"].n += 1
                t.wait_ge(s["sGS"].h, j * 8 + 4)
                for cc in range(4):
                    mm = None
                    for ft in range(4):
                        mm = t.matmul(pGP[:, cc:cc + 1],
                                      hid_bf[:, ft * 512 + cc * 128: ft * 512 + (cc + 1) * 128],
                                      egw2_bf[:, ft:ft + 1],
                                      start=(ft == 0), stop=(ft == 3))
                    mm.then_inc(s["sGPE"].h, 1)
                    s["sGPE"].n += 1
            bdone(t)

        @blk.scalar
        def _(sc):
            bwait(sc)
            for j in range(NSUP):
                for ft in range(4):
                    sc.wait_ge(s["sGPE"].h, j * 8 + ft + 1)
                    sc.activation(hid_bf[:, ft * 512:(ft + 1) * 512], pHh[ft % 2][:],
                                  AF.Relu, bias=b1_col[:, ft:ft + 1]).then_inc(s["sGS"].h, 1)
                    s["sGS"].n += 1
                for cc in range(4):
                    c = j * 4 + cc
                    sc.wait_ge(s["sGPE"].h, j * 8 + 4 + cc + 1)
                    sc.activation(p_col[:, c:c + 1], pGP[:, cc:cc + 1], AF.Sigmoid,
                                  bias=b2_bc[:, 0:1]).then_inc(s["sGS"].h, 1)
                    s["sGS"].n += 1
            bdone(sc, drain=True)

        @blk.vector
        def _(v):
            bwait(v)
            bdone(v)
    ph.close()
    next_phase()

    for grp in range(4):
        ph = ExitStack()
        wkT = sb(ph, f"wkT{grp}", [128, HT * 512], BF16, is_transpose_dest=True)
        wvT = sb(ph, f"wvT{grp}", [128, HT * 512], BF16, is_transpose_dest=True)
        sqTb = sb(ph, f"sqTb{grp}", [128, 4 * S], BF16)
        skTg = sb(ph, f"skTg{grp}", [128, 4 * S], BF16)
        vg = sb(ph, f"vg{grp}", [128, 4 * S], BF16)
        skp = sb(ph, f"skp{grp}", [128, 2 * S], BF16, is_transpose_dest=True)
        ocl = sb(ph, f"ocl{grp}", [128, 4 * 128], BF16)
        ybuf = sb(ph, f"ybuf{grp}", [128, S], BF16)
        S_sb = sb(ph, f"S_sb{grp}", [128, 129], F32)
        S_bfs = sb(ph, f"S_bfs{grp}", [128, 129], BF16)
        ebuf2 = [sb(ph, f"eb2_{grp}_{i}", [128, 512], F32) for i in range(2)]
        rbuf2 = [sb(ph, f"rb2_{grp}_{i}", [128, 512], F32) for i in range(2)]
        lo = sb(ph, f"lo{grp}", [128, 128], F32)
        of = sb(ph, f"of{grp}", [128, 128], F32)
        ocf = sb(ph, f"ocf{grp}", [128, 128], F32)
        t1b = sb(ph, f"t1b{grp}", [128, 128], F32)
        memb = sb(ph, f"memb{grp}", [128, 128], F32)
        t2b = sb(ph, f"t2b{grp}", [128, 128], F32)
        mst = sb(ph, f"mst{grp}", [128, 128], BF16)
        dent = sb(ph, f"dent{grp}", [128, 1], F32)
        recl = sb(ph, f"recl{grp}", [128, 1], F32)
        pKV = [psum(ph, f"pKV{grp}_{i}", [128, 512]) for i in range(2)]
        pSC = psum(ph, f"pSC{grp}", [128, 128])
        pNM = [psum(ph, f"pNM{grp}_{i}", [128, 129]) for i in range(2)]
        pDS = psum(ph, f"pDS{grp}", [128, 129])
        pFN = psum(ph, f"pFN{grp}", [128, 512])
        pDEN = psum(ph, f"pDEN{grp}", [128, 1])

        NPRJ = 4 * NSUP + ST
        kp0 = s["sKP"].n
        kx0 = s["sKX"].n
        ke0 = s["sKE"].n
        pe0 = s["sPEb"].n
        vb0 = s["sVb"].n
        sb0 = s["sSb"].n
        gb0 = s["sGb"].n
        ocld0 = s["sOCld"].n
        skp0 = s["sSkp"].n
        skst0 = s["sSKst"].n
        yst0 = s["sYst"].n

        PE_C = 4
        V_C = 6
        G_C = 2

        def pe_scan(hl, c, k):
            return pe0 + (hl * ST + c) * PE_C + k

        def v_scan(hl, c, k):
            return vb0 + (hl * ST + c) * V_C + k

        def sb_cnt(hl, c):
            return sb0 + hl * (ST - 1) + c + 1

        with nc.Block(f"a2b{grp}") as blk:
            @blk.sync
            def _(sy):
                bwait(sy)
                if grp > 0:
                    sy.wait_ge(s["sPEb"].h, pe0)
                sy.wait_ge(s["sWS"].h, 16 * PREP_WV)
                for ht in range(HT):
                    sy.dma_start(wkT[:, ht * 512:(ht + 1) * 512],
                                 wk_bf[grp * 512:(grp + 1) * 512, ht * 128:(ht + 1) * 128],
                                 transpose=True).then_inc(s["sKVw"].h, 16)
                    s["sKVw"].n += 16
                for ht in range(HT):
                    sy.dma_start(wvT[:, ht * 512:(ht + 1) * 512],
                                 wv_bf[grp * 512:(grp + 1) * 512, ht * 128:(ht + 1) * 128],
                                 transpose=True).then_inc(s["sKVw"].h, 16)
                    s["sKVw"].n += 16
                sy.wait_ge(s["sSQstr"].h, 16 * (grp + 1))
                for el in range(4):
                    sy.dma_start(sqTb[:, el * S:(el + 1) * S],
                                 sq_scr[grp * 512 + el * 128:grp * 512 + (el + 1) * 128, :]
                                 ).then_inc(s["sKVw"].h, 16)
                    s["sKVw"].n += 16
                sy.wait_ge(s["sKE"].h, ke0 + 4 * NSUP)
                sy.dma_start(
                    skT_scr[grp * 512:(grp + 1) * 512, :].rearrange("(el p) t -> p el t", p=128),
                    skTg.rearrange("p (el t) -> p el t", el=4)).then_inc(s["sSKst"].h, 16)
                s["sSKst"].n += 16
                sy.wait_ge(s["sSKst"].h, skst0 + 16)

                def load_skp(hl):
                    if hl >= 2:
                        sy.wait_ge(s["sPEb"].h, pe0 + (hl - 1) * ST * PE_C)
                    for c in range(ST):
                        sy.dma_start(
                            skp[:, ((hl % 2) * ST + c) * 128:((hl % 2) * ST + c + 1) * 128],
                            skT_scr[(grp * 4 + hl) * 128:(grp * 4 + hl + 1) * 128,
                                    c * 128:(c + 1) * 128],
                            transpose=True).then_inc(s["sSkp"].h, 16)
                        s["sSkp"].n += 16
                load_skp(0)
                load_skp(1)
                for hl in range(4):
                    sy.wait_ge(s["sVb"].h, v_scan(hl, ST - 1, V_C))
                    h = grp * 4 + hl
                    sy.dma_start(
                        y_scr[:, h * 128:(h + 1) * 128].rearrange("(c p) e -> p c e", p=128),
                        ybuf.rearrange("p (c e) -> p c e", e=128)).then_inc(s["sYst"].h, 16)
                    s["sYst"].n += 16
                    if hl + 2 < 4:
                        load_skp(hl + 2)
                bdone(sy)

            @blk.scalar
            def _(sc):
                bwait(sc)
                if grp > 0:
                    sc.wait_ge(s["sGb"].h, gb0)
                for el in range(4):
                    for j in range(NSUP):
                        gq = el * NSUP + j
                        sc.wait_ge(s["sKP"].h, kp0 + gq + 1)
                        if gq >= 2:
                            sc.wait_ge(s["sKE"].h, ke0 + gq - 1)
                        sc.activation(ebuf2[gq % 2][:], pKV[gq % 2][:], AF.Exp)
                        sc.activation(rbuf2[gq % 2][:], pKV[gq % 2][:], AF.Relu
                                      ).then_inc(s["sKX"].h, 1)
                        s["sKX"].n += 1
                for k0 in range(min(4, 4 * ST)):
                    hl0, c0 = k0 // ST, k0 % ST
                    h0 = grp * 4 + hl0
                    sc.dma_start(ocl[:, (k0 % 4) * 128:((k0 % 4) + 1) * 128],
                                 oc_scr[c0 * 128:(c0 + 1) * 128, h0 * 128:(h0 + 1) * 128]
                                 ).then_inc(s["sOCld"].h, 16)
                    s["sOCld"].n += 16
                for hl in range(4):
                    for c in range(ST):
                        k = hl * ST + c
                        if c < ST - 1:
                            sc.wait_ge(s["sVb"].h, v_scan(hl, c, 4))
                            sc.copy(S_bfs[:], S_sb[:]).then_inc(s["sSb"].h, 1)
                            s["sSb"].n += 1
                        kn = k + 4
                        if kn < 4 * ST:
                            hln, cn_ = kn // ST, kn % ST
                            hn = grp * 4 + hln
                            sc.wait_ge(s["sGb"].h, gb0 + k * 2 + 1)
                            sc.dma_start(ocl[:, (kn % 4) * 128:((kn % 4) + 1) * 128],
                                         oc_scr[cn_ * 128:(cn_ + 1) * 128,
                                                hn * 128:(hn + 1) * 128]
                                         ).then_inc(s["sOCld"].h, 16)
                            s["sOCld"].n += 16
                bdone(sc, drain=True)

            @blk.tensor
            def _(t):
                bwait(t)
                t.wait_ge(s["sKVw"].h, 16 * (2 * HT + 4) * (grp + 1))
                for el in range(4):
                    for j in range(NSUP):
                        gq = el * NSUP + j
                        pq = pKV[gq % 2]
                        if gq >= 2:
                            t.wait_ge(s["sKX"].h, kx0 + gq - 1)
                        mm = None
                        for ht in range(HT):
                            mm = t.matmul(pq[:],
                                          wkT[:, ht * 512 + el * 128: ht * 512 + (el + 1) * 128],
                                          xnT[:, ht * S + j * 512: ht * S + (j + 1) * 512],
                                          start=(ht == 0), stop=(ht == HT - 1))
                        mm.then_inc(s["sKP"].h, 1)
                        s["sKP"].n += 1
                for c in range(ST):
                    gq = 4 * NSUP + c
                    pq = pKV[gq % 2]
                    t.wait_ge(s["sKE"].h, ke0 + gq - 1)
                    mm = None
                    for ht in range(HT):
                        mm = t.matmul(pq[:],
                                      xnT[:, ht * S + c * 128: ht * S + (c + 1) * 128],
                                      wvT[:, ht * 512:(ht + 1) * 512],
                                      start=(ht == 0), stop=(ht == HT - 1))
                    mm.then_inc(s["sKP"].h, 1)
                    s["sKP"].n += 1
                for hl in range(4):
                    h = grp * 4 + hl
                    t.wait_ge(s["sKE"].h, ke0 + NPRJ)
                    t.wait_ge(s["sSkp"].h, skp0 + 16 * (hl + 1) * ST)
                    for c in range(ST):
                        k = hl * ST + c
                        if k >= 1:
                            pc, cc = (k - 1) // ST, (k - 1) % ST
                        if k >= 1:
                            t.wait_ge(s["sVb"].h, v_scan(pc, cc, 1))
                        t.matmul(pSC[:], skTg[:, hl * S + c * 128:hl * S + (c + 1) * 128],
                                 sqTb[:, hl * S + c * 128:hl * S + (c + 1) * 128],
                                 start=True, stop=True).then_inc(s["sPEb"].h, 1)
                        s["sPEb"].n += 1
                        if k >= 1:
                            t.wait_ge(s["sVb"].h, v_scan(pc, cc, 3))
                        mm = None
                        for m in range(NB):
                            mm = t.matmul(pFN[:, m * 128:(m + 1) * 128],
                                          sqTb[:, hl * S + c * 128:hl * S + (c + 1) * 128],
                                          F_bf[:, h * 512 + m * 128: h * 512 + (m + 1) * 128],
                                          start=True, stop=True)
                        mm.then_inc(s["sPEb"].h, 1)
                        s["sPEb"].n += 1
                        t.wait_ge(s["sVb"].h, v_scan(hl, c, 1))
                        if k >= 1:
                            t.wait_ge(s["sVb"].h, v_scan(pc, cc, 2))
                        pn = pNM[c % 2]
                        sqt = sqTb[:, hl * S + c * 128:hl * S + (c + 1) * 128]
                        if c == 0:
                            t.matmul(pn[:, 0:128], mst[:],
                                     vg[:, c * 512 + hl * 128:c * 512 + (hl + 1) * 128],
                                     start=True, stop=True)
                            mm = t.matmul(pDEN[:], mst[:], onc_bf[:],
                                          start=True, stop=True)
                        else:
                            t.matmul(pn[:, 0:128], mst[:],
                                     vg[:, c * 512 + hl * 128:c * 512 + (hl + 1) * 128],
                                     start=True, stop=False)
                            t.matmul(pDEN[:], mst[:], onc_bf[:], start=True, stop=False)
                            t.wait_ge(s["sSb"].h, sb_cnt(hl, c - 1))
                            t.matmul(pn[:, 0:128], sqt, S_bfs[:, 0:128],
                                     start=False, stop=True)
                            mm = t.matmul(pDEN[:], sqt, S_bfs[:, 128:129],
                                          start=False, stop=True)
                        mm.then_inc(s["sPEb"].h, 1)
                        s["sPEb"].n += 1
                        if c >= 1:
                            t.wait_ge(s["sVb"].h, v_scan(hl, c - 1, 4))
                        elif hl >= 1:
                            t.wait_ge(s["sVb"].h, v_scan(hl - 1, ST - 1, 4))
                        t.matmul(pDS[:, 0:128],
                                 skp[:, ((hl % 2) * ST + c) * 128:((hl % 2) * ST + c + 1) * 128],
                                 vg[:, c * 512 + hl * 128:c * 512 + (hl + 1) * 128],
                                 start=True, stop=True)
                        t.matmul(pDS[:, 128:129],
                                 skp[:, ((hl % 2) * ST + c) * 128:((hl % 2) * ST + c + 1) * 128],
                                 onc_bf[:], start=True, stop=True).then_inc(s["sPEb"].h, 1)
                        s["sPEb"].n += 1
                bdone(t)

            @blk.vector
            def _(v):
                bwait(v)
                for el in range(4):
                    for j in range(NSUP):
                        gq = el * NSUP + j
                        v.wait_ge(s["sKX"].h, kx0 + gq + 1)
                        v.scalar_tensor_tensor(
                            skTg[:, (el * S + j * 512):(el * S + (j + 1) * 512)],
                            ebuf2[gq % 2][:], 1.0, rbuf2[gq % 2][:], ALU.min, ALU.add
                        ).then_inc(s["sKE"].h, 1)
                        s["sKE"].n += 1
                for c in range(ST):
                    gq = 4 * NSUP + c
                    v.wait_ge(s["sKP"].h, kp0 + gq + 1)
                    v.tensor_copy(vg[:, c * 512:(c + 1) * 512], pKV[gq % 2][:]
                                  ).then_inc(s["sKE"].h, 1)
                    s["sKE"].n += 1
                for hl in range(4):
                    h = grp * 4 + hl
                    v.memset(S_sb[:], 0.0)
                    for c in range(ST):
                        k = hl * ST + c
                        base = h * ST * NB + c * NB
                        v.wait_ge(s["sPEb"].h, pe_scan(hl, c, 1))
                        if k >= 1:
                            pc, cc = (k - 1) // ST, (k - 1) % ST
                            v.wait_ge(s["sPEb"].h, pe_scan(pc, cc, 3))
                        v.tensor_tensor(mst[:], pSC[:], maskT[:], ALU.mult
                                        ).then_inc(s["sVb"].h, 1)
                        s["sVb"].n += 1
                        v.wait_ge(s["sPEb"].h, pe_scan(hl, c, 3))
                        v.tensor_scalar(dent[:], pDEN[:], EPS, None, ALU.max)
                        v.tensor_scalar(of[:], pFN[:, 0:128], rcw_all[:, base:base + 1],
                                        None, ALU.mult)
                        v.reciprocal(recl[:], dent[:])
                        v.scalar_tensor_tensor(of[:], pFN[:, 128:256],
                                               rcw_all[:, base + 1:base + 2], of[:],
                                               ALU.mult, ALU.add)
                        v.tensor_scalar(lo[:], pNM[c % 2][:, 0:128], recl[:, 0:1],
                                        None, ALU.mult).then_inc(s["sVb"].h, 1)
                        s["sVb"].n += 1
                        v.scalar_tensor_tensor(of[:], pFN[:, 256:384],
                                               rcw_all[:, base + 2:base + 3], of[:],
                                               ALU.mult, ALU.add)
                        v.scalar_tensor_tensor(of[:], pFN[:, 384:512],
                                               rcw_all[:, base + 3:base + 4], of[:],
                                               ALU.mult, ALU.add).then_inc(s["sVb"].h, 1)
                        s["sVb"].n += 1
                        v.wait_ge(s["sPEb"].h, pe_scan(hl, c, 4))
                        v.tensor_tensor(S_sb[:], pDS[:], S_sb[:], ALU.add
                                        ).then_inc(s["sVb"].h, 1)
                        s["sVb"].n += 1
                        v.wait_ge(s["sGb"].h, gb0 + k * G_C + 1)
                        v.scalar_tensor_tensor(memb[:], t1b[:], p_col[:, c:c + 1], ocf[:],
                                               ALU.mult, ALU.add).then_inc(s["sVb"].h, 1)
                        s["sVb"].n += 1
                        v.wait_ge(s["sGb"].h, gb0 + k * G_C + 2)
                        if c == 0:
                            if hl == 0:
                                if grp > 0:
                                    v.wait_ge(s["sYst"].h, yst0)
                            else:
                                v.wait_ge(s["sYst"].h, yst0 + 16 * hl)
                        v.scalar_tensor_tensor(ybuf[:, c * 128:(c + 1) * 128], t2b[:],
                                               g_bc[:, h:h + 1], lo[:], ALU.mult, ALU.add
                                               ).then_inc(s["sVb"].h, 1)
                        s["sVb"].n += 1
                bdone(v, drain=True)

            @blk.gpsimd
            def _(g):
                for hl in range(4):
                    for c in range(ST):
                        k = hl * ST + c
                        g.wait_ge(s["sOCld"].h, ocld0 + 16 * (k + 1))
                        g.wait_ge(s["sVb"].h, v_scan(hl, c, 3))
                        g.tensor_copy(ocf[:], ocl[:, (k % 4) * 128:((k % 4) + 1) * 128])
                        g.tensor_sub(t1b[:], of[:], ocf[:]).then_inc(s["sGb"].h, 1)
                        s["sGb"].n += 1
                        g.wait_ge(s["sVb"].h, v_scan(hl, c, 5))
                        g.wait_ge(s["sVb"].h, v_scan(hl, c, 2))
                        if k >= 1:
                            pc, cc = (k - 1) // ST, (k - 1) % ST
                            g.wait_ge(s["sVb"].h, v_scan(pc, cc, 6))
                        g.tensor_sub(t2b[:], memb[:], lo[:]).then_inc(s["sGb"].h, 1)
                        s["sGb"].n += 1
        ph.close()
        next_phase()
    attn.close()

    ph = ExitStack()
    woT = sb(ph, "woT", [128, HT * 2048], BF16, is_transpose_dest=True)
    yT = [sb(ph, f"yT{i}", [128, HT * 512], BF16, is_transpose_dest=True) for i in range(2)]
    xre = [sb(ph, f"xre{i}", [128, H], F32) for i in range(2)]
    x1t = [sb(ph, f"x1t{i}", [128, H], F32) for i in range(2)]
    xn2t = [sb(ph, f"xn2t{i}", [128, H], BF16) for i in range(2)]
    junk2 = sb(ph, "junk2", [128, H], BF16)
    ssq2 = [sb(ph, f"ssq2{i}", [128, 1], F32) for i in range(2)]
    s1b = [sb(ph, f"s1b{i}", [128, 1], F32) for i in range(2)]
    vtmp2 = sb(ph, "vtmp2", [128, 1], F32)
    mean2b = sb(ph, "mean2b", [128, 1], F32)
    varb = [sb(ph, f"varb{i}", [128, 1], F32) for i in range(2)]
    sqvb = [sb(ph, f"sqvb{i}", [128, 1], F32) for i in range(2)]
    rstdb = [sb(ph, f"rstdb{i}", [128, 1], F32) for i in range(2)]
    nbiasb = [sb(ph, f"nbiasb{i}", [128, 1], F32) for i in range(2)]
    pO = [psum(ph, f"pO{i}", [128, 2048]) for i in range(2)]

    with nc.Block("a3") as blk:
        @blk.sync
        def _(sy):
            bwait(sy)
            sy.wait_ge(s["sWS"].h, 16 * PREP_WO)
            for et in range(HT):
                for hob in range(4):
                    sy.dma_start(woT[:, et * 2048 + hob * 512: et * 2048 + (hob + 1) * 512],
                                 wo_bf[hob * 512:(hob + 1) * 512, et * 128:(et + 1) * 128],
                                 transpose=True).then_inc(s["sWo"].h, 16)
                    s["sWo"].n += 16
            sy.wait_ge(s["sYst"].h, 16 * NH)

            def load_yT(j):
                if j >= 2:
                    sy.wait_ge(s["sOP"].h, (j - 1) * 4)
                for et in range(HT):
                    sy.dma_start(yT[j % 2][:, et * 512:(et + 1) * 512],
                                 y_scr[j * 512:(j + 1) * 512, et * 128:(et + 1) * 128],
                                 transpose=True).then_inc(s["sYT"].h, 16)
                    s["sYT"].n += 16
            load_yT(0)
            if NSUP > 1:
                load_yT(1)
            for i in range(ST):
                sy.wait_ge(s["sRv"].h, i + 1)
                sy.wait_ge(s["sXN2v"].h, i + 1)
                sy.dma_start(x1_scr[i * 128:(i + 1) * 128, :], x1t[i % 2][:]
                             ).then_inc(s["sX1st"].h, 16)
                s["sX1st"].n += 16
                sy.dma_start(xn2_scr[i * 128:(i + 1) * 128, :], xn2t[i % 2][:]
                             ).then_inc(s["sXN2st"].h, 16)
                s["sXN2st"].n += 16
                if i % 4 == 3 and (i // 4) + 2 < NSUP:
                    load_yT((i // 4) + 2)
            bdone(sy)

        @blk.scalar
        def _(sc):
            bwait(sc)
            for i in range(min(2, ST)):
                sc.dma_start(xre[i % 2][:], x_in[i * 128:(i + 1) * 128, :]
                             ).then_inc(s["sXre"].h, 16)
                s["sXre"].n += 16
            for i in range(ST):
                sc.wait_ge(s["sRv"].h, i + 1)
                if i >= 2:
                    sc.wait_ge(s["sVAR2"].h, i - 1)
                sc.activation(junk2[:], x1t[i % 2][:], AF.Square,
                              accum_out=ssq2[i % 2][:])
                sc.drain()
                sc.sem_inc(s["sSQ2"].h, 1)
                s["sSQ2"].n += 1
                sc.wait_ge(s["sVAR2"].h, i + 1)
                sc.activation(sqvb[i % 2][:], varb[i % 2][:], AF.Sqrt
                              ).then_inc(s["sSQT2"].h, 1)
                s["sSQT2"].n += 1
                if i + 2 < ST:
                    sc.dma_start(xre[i % 2][:], x_in[(i + 2) * 128:(i + 3) * 128, :]
                                 ).then_inc(s["sXre"].h, 16)
                    s["sXre"].n += 16
            bdone(sc, drain=True)

        @blk.tensor
        def _(t):
            bwait(t)
            t.wait_ge(s["sWo"].h, 16 * HT * 4)
            for j in range(NSUP):
                t.wait_ge(s["sYT"].h, 16 * HT * (j + 1))
                for cc in range(4):
                    c = j * 4 + cc
                    if c >= 2:
                        t.wait_ge(s["sRv"].h, c - 1)
                    mm = None
                    for et in range(HT):
                        for hob in range(4):
                            mm = t.matmul(pO[c % 2][:, hob * 512:(hob + 1) * 512],
                                          yT[j % 2][:, et * 512 + cc * 128: et * 512 + (cc + 1) * 128],
                                          woT[:, et * 2048 + hob * 512: et * 2048 + (hob + 1) * 512],
                                          start=(et == 0), stop=(et == HT - 1))
                    mm.then_inc(s["sOP"].h, 1)
                    s["sOP"].n += 1
            bdone(t)

        @blk.vector
        def _(v):
            bwait(v)
            for i in range(ST + 1):
                if i < ST:
                    v.wait_ge(s["sOP"].h, i + 1)
                    v.wait_ge(s["sXre"].h, 16 * (i + 1))
                    if i >= 2:
                        v.wait_ge(s["sX1st"].h, 16 * (i - 1))
                        v.wait_ge(s["sSQ2"].h, i - 1)
                        v.wait_ge(s["sXN2v"].h, i - 1)
                    v.tensor_tensor(x1t[i % 2][:], pO[i % 2][:], xre[i % 2][:], ALU.add
                                    ).then_inc(s["sRv"].h, 1)
                    s["sRv"].n += 1
                    v.reduce_sum(s1b[i % 2][:], x1t[i % 2][:], AX.X)
                if i >= 1:
                    j = i - 1
                    if i == ST:
                        v.drain()
                    v.scalar_tensor_tensor(nbiasb[j % 2][:], s1b[j % 2][:], -1.0 / H,
                                           rstdb[j % 2][:], ALU.mult, ALU.mult)
                if i < ST:
                    v.wait_ge(s["sSQ2"].h, i + 1)
                    v.tensor_scalar(vtmp2[:], ssq2[i % 2][:], 1.0 / H, LN_EPS,
                                    ALU.mult, ALU.add)
                    v.scalar_tensor_tensor(mean2b[:], s1b[i % 2][:], 1.0 / (H * float(H)),
                                           s1b[i % 2][:], ALU.mult, ALU.mult)
                if i >= 1:
                    j = i - 1
                    if j >= 2:
                        v.wait_ge(s["sXN2st"].h, 16 * (j - 1))
                    if i == ST:
                        v.drain()
                    v.tensor_scalar(xn2t[j % 2][:], x1t[j % 2][:], rstdb[j % 2][:, 0:1],
                                    nbiasb[j % 2][:, 0:1], ALU.mult, ALU.add
                                    ).then_inc(s["sXN2v"].h, 1)
                    s["sXN2v"].n += 1
                if i < ST:
                    if i == 0:
                        v.drain()
                    v.tensor_sub(varb[i % 2][:], vtmp2[:], mean2b[:]).then_inc(s["sVAR2"].h, 1)
                    s["sVAR2"].n += 1
                    v.wait_ge(s["sSQT2"].h, i + 1)
                    v.reciprocal(rstdb[i % 2][:], sqvb[i % 2][:])
            bdone(v, drain=True)
    ph.close()
    next_phase()

    ph = ExitStack()
    xn2T = sb(ph, "xn2T", [128, HT * 512], BF16, is_transpose_dest=True)
    hact = sb(ph, "hact", [128, IT * 512], BF16)
    wgT = [sb(ph, f"wgT{i}", [128, HT * 512], BF16, is_transpose_dest=True) for i in range(2)]
    wuT = [sb(ph, f"wuT{i}", [128, HT * 512], BF16, is_transpose_dest=True) for i in range(2)]
    wdT = [sb(ph, f"wdT{i}", [128, 512], BF16, is_transpose_dest=True) for i in range(4)]
    gs_t = [sb(ph, f"gs_t{i}", [128, 512], F32) for i in range(2)]
    x1l = sb(ph, "x1l", [128, 4 * 2048], F32)
    rr = sb(ph, "rr", [128, 4], F32)
    outb = [sb(ph, f"outb{i}", [128, 512], F32) for i in range(2)]
    pGf = psum(ph, "pGf", [128, 512])
    pUf = psum(ph, "pUf", [128, 512])
    pDN = [psum(ph, f"pDN{i}", [128, 512]) for i in range(4)]
    pR2 = psum(ph, "pR2", [128, 4])

    with nc.Block("ffn") as blk:
        @blk.sync
        def _(sy):
            bwait(sy)
            sy.wait_ge(s["sWS"].h, 16 * N_PREP)
            for sbk in range(NFS):
                sy.wait_ge(s["sXN2st"].h, 16 * 4 * (sbk + 1))
                if sbk >= 1:
                    sy.wait_ge(s["sGU"].h, sbk * 2 * IT)
                for ht in range(HT):
                    sy.dma_start(xn2T[:, ht * 512:(ht + 1) * 512],
                                 xn2_scr[sbk * 512:(sbk + 1) * 512, ht * 128:(ht + 1) * 128],
                                 transpose=True).then_inc(s["sX2T"].h, 16)
                    s["sX2T"].n += 16
                for isup in range(16):
                    gi = sbk * 16 + isup
                    if gi >= 2:
                        sy.wait_ge(s["sGU"].h, (gi - 1) * 8)
                    for ht in range(HT):
                        sy.dma_start(wgT[gi % 2][:, ht * 512:(ht + 1) * 512],
                                     wg_bf[isup * 512:(isup + 1) * 512, ht * 128:(ht + 1) * 128],
                                     transpose=True).then_inc(s["sFW"].h, 16)
                        s["sFW"].n += 16
                    for ht in range(HT):
                        sy.dma_start(wuT[gi % 2][:, ht * 512:(ht + 1) * 512],
                                     wu_bf[isup * 512:(isup + 1) * 512, ht * 128:(ht + 1) * 128],
                                     transpose=True).then_inc(s["sFW"].h, 16)
                        s["sFW"].n += 16
                for hob in range(4):
                    for it in range(IT):
                        k = (sbk * 4 + hob) * IT + it
                        if k >= 4:
                            sy.wait_ge(s["sDP"].h, k - 3)
                        sy.dma_start(wdT[it % 4][:],
                                     wd_bf[hob * 512:(hob + 1) * 512, it * 128:(it + 1) * 128],
                                     transpose=True).then_inc(s["sDW"].h, 16)
                        s["sDW"].n += 16
            bdone(sy)

        @blk.scalar
        def _(sc):
            bwait(sc)
            for sbk in range(NFS):
                sc.wait_ge(s["sX1st"].h, 16 * 4 * (sbk + 1))
                if sbk >= 1:
                    sc.wait_ge(s["sDE"].h, sbk * 16)
                for ss in range(4):
                    sc.dma_start(x1l[:, ss * 2048:(ss + 1) * 2048],
                                 x1_scr[(sbk * 4 + ss) * 128:(sbk * 4 + ss + 1) * 128, :]
                                 ).then_inc(s["sXR1"].h, 16)
                    s["sXR1"].n += 16
                for it in range(IT):
                    gi = sbk * IT + it
                    sc.wait_ge(s["sGU"].h, gi * 2 + 1)
                    if gi >= 2:
                        sc.wait_ge(s["sHA"].h, gi - 1)
                    sc.activation(gs_t[it % 2][:], pGf[:], AF.Silu,
                                  scale=sg_col[:, it:it + 1]).then_inc(s["sSil"].h, 1)
                    s["sSil"].n += 1
                for hob in range(4):
                    for ss in range(4):
                        k = hob * 4 + ss
                        sc.wait_ge(s["sDE"].h, sbk * 16 + k + 1)
                        sc.dma_start(out[(sbk * 4 + ss) * 128:(sbk * 4 + ss + 1) * 128,
                                         hob * 512:(hob + 1) * 512],
                                     outb[k % 2][:]).then_inc(s["sOut"].h, 16)
                        s["sOut"].n += 16
            sc.wait_ge(s["sOut"].h, s["sOut"].n)
            bdone(sc, drain=True)

        @blk.tensor
        def _(t):
            bwait(t)
            for sbk in range(NFS):
                t.wait_ge(s["sX2T"].h, 16 * HT * (sbk + 1))
                for it in range(IT):
                    gi = sbk * IT + it
                    isup, itl = it // 4, it % 4
                    t.wait_ge(s["sFW"].h, 16 * 2 * HT * (sbk * 16 + isup + 1))
                    if gi >= 1:
                        t.wait_ge(s["sSil"].h, gi)
                    mm = None
                    for ht in range(HT):
                        mm = t.matmul(pGf[:],
                                      wgT[(sbk * 16 + isup) % 2][:, ht * 512 + itl * 128: ht * 512 + (itl + 1) * 128],
                                      xn2T[:, ht * 512:(ht + 1) * 512],
                                      start=(ht == 0), stop=(ht == HT - 1))
                    mm.then_inc(s["sGU"].h, 1)
                    s["sGU"].n += 1
                    if gi >= 1:
                        t.wait_ge(s["sHA"].h, gi)
                    mm = None
                    for ht in range(HT):
                        mm = t.matmul(pUf[:],
                                      wuT[(sbk * 16 + isup) % 2][:, ht * 512 + itl * 128: ht * 512 + (itl + 1) * 128],
                                      xn2T[:, ht * 512:(ht + 1) * 512],
                                      start=(ht == 0), stop=(ht == HT - 1))
                    mm.then_inc(s["sGU"].h, 1)
                    s["sGU"].n += 1
                t.wait_ge(s["sHA"].h, (sbk + 1) * IT)
                for ss in range(4):
                    mm = None
                    for it in range(IT):
                        mm = t.matmul(pR2[:, ss:ss + 1],
                                      hact[:, it * 512 + ss * 128: it * 512 + (ss + 1) * 128],
                                      onc_bf[:], start=(it == 0), stop=(it == IT - 1))
                    mm.then_inc(s["sR2"].h, 1)
                    s["sR2"].n += 1
                for hob in range(4):
                    for it in range(IT):
                        k = (sbk * 4 + hob) * IT + it
                        t.wait_ge(s["sDW"].h, 16 * (k + 1))
                        if hob >= 1 and it == 0:
                            t.wait_ge(s["sDE"].h, sbk * 16 + hob * 4)
                        mm = None
                        for ss in range(4):
                            mm = t.matmul(pDN[ss][:],
                                          hact[:, it * 512 + ss * 128: it * 512 + (ss + 1) * 128],
                                          wdT[it % 4][:],
                                          start=(it == 0), stop=(it == IT - 1))
                        mm.then_inc(s["sDP"].h, 1)
                        s["sDP"].n += 1
                if sbk < NFS - 1:
                    t.wait_ge(s["sDE"].h, (sbk + 1) * 16)
            bdone(t)

        @blk.vector
        def _(v):
            bwait(v)
            for sbk in range(NFS):
                for it in range(IT):
                    gi = sbk * IT + it
                    v.wait_ge(s["sSil"].h, gi + 1)
                    v.wait_ge(s["sGU"].h, gi * 2 + 2)
                    v.scalar_tensor_tensor(hact[:, it * 512:(it + 1) * 512],
                                           pUf[:], su_col[:, it:it + 1],
                                           gs_t[it % 2][:], ALU.mult, ALU.mult
                                           ).then_inc(s["sHA"].h, 1)
                    s["sHA"].n += 1
                v.wait_ge(s["sR2"].h, (sbk + 1) * 4)
                v.tensor_scalar(rr[:], pR2[:], 128.0, None, ALU.mult)
                v.drain()
                for hob in range(4):
                    v.wait_ge(s["sDP"].h, (sbk * 4 + hob + 1) * IT)
                    v.wait_ge(s["sXR1"].h, 16 * 4 * (sbk + 1))
                    for ss in range(4):
                        k = hob * 4 + ss
                        if sbk * 16 + k >= 2:
                            v.wait_ge(s["sOut"].h, 16 * (sbk * 16 + k - 1))
                        v.scalar_tensor_tensor(outb[k % 2][:], pDN[ss][:], rr[:, ss:ss + 1],
                                               sd_bc[:, hob * 512:(hob + 1) * 512],
                                               ALU.subtract, ALU.mult)
                        v.tensor_tensor(outb[k % 2][:], outb[k % 2][:],
                                        x1l[:, ss * 2048 + hob * 512: ss * 2048 + (hob + 1) * 512],
                                        ALU.add).then_inc(s["sDE"].h, 1)
                        s["sDE"].n += 1
            bdone(v, drain=True)
    ph.close()
    top.close()
    return nc


_NC_CACHE = {}


def get_nc(S=2048):
    if S not in _NC_CACHE:
        _NC_CACHE[S] = build_nc(S)
    return _NC_CACHE[S]


def kernel(**inputs):
    x = np.ascontiguousarray(inputs["L0_attn_hidden_states"])
    nc = get_nc(2048)
    common = {
        "wq": np.ascontiguousarray(inputs["L0_attn_w_q"]),
        "wk": np.ascontiguousarray(inputs["L0_attn_w_k"]),
        "wv": np.ascontiguousarray(inputs["L0_attn_w_v"]),
        "wo": np.ascontiguousarray(inputs["L0_attn_w_o"]),
        "egw1": np.ascontiguousarray(inputs["L0_attn_eg_w1"]),
        "egb1": np.ascontiguousarray(inputs["L0_attn_eg_b1"]),
        "egw2": np.ascontiguousarray(inputs["L0_attn_eg_w2"]),
        "egb2": np.ascontiguousarray(inputs["L0_attn_eg_b2"]),
        "mgate": np.ascontiguousarray(inputs["L0_attn_memory_gate"]),
        "fmem": np.ascontiguousarray(inputs["L0_attn_fine_memories"]),
        "fnorm": np.ascontiguousarray(inputs["L0_attn_fine_memory_norms"]),
        "gw": np.ascontiguousarray(inputs["L0_ffn_gate_w"]),
        "gs": np.ascontiguousarray(inputs["L0_ffn_gate_scale"]),
        "uw": np.ascontiguousarray(inputs["L0_ffn_up_w"]),
        "us": np.ascontiguousarray(inputs["L0_ffn_up_scale"]),
        "dw": np.ascontiguousarray(inputs["L0_ffn_down_w"]),
        "ds": np.ascontiguousarray(inputs["L0_ffn_down_scale"]),
    }
    in_maps = []
    for b in range(NCORES):
        m = dict(common)
        m["x"] = np.ascontiguousarray(x[b])
        in_maps.append(m)
    res = run_bass_kernel_spmd(nc, in_maps, core_ids=list(range(NCORES)))
    return np.stack([res.results[b]["out"] for b in range(NCORES)], axis=0)


# version 03
# speedup vs baseline: 1.0020x; 1.0020x over previous
from contextlib import ExitStack

import numpy as np
import ml_dtypes
import concourse.bass as bass
import concourse.mybir as mybir
from concourse.bass_utils import run_bass_kernel_spmd

F32 = mybir.dt.float32
BF16 = mybir.dt.bfloat16
I32 = mybir.dt.int32
AF = mybir.ActivationFunctionType
ALU = mybir.AluOpType
AX = mybir.AxisListType

H = 2048
NH = 16
HD = 128
NB = 4
HT = H // 128
II = 8192
IT = II // 128
EPS = 1e-6
LN_EPS = 1e-5
NCORES = 8


class Sem:
    def __init__(self, nc, name):
        self.h = nc.alloc_semaphore(name=name)
        self.n = 0


def build_nc(S=2048, dbg=False):
    ST = S // 128
    NSUP = S // 512
    NFS = S // 512

    nc = bass.Bass("TRN2", target_bir_lowering=False, debug=False,
                   detect_race_conditions=False)

    x_in = nc.dram_tensor("x", [S, H], F32, kind="ExternalInput").ap()
    wq_in = nc.dram_tensor("wq", [H, H], F32, kind="ExternalInput").ap()
    wk_in = nc.dram_tensor("wk", [H, H], F32, kind="ExternalInput").ap()
    wv_in = nc.dram_tensor("wv", [H, H], F32, kind="ExternalInput").ap()
    wo_in = nc.dram_tensor("wo", [H, H], F32, kind="ExternalInput").ap()
    egw1_in = nc.dram_tensor("egw1", [512, H], F32, kind="ExternalInput").ap()
    egb1_in = nc.dram_tensor("egb1", [512], F32, kind="ExternalInput").ap()
    egw2_in = nc.dram_tensor("egw2", [1, 512], F32, kind="ExternalInput").ap()
    egb2_in = nc.dram_tensor("egb2", [1], F32, kind="ExternalInput").ap()
    mg_in = nc.dram_tensor("mgate", [NH], F32, kind="ExternalInput").ap()
    fm_in = nc.dram_tensor("fmem", [NB, NH, HD, HD], F32, kind="ExternalInput").ap()
    fn_in = nc.dram_tensor("fnorm", [NB, NH, HD], F32, kind="ExternalInput").ap()
    wg_in = nc.dram_tensor("gw", [II, H], I32, kind="ExternalInput").ap()
    sg_in = nc.dram_tensor("gs", [II], F32, kind="ExternalInput").ap()
    wu_in = nc.dram_tensor("uw", [II, H], I32, kind="ExternalInput").ap()
    su_in = nc.dram_tensor("us", [II], F32, kind="ExternalInput").ap()
    wd_in = nc.dram_tensor("dw", [H, II], I32, kind="ExternalInput").ap()
    sd_in = nc.dram_tensor("ds", [H], F32, kind="ExternalInput").ap()
    out = nc.dram_tensor("out", [S, H], F32, kind="ExternalOutput").ap()

    wq_bf = nc.dram_tensor("wq_bf", [H, H], BF16, kind="Internal").ap()
    wk_bf = nc.dram_tensor("wk_bf", [H, H], BF16, kind="Internal").ap()
    wv_bf = nc.dram_tensor("wv_bf", [H, H], BF16, kind="Internal").ap()
    wo_bf = nc.dram_tensor("wo_bf", [H, H], BF16, kind="Internal").ap()
    egw1_bf = nc.dram_tensor("egw1_bf", [512, H], BF16, kind="Internal").ap()
    wg_bf = nc.dram_tensor("wg_bf", [II, H], BF16, kind="Internal").ap()
    wu_bf = nc.dram_tensor("wu_bf", [II, H], BF16, kind="Internal").ap()
    wd_bf = nc.dram_tensor("wd_bf", [H, II], BF16, kind="Internal").ap()
    kscr = "ExternalOutput" if dbg else "Internal"
    xn_scr = nc.dram_tensor("xn_scr", [S, H], BF16, kind=kscr).ap()
    sq_scr = nc.dram_tensor("sq_scr", [H, S], BF16, kind=kscr).ap()
    skT_scr = nc.dram_tensor("skT_scr", [H, S], BF16, kind=kscr).ap()
    oc_scr = nc.dram_tensor("oc_scr", [S, H], BF16, kind=kscr).ap()
    y_scr = nc.dram_tensor("y_scr", [S, H], BF16, kind=kscr).ap()
    x1_scr = nc.dram_tensor("x1_scr", [S, H], F32, kind=kscr).ap()
    xn2_scr = nc.dram_tensor("xn2_scr", [S, H], BF16, kind=kscr).ap()

    ident_c = nc.inline_tensor(np.eye(128, dtype=np.float32), "ident_c").ap()
    maskT_c = nc.inline_tensor(np.triu(np.ones((128, 128), dtype=np.float32)), "maskT_c").ap()
    onc_bf_c = nc.inline_tensor(np.ones((128, 1), dtype=ml_dtypes.bfloat16), "onc_bf_c").ap()
    onc_f_c = nc.inline_tensor(np.ones((128, 1), dtype=np.float32), "onc_f_c").ap()
    onr_f_c = nc.inline_tensor(np.ones((1, 128), dtype=np.float32), "onr_f_c").ap()

    sems = {}
    for nm in ["bar",
               "sXL", "sSQ", "sVAR", "sSQT", "sXN", "sXNst",
               "sSm", "sFM", "sFMV", "sPE0", "sV0", "sS0",
               "sWL", "sWS",
               "sXnT", "sWq", "sQP", "sQX", "sQE", "sCN", "sCV", "sRe", "sSx",
               "sOCst", "sSQstr",
               "sGW", "sCrT", "sGPE", "sGS",
               "sKVw", "sKP", "sKX", "sKE", "sSKst", "sSkp", "sOCld",
               "sPEb", "sVb", "sSb", "sGb", "sYst",
               "sWo", "sYT", "sOP", "sXre", "sRv", "sXN2v", "sSQ2", "sVAR2", "sSQT2",
               "sX1st", "sXN2st",
               "sX2T", "sFW", "sGU", "sSil", "sHA", "sR2", "sDW", "sDP", "sDE",
               "sXR1", "sOut"]:
        sems[nm] = Sem(nc, nm)
    s = sems
    BAR_ENG = 4
    cur_phase = [0]

    def bwait(eng):
        eng.wait_ge(s["bar"].h, BAR_ENG * cur_phase[0])

    def bdone(eng, drain=False):
        if drain:
            eng.drain()
        eng.sem_inc(s["bar"].h, 1)

    def next_phase():
        cur_phase[0] += 1

    top = ExitStack()

    def sb(stack, name, shape, dtype, **kw):
        return stack.enter_context(nc.sbuf_tensor(name, shape, dtype, **kw))

    def psum(stack, name, shape, dtype=F32):
        return stack.enter_context(nc.psum_tensor(name, shape, dtype))

    identf = sb(top, "identf", [128, 128], F32)
    maskT = sb(top, "maskTs", [128, 128], F32)
    onc_bf = sb(top, "onc_bfs", [128, 1], BF16)
    onc_f = sb(top, "onc_fs", [128, 1], F32)
    onr_f = sb(top, "onr_fs", [1, 128], F32)
    sg_col = sb(top, "sg_col", [128, 64], F32)
    su_col = sb(top, "su_col", [128, 64], F32)
    sd_bc = sb(top, "sd_bc", [128, H], F32)
    b2_bc = sb(top, "b2_bc", [128, 1], F32)
    p_col = sb(top, "p_col", [128, ST], F32)

    attn = ExitStack()
    xnT = sb(attn, "xnT", [128, HT * S], BF16, is_transpose_dest=True)
    F_bf = sb(attn, "F_bf", [128, NH * NB * 128], BF16)
    Mp_bf = sb(attn, "Mp_bf", [128, NH * 129], BF16)
    norms_f = sb(attn, "norms_f", [128, 64], F32)
    norms_hm = sb(attn, "norms_hm", [128, 64], BF16)
    z_sb = sb(attn, "z_sb", [128, NH], F32)
    g_bc = sb(attn, "g_bc", [128, NH], F32)
    invg_bc = sb(attn, "invg_bc", [128, NH], F32)
    indz_bc = sb(attn, "indz_bc", [128, 1], F32)
    indb_row = sb(attn, "indb_row", [1, NB], F32)
    bsums = sb(attn, "bsums", [1, NB], F32)
    g_row = sb(attn, "g_row", [1, NH], F32)
    b1_col = sb(attn, "b1_col", [128, 4], F32)
    egw2_bf = sb(attn, "egw2_bf", [128, 4], BF16)
    rcw_all = sb(attn, "rcw_all", [128, NH * ST * NB], F32)

    SMALL_LOADS = 13

    ph = ExitStack()
    xbuf = [sb(ph, f"xb{i}", [128, H], F32) for i in range(2)]
    xnb = [sb(ph, f"xnb{i}", [128, H], BF16) for i in range(2)]
    junk = sb(ph, "junk", [128, H], BF16)
    ssq = [sb(ph, f"ssq{i}", [128, 1], F32) for i in range(2)]
    s1 = [sb(ph, f"s1_{i}", [128, 1], F32) for i in range(2)]
    vtmp = sb(ph, "vtmp", [128, 1], F32)
    mean2 = sb(ph, "mean2", [128, 1], F32)
    var = [sb(ph, f"var{i}", [128, 1], F32) for i in range(2)]
    sqv = [sb(ph, f"sqv{i}", [128, 1], F32) for i in range(2)]
    rstd = [sb(ph, f"rstd{i}", [128, 1], F32) for i in range(2)]
    nbias = [sb(ph, f"nbias{i}", [128, 1], F32) for i in range(2)]
    wstg = [sb(ph, f"wstg{i}", [128, 2048], BF16) for i in range(4)]
    nrm_raw = sb(ph, "nrm_raw", [64, 128], F32)
    sgr = sb(ph, "sgr", [64, 128], F32)
    sur = sb(ph, "sur", [64, 128], F32)
    b1r = sb(ph, "b1r", [4, 128], F32)
    w2r = sb(ph, "w2r", [4, 128], F32)
    mg_row = sb(ph, "mg_row", [1, NH], F32)
    invg_row = sb(ph, "invg_row", [1, NH], F32)
    b2_row = sb(ph, "b2_row", [1, 1], F32)
    sd_row = sb(ph, "sd_row", [1, H], F32)
    indz_f = sb(ph, "indz_f", [1, 1], F32)
    zt = sb(ph, "zt", [1, 1], F32)
    fstage = [sb(ph, f"fstage{i}", [128, 512], F32) for i in range(2)]
    mstage = sb(ph, "mstage", [128, 128], F32)
    egw2f = sb(ph, "egw2f", [128, 4], F32)
    pA = psum(ph, "pA", [128, 64])
    pB = psum(ph, "pB", [128, 512])
    pC = psum(ph, "pC", [128, 16])
    pD = psum(ph, "pD", [1, 16])

    prep_chunks = []
    for r in range(16):
        prep_chunks.append((wq_in[r * 128:(r + 1) * 128, :], wq_bf[r * 128:(r + 1) * 128, :]))
    for r in range(16):
        prep_chunks.append((wk_in[r * 128:(r + 1) * 128, :], wk_bf[r * 128:(r + 1) * 128, :]))
    for r in range(16):
        prep_chunks.append((wv_in[r * 128:(r + 1) * 128, :], wv_bf[r * 128:(r + 1) * 128, :]))
    for r in range(16):
        prep_chunks.append((wo_in[r * 128:(r + 1) * 128, :], wo_bf[r * 128:(r + 1) * 128, :]))
    for r in range(4):
        prep_chunks.append((egw1_in[r * 128:(r + 1) * 128, :], egw1_bf[r * 128:(r + 1) * 128, :]))
    for r in range(64):
        prep_chunks.append((wg_in[r * 128:(r + 1) * 128, :], wg_bf[r * 128:(r + 1) * 128, :]))
    for r in range(64):
        prep_chunks.append((wu_in[r * 128:(r + 1) * 128, :], wu_bf[r * 128:(r + 1) * 128, :]))
    for r in range(16):
        for c in range(4):
            prep_chunks.append((wd_in[r * 128:(r + 1) * 128, c * 2048:(c + 1) * 2048],
                                wd_bf[r * 128:(r + 1) * 128, c * 2048:(c + 1) * 2048]))
    N_PREP = len(prep_chunks)
    PREP_WQ, PREP_WK, PREP_WV, PREP_WO, PREP_EG = 16, 32, 48, 64, 68

    with nc.Block("p0", no_gpsimd_drain=True) as blk:
        @blk.gpsimd
        def _(g):
            for ci, (src, dst) in enumerate(prep_chunks):
                buf = wstg[ci % 4]
                if ci >= 4:
                    g.wait_ge(s["sWS"].h, 16 * (ci - 3))
                g.dma_start(buf[:], src).then_inc(s["sWL"].h, 16)
                s["sWL"].n += 16
                g.wait_ge(s["sWL"].h, 16 * (ci + 1))
                g.dma_start(dst, buf[:]).then_inc(s["sWS"].h, 16)
                s["sWS"].n += 16

        @blk.sync
        def _(sy):
            for i in range(ST):
                if i >= 2:
                    sy.wait_ge(s["sXN"].h, i - 1)
                    sy.dma_start(xn_scr[(i - 2) * 128:(i - 1) * 128, :],
                                 xnb[i % 2][:]).then_inc(s["sXNst"].h, 16)
                    s["sXNst"].n += 16
                sy.dma_start(xbuf[i % 2][:], x_in[i * 128:(i + 1) * 128, :]
                             ).then_inc(s["sXL"].h, 16)
                s["sXL"].n += 16
            for i in range(ST - 2, ST):
                sy.wait_ge(s["sXN"].h, i + 1)
                sy.dma_start(xn_scr[i * 128:(i + 1) * 128, :],
                             xnb[i % 2][:]).then_inc(s["sXNst"].h, 16)
                s["sXNst"].n += 16
            bdone(sy)

        @blk.scalar
        def _(sc):
            small = [(ident_c, identf[:]), (maskT_c, maskT[:]),
                     (onc_bf_c, onc_bf[:]), (onc_f_c, onc_f[:]), (onr_f_c, onr_f[:]),
                     (fn_in.rearrange("m h d -> (m h) d"), nrm_raw[:]),
                     (sg_in.rearrange("(a b) -> a b", b=128), sgr[:]),
                     (su_in.rearrange("(a b) -> a b", b=128), sur[:]),
                     (egb1_in.rearrange("(a b) -> a b", b=128), b1r[:]),
                     (egw2_in.rearrange("a (b c) -> (a b) c", c=128), w2r[:]),
                     (mg_in.rearrange("(a b) -> a b", a=1), mg_row[:]),
                     (egb2_in.rearrange("(a b) -> a b", a=1), b2_row[:]),
                     (sd_in.rearrange("(a b) -> a b", a=1), sd_row[:])]
            assert len(small) == SMALL_LOADS
            for src, dst in small:
                sc.dma_start(dst, src).then_inc(s["sSm"].h, 16)
                s["sSm"].n += 16
            sc.wait_ge(s["sSm"].h, 16 * SMALL_LOADS)
            sc.activation(g_row[:], mg_row[:], AF.Sigmoid).then_inc(s["sS0"].h, 1)
            s["sS0"].n += 1
            for h in range(NH):
                if h >= 2:
                    sc.wait_ge(s["sFMV"].h, h - 1)
                for m in range(NB):
                    sc.dma_start(fstage[h % 2][:, m * 128:(m + 1) * 128],
                                 fm_in[m, h]).then_inc(s["sFM"].h, 16)
                    s["sFM"].n += 16
            for i in range(ST):
                sc.wait_ge(s["sXL"].h, 16 * (i + 1))
                if i >= 2:
                    sc.wait_ge(s["sVAR"].h, i - 1)
                sc.activation(junk[:], xbuf[i % 2][:], AF.Square,
                              accum_out=ssq[i % 2][:])
                sc.drain()
                sc.sem_inc(s["sSQ"].h, 1)
                s["sSQ"].n += 1
                sc.wait_ge(s["sVAR"].h, i + 1)
                sc.activation(sqv[i % 2][:], var[i % 2][:], AF.Sqrt
                              ).then_inc(s["sSQT"].h, 1)
                s["sSQT"].n += 1
            bdone(sc, drain=True)

        @blk.tensor
        def _(t):
            t.wait_ge(s["sSm"].h, 16 * SMALL_LOADS)
            t.transpose(pA[:], nrm_raw[:], identf[0:64, 0:64]).then_inc(s["sPE0"].h, 1)
            t.wait_ge(s["sV0"].h, 1)
            t.transpose(pA[:], sgr[:], identf[0:64, 0:64]).then_inc(s["sPE0"].h, 1)
            t.wait_ge(s["sV0"].h, 2)
            t.transpose(pA[:], sur[:], identf[0:64, 0:64]).then_inc(s["sPE0"].h, 1)
            t.wait_ge(s["sV0"].h, 3)
            t.transpose(pA[0:128, 0:4], b1r[:], identf[0:4, 0:4]).then_inc(s["sPE0"].h, 1)
            t.wait_ge(s["sV0"].h, 4)
            t.transpose(pA[0:128, 0:4], w2r[:], identf[0:4, 0:4]).then_inc(s["sPE0"].h, 1)
            for m in range(NB):
                t.wait_ge(s["sV0"].h, 5 + m)
                t.matmul(pD[:], onc_f[:], norms_f[:, m * 16:(m + 1) * 16],
                         start=True, stop=True).then_inc(s["sPE0"].h, 1)
            t.wait_ge(s["sV0"].h, 10)
            t.matmul(pC[:], onr_f[:], g_row[:], start=True, stop=True
                     ).then_inc(s["sPE0"].h, 1)
            t.wait_ge(s["sV0"].h, 11)
            t.matmul(pC[:], onr_f[:], invg_row[:], start=True, stop=True
                     ).then_inc(s["sPE0"].h, 1)
            t.wait_ge(s["sV0"].h, 12)
            t.matmul(pC[0:128, 0:1], onr_f[:], indz_f[:], start=True, stop=True
                     ).then_inc(s["sPE0"].h, 1)
            t.wait_ge(s["sV0"].h, 13)
            t.matmul(pC[0:128, 0:1], onr_f[:], b2_row[:], start=True, stop=True
                     ).then_inc(s["sPE0"].h, 1)
            for b in range(4):
                t.wait_ge(s["sV0"].h, 14 + b)
                t.matmul(pB[:], onr_f[:], sd_row[0:1, b * 512:(b + 1) * 512],
                         start=True, stop=True).then_inc(s["sPE0"].h, 1)
            s["sPE0"].n = 17
            bdone(t)

        @blk.vector
        def _(v):
            v.wait_ge(s["sPE0"].h, 1)
            v.tensor_copy(norms_f[:], pA[:])
            v.tensor_copy(norms_hm.rearrange("p (hh m) -> p hh m", m=NB),
                          pA.rearrange("p (m hh) -> p hh m", hh=NH))
            v.tensor_add(z_sb[:], norms_f[:, 0:16], norms_f[:, 16:32])
            v.drain()
            v.tensor_add(z_sb[:], z_sb[:], norms_f[:, 32:48])
            v.drain()
            v.tensor_add(z_sb[:], z_sb[:], norms_f[:, 48:64]).then_inc(s["sV0"].h, 1)
            v.wait_ge(s["sPE0"].h, 2)
            v.tensor_copy(sg_col[:], pA[:]).then_inc(s["sV0"].h, 1)
            v.wait_ge(s["sPE0"].h, 3)
            v.tensor_copy(su_col[:], pA[:]).then_inc(s["sV0"].h, 1)
            v.wait_ge(s["sPE0"].h, 4)
            v.tensor_copy(b1_col[:], pA[0:128, 0:4]).then_inc(s["sV0"].h, 1)
            v.wait_ge(s["sPE0"].h, 5)
            v.tensor_copy(egw2f[:], pA[0:128, 0:4])
            v.drain()
            v.tensor_copy(egw2_bf[:], egw2f[:]).then_inc(s["sV0"].h, 1)
            for m in range(NB):
                v.wait_ge(s["sPE0"].h, 6 + m)
                v.reduce_sum(bsums[0:1, m:m + 1], pD[:], AX.X).then_inc(s["sV0"].h, 1)
            v.drain()
            v.tensor_scalar(indb_row[:], bsums[:], EPS, None, ALU.is_ge)
            v.reduce_sum(zt[:], bsums[:], AX.X)
            v.drain()
            v.tensor_scalar(indz_f[:], zt[:], EPS, None, ALU.is_ge)
            v.drain()
            v.wait_ge(s["sS0"].h, 1)
            v.tensor_scalar(invg_row[:], g_row[:], -1.0, 1.0, ALU.mult, ALU.add
                            ).then_inc(s["sV0"].h, 1)
            v.wait_ge(s["sPE0"].h, 10)
            v.tensor_copy(g_bc[:], pC[:]).then_inc(s["sV0"].h, 1)
            v.wait_ge(s["sPE0"].h, 11)
            v.tensor_copy(invg_bc[:], pC[:]).then_inc(s["sV0"].h, 1)
            v.wait_ge(s["sPE0"].h, 12)
            v.tensor_copy(indz_bc[:], pC[0:128, 0:1]).then_inc(s["sV0"].h, 1)
            v.wait_ge(s["sPE0"].h, 13)
            v.tensor_copy(b2_bc[:], pC[0:128, 0:1]).then_inc(s["sV0"].h, 1)
            for b in range(4):
                v.wait_ge(s["sPE0"].h, 14 + b)
                v.tensor_copy(sd_bc[:, b * 512:(b + 1) * 512], pB[:]).then_inc(s["sV0"].h, 1)
            s["sV0"].n = 18
            for h in range(NH):
                v.wait_ge(s["sFM"].h, 16 * 4 * (h + 1))
                fs = fstage[h % 2]
                v.tensor_add(mstage[:], fs[:, 0:128], fs[:, 128:256])
                v.tensor_add(mstage[:], mstage[:], fs[:, 256:384])
                v.tensor_add(mstage[:], mstage[:], fs[:, 384:512])
                v.tensor_copy(Mp_bf[:, h * 129:h * 129 + 128], mstage[:])
                v.tensor_copy(Mp_bf[:, h * 129 + 128:h * 129 + 129], z_sb[:, h:h + 1])
                v.tensor_copy(F_bf[:, h * 512:(h + 1) * 512], fs[:]).then_inc(s["sFMV"].h, 1)
                s["sFMV"].n += 1
            for i in range(ST + 1):
                if i < ST:
                    v.wait_ge(s["sXL"].h, 16 * (i + 1))
                    v.reduce_sum(s1[i % 2][:], xbuf[i % 2][:], AX.X)
                if i >= 1:
                    j = i - 1
                    if i == ST:
                        v.drain()
                    v.scalar_tensor_tensor(nbias[j % 2][:], s1[j % 2][:], -1.0 / H,
                                           rstd[j % 2][:], ALU.mult, ALU.mult)
                if i < ST:
                    v.wait_ge(s["sSQ"].h, i + 1)
                    v.tensor_scalar(vtmp[:], ssq[i % 2][:], 1.0 / H, LN_EPS,
                                    ALU.mult, ALU.add)
                    v.scalar_tensor_tensor(mean2[:], s1[i % 2][:], 1.0 / (H * float(H)),
                                           s1[i % 2][:], ALU.mult, ALU.mult)
                if i >= 1:
                    j = i - 1
                    if j >= 2:
                        v.wait_ge(s["sXNst"].h, 16 * (j - 1))
                    if i == ST:
                        v.drain()
                    v.tensor_scalar(xnb[j % 2][:], xbuf[j % 2][:], rstd[j % 2][:, 0:1],
                                    nbias[j % 2][:, 0:1], ALU.mult, ALU.add
                                    ).then_inc(s["sXN"].h, 1)
                    s["sXN"].n += 1
                if i < ST:
                    if i == 0:
                        v.drain()
                    v.tensor_sub(var[i % 2][:], vtmp[:], mean2[:]).then_inc(s["sVAR"].h, 1)
                    s["sVAR"].n += 1
                    v.wait_ge(s["sSQT"].h, i + 1)
                    v.reciprocal(rstd[i % 2][:], sqv[i % 2][:])
            bdone(v, drain=True)
    ph.close()
    next_phase()

    CVH = ST + 2
    CNH = ST + 3

    def cv(h, what, c=0):
        base = h * CVH
        return {"nf": base + 1, "w": base + 2, "oc": base + 2 + c + 1}[what]

    def cn(h, what, c=0):
        base = h * CNH
        return {"nf": base + 1, "rel": base + 2, "wbc": base + 3, "co": base + 3 + c + 1}[what]

    ph = ExitStack()
    wqT = [sb(ph, f"wqT{i}", [128, HT * 512], BF16, is_transpose_dest=True) for i in range(2)]
    sqTg = [sb(ph, f"sqTg{i}", [128, 4 * S], BF16) for i in range(2)]
    ocb = sb(ph, "ocb", [128, S], BF16)
    nf_sb = sb(ph, "nf_sb", [128, NB * ST], F32)
    rc_sb = sb(ph, "rc_sb", [128, NB * ST], F32)
    rel_r = sb(ph, "rel_r", [1, NB], F32)
    rel_m = sb(ph, "rel_m", [1, 1], F32)
    rel_e = sb(ph, "rel_e", [1, NB], F32)
    rel_x = sb(ph, "rel_x", [1, NB], F32)
    rel_s = sb(ph, "rel_s", [1, 1], F32)
    w_row = sb(ph, "w_row", [1, NB], F32)
    w_pre = sb(ph, "w_pre", [1, NB], F32)
    wbc = sb(ph, "wbc", [128, NB], F32)
    ebuf = [sb(ph, f"ebuf{i}", [128, 512], F32) for i in range(2)]
    rbuf = [sb(ph, f"rbuf{i}", [128, 512], F32) for i in range(2)]
    nct = [sb(ph, f"nct{i}", [128, 1], F32) for i in range(2)]
    recc = [sb(ph, f"recc{i}", [128, 1], F32) for i in range(2)]
    pQ = [psum(ph, f"pQ{i}", [128, 512]) for i in range(2)]
    pAC = [psum(ph, f"pAC{i}", [128, 129]) for i in range(2)]
    pNF = psum(ph, "pNF", [128, NB * ST])
    pREL = psum(ph, "pREL", [1, NB])
    pWBC = psum(ph, "pWBC", [128, NB])

    with nc.Block("a2a", no_gpsimd_drain=True) as blk:
        @blk.sync
        def _(sy):
            bwait(sy)
            sy.wait_ge(s["sXNst"].h, 16 * ST)
            for ht in range(HT):
                for j in range(NSUP):
                    sy.dma_start(xnT[:, ht * S + j * 512: ht * S + (j + 1) * 512],
                                 xn_scr[j * 512:(j + 1) * 512, ht * 128:(ht + 1) * 128],
                                 transpose=True).then_inc(s["sXnT"].h, 16)
                    s["sXnT"].n += 16
            sy.wait_ge(s["sWS"].h, 16 * PREP_WQ)
            def load_wqT(es):
                for ht in range(HT):
                    sy.dma_start(wqT[es % 2][:, ht * 512:(ht + 1) * 512],
                                 wq_bf[es * 512:(es + 1) * 512, ht * 128:(ht + 1) * 128],
                                 transpose=True).then_inc(s["sWq"].h, 16)
                    s["sWq"].n += 16
            load_wqT(0)
            load_wqT(1)
            for es in range(4):
                for hl in range(4):
                    h = es * 4 + hl
                    sy.wait_ge(s["sCV"].h, cv(h, "oc", ST - 1))
                    sy.dma_start(
                        oc_scr[:, h * 128:(h + 1) * 128].rearrange("(c p) e -> p c e", p=128),
                        ocb.rearrange("p (c e) -> p c e", e=128)).then_inc(s["sOCst"].h, 16)
                    s["sOCst"].n += 16
                sy.wait_ge(s["sQE"].h, (es + 1) * 4 * NSUP)
                sy.dma_start(
                    sq_scr[es * 512:(es + 1) * 512, :].rearrange("(el p) t -> p el t", p=128),
                    sqTg[es % 2].rearrange("p (el t) -> p el t", el=4)).then_inc(s["sSQstr"].h, 16)
                s["sSQstr"].n += 16
                if es + 2 < 4:
                    sy.wait_ge(s["sQP"].h, (es + 1) * 4 * NSUP)
                    load_wqT(es + 2)
            bdone(sy)

        @blk.tensor
        def _(t):
            bwait(t)
            t.wait_ge(s["sXnT"].h, 16 * HT * NSUP)
            for es in range(4):
                t.wait_ge(s["sWq"].h, 16 * HT * (es + 1))
                if es >= 2:
                    t.wait_ge(s["sSQstr"].h, 16 * (es - 1))
                for el in range(4):
                    for j in range(NSUP):
                        gq = es * 4 * NSUP + el * NSUP + j
                        pq = pQ[gq % 2]
                        if gq >= 2:
                            t.wait_ge(s["sQX"].h, gq - 1)
                        mm = None
                        for ht in range(HT):
                            mm = t.matmul(pq[:],
                                          wqT[es % 2][:, ht * 512 + el * 128: ht * 512 + (el + 1) * 128],
                                          xnT[:, ht * S + j * 512: ht * S + (j + 1) * 512],
                                          start=(ht == 0), stop=(ht == HT - 1))
                        mm.then_inc(s["sQP"].h, 1)
                        s["sQP"].n += 1
                for hl in range(4):
                    h = es * 4 + hl
                    t.wait_ge(s["sQE"].h, es * 4 * NSUP + (hl + 1) * NSUP)
                    if h >= 1:
                        t.wait_ge(s["sCV"].h, cv(h - 1, "w"))
                    mm = None
                    for c in range(ST):
                        mm = t.matmul(pNF[:, c * NB:(c + 1) * NB],
                                      sqTg[es % 2][:, hl * S + c * 128:hl * S + (c + 1) * 128],
                                      norms_hm[:, h * NB:(h + 1) * NB],
                                      start=True, stop=True)
                    mm.then_inc(s["sCN"].h, 1)
                    s["sCN"].n += 1
                    t.wait_ge(s["sCV"].h, cv(h, "nf"))
                    mm = None
                    for c in range(ST):
                        mm = t.matmul(pREL[:], onc_f[:], nf_sb[:, c * NB:(c + 1) * NB],
                                      start=(c == 0), stop=(c == ST - 1))
                    mm.then_inc(s["sCN"].h, 1)
                    s["sCN"].n += 1
                    t.wait_ge(s["sCV"].h, cv(h, "w"))
                    t.matmul(pWBC[:], onr_f[:], w_row[:], start=True, stop=True
                             ).then_inc(s["sCN"].h, 1)
                    s["sCN"].n += 1
                    for c in range(ST):
                        k = h * ST + c
                        if k >= 2:
                            h2, c2 = (k - 2) // ST, (k - 2) % ST
                            t.wait_ge(s["sCV"].h, cv(h2, "oc", c2))
                        t.matmul(pAC[c % 2][:],
                                 sqTg[es % 2][:, hl * S + c * 128:hl * S + (c + 1) * 128],
                                 Mp_bf[:, h * 129:(h + 1) * 129],
                                 start=True, stop=True).then_inc(s["sCN"].h, 1)
                        s["sCN"].n += 1
            bdone(t)

        @blk.scalar
        def _(sc):
            bwait(sc)
            for es in range(4):
                for el in range(4):
                    for j in range(NSUP):
                        gq = es * 4 * NSUP + el * NSUP + j
                        pq = pQ[gq % 2]
                        sc.wait_ge(s["sQP"].h, gq + 1)
                        if gq >= 2:
                            sc.wait_ge(s["sQE"].h, gq - 1)
                        sc.activation(ebuf[gq % 2][:], pq[:], AF.Exp)
                        sc.activation(rbuf[gq % 2][:], pq[:], AF.Relu).then_inc(s["sQX"].h, 1)
                        s["sQX"].n += 1
                for hl in range(4):
                    h = es * 4 + hl
                    sc.wait_ge(s["sRe"].h, h + 1)
                    sc.activation(rel_x[:], rel_e[:], AF.Exp).then_inc(s["sSx"].h, 1)
                    s["sSx"].n += 1
            bdone(sc, drain=True)

        @blk.vector
        def _(v):
            bwait(v)
            for es in range(4):
                for el in range(4):
                    for j in range(NSUP):
                        gq = es * 4 * NSUP + el * NSUP + j
                        v.wait_ge(s["sQX"].h, gq + 1)
                        v.scalar_tensor_tensor(
                            sqTg[es % 2][:, (el * S + j * 512):(el * S + (j + 1) * 512)],
                            ebuf[gq % 2][:], 1.0, rbuf[gq % 2][:], ALU.min, ALU.add
                        ).then_inc(s["sQE"].h, 1)
                        s["sQE"].n += 1
                for hl in range(4):
                    h = es * 4 + hl
                    v.wait_ge(s["sCN"].h, cn(h, "nf"))
                    v.tensor_copy(nf_sb[:], pNF[:]).then_inc(s["sCV"].h, 1)
                    s["sCV"].n += 1
                    v.wait_ge(s["sCN"].h, cn(h, "rel"))
                    v.tensor_scalar(rel_r[:], pREL[:], 1.0 / S, None, ALU.mult)
                    v.tensor_scalar(rc_sb[:], nf_sb[:], EPS, None, ALU.max)
                    v.reduce_max(rel_m[:], rel_r[:], AX.X)
                    v.reciprocal(rc_sb[:], rc_sb[:])
                    v.tensor_single_scalar(rel_e[:], rel_r[:], rel_m[0:1, 0:1], ALU.subtract
                                           ).then_inc(s["sRe"].h, 1)
                    s["sRe"].n += 1
                    v.wait_ge(s["sSx"].h, h + 1)
                    v.tensor_mul(w_pre[:], rel_x[:], indb_row[:])
                    v.reduce_sum(rel_s[:], rel_x[:], AX.X)
                    v.drain()
                    v.reciprocal(rel_s[:], rel_s[:])
                    v.drain()
                    v.tensor_single_scalar(w_row[:], w_pre[:], rel_s[0:1, 0:1], ALU.mult
                                           ).then_inc(s["sCV"].h, 1)
                    s["sCV"].n += 1
                    v.wait_ge(s["sCN"].h, cn(h, "wbc"))
                    v.tensor_copy(wbc[:], pWBC[:])
                    v.drain()
                    rc3 = rc_sb.rearrange("p (c m) -> p c m", m=NB)
                    rw3 = rcw_all.rearrange("p (hh c m) -> p hh c m", hh=NH, m=NB)
                    for m in range(NB):
                        v.tensor_scalar(rw3[:, h, :, m], rc3[:, :, m], wbc[:, m:m + 1],
                                        None, ALU.mult)
                    if h >= 1:
                        v.wait_ge(s["sOCst"].h, 16 * h)
                    for c in range(ST + 1):
                        if c < ST:
                            v.wait_ge(s["sCN"].h, cn(h, "co", c))
                            v.tensor_scalar(nct[c % 2][:], pAC[c % 2][:, 128:129], EPS,
                                            None, ALU.max)
                        if c >= 1:
                            if c == ST:
                                v.drain()
                            v.tensor_scalar(ocb[:, (c - 1) * 128:c * 128],
                                            pAC[(c - 1) % 2][:, 0:128],
                                            recc[(c - 1) % 2][:, 0:1], indz_bc[:, 0:1],
                                            ALU.mult, ALU.mult).then_inc(s["sCV"].h, 1)
                            s["sCV"].n += 1
                        if c < ST:
                            if c == 0:
                                v.drain()
                            v.reciprocal(recc[c % 2][:], nct[c % 2][:])
            bdone(v, drain=True)
    ph.close()
    next_phase()

    ph = ExitStack()
    egw1T = sb(ph, "egw1T", [128, 16 * 512], BF16, is_transpose_dest=True)
    crT = [sb(ph, f"crT{i}", [128, 16 * 512], BF16, is_transpose_dest=True) for i in range(2)]
    hid_bf = sb(ph, "hid_bf", [128, 4 * 512], BF16)
    pHh = [psum(ph, f"pHh{i}", [128, 512]) for i in range(2)]
    pGP = psum(ph, "pGP", [128, 4])

    def hid_sgs(k_h):
        return (k_h // 4) * 8 + (k_h % 4) + 1

    with nc.Block("gprob", no_gpsimd_drain=True) as blk:
        @blk.sync
        def _(sy):
            bwait(sy)
            sy.wait_ge(s["sWS"].h, 16 * PREP_EG)
            for et in range(HT):
                sy.dma_start(egw1T[:, et * 512:(et + 1) * 512],
                             egw1_bf[0:512, et * 128:(et + 1) * 128],
                             transpose=True).then_inc(s["sGW"].h, 16)
                s["sGW"].n += 16
            sy.wait_ge(s["sOCst"].h, 16 * NH)
            for j in range(NSUP):
                if j >= 2:
                    sy.wait_ge(s["sGPE"].h, (j - 1) * 8)
                for et in range(HT):
                    sy.dma_start(crT[j % 2][:, et * 512:(et + 1) * 512],
                                 oc_scr[j * 512:(j + 1) * 512, et * 128:(et + 1) * 128],
                                 transpose=True).then_inc(s["sCrT"].h, 16)
                    s["sCrT"].n += 16
            bdone(sy)

        @blk.tensor
        def _(t):
            bwait(t)
            t.wait_ge(s["sGW"].h, 16 * HT)
            for j in range(NSUP):
                t.wait_ge(s["sCrT"].h, 16 * HT * (j + 1))
                for ft in range(4):
                    k_h = j * 4 + ft
                    if k_h >= 2:
                        t.wait_ge(s["sGS"].h, hid_sgs(k_h - 2))
                    mm = None
                    for et in range(HT):
                        mm = t.matmul(pHh[ft % 2][:],
                                      egw1T[:, et * 512 + ft * 128: et * 512 + (ft + 1) * 128],
                                      crT[j % 2][:, et * 512:(et + 1) * 512],
                                      start=(et == 0), stop=(et == HT - 1))
                    mm.then_inc(s["sGPE"].h, 1)
                    s["sGPE"].n += 1
                t.wait_ge(s["sGS"].h, j * 8 + 4)
                for cc in range(4):
                    mm = None
                    for ft in range(4):
                        mm = t.matmul(pGP[:, cc:cc + 1],
                                      hid_bf[:, ft * 512 + cc * 128: ft * 512 + (cc + 1) * 128],
                                      egw2_bf[:, ft:ft + 1],
                                      start=(ft == 0), stop=(ft == 3))
                    mm.then_inc(s["sGPE"].h, 1)
                    s["sGPE"].n += 1
            bdone(t)

        @blk.scalar
        def _(sc):
            bwait(sc)
            for j in range(NSUP):
                for ft in range(4):
                    sc.wait_ge(s["sGPE"].h, j * 8 + ft + 1)
                    sc.activation(hid_bf[:, ft * 512:(ft + 1) * 512], pHh[ft % 2][:],
                                  AF.Relu, bias=b1_col[:, ft:ft + 1]).then_inc(s["sGS"].h, 1)
                    s["sGS"].n += 1
                for cc in range(4):
                    c = j * 4 + cc
                    sc.wait_ge(s["sGPE"].h, j * 8 + 4 + cc + 1)
                    sc.activation(p_col[:, c:c + 1], pGP[:, cc:cc + 1], AF.Sigmoid,
                                  bias=b2_bc[:, 0:1]).then_inc(s["sGS"].h, 1)
                    s["sGS"].n += 1
            bdone(sc, drain=True)

        @blk.vector
        def _(v):
            bwait(v)
            bdone(v)
    ph.close()
    next_phase()

    for grp in range(4):
        ph = ExitStack()
        wkT = sb(ph, f"wkT{grp}", [128, HT * 512], BF16, is_transpose_dest=True)
        wvT = sb(ph, f"wvT{grp}", [128, HT * 512], BF16, is_transpose_dest=True)
        sqTb = sb(ph, f"sqTb{grp}", [128, 4 * S], BF16)
        skTg = sb(ph, f"skTg{grp}", [128, 4 * S], BF16)
        vg = sb(ph, f"vg{grp}", [128, 4 * S], BF16)
        skp = sb(ph, f"skp{grp}", [128, 2 * S], BF16, is_transpose_dest=True)
        ocl = sb(ph, f"ocl{grp}", [128, 4 * 128], BF16)
        ybuf = sb(ph, f"ybuf{grp}", [128, S], BF16)
        S_sb = sb(ph, f"S_sb{grp}", [128, 129], F32)
        S_bfs = sb(ph, f"S_bfs{grp}", [128, 129], BF16)
        ebuf2 = [sb(ph, f"eb2_{grp}_{i}", [128, 512], F32) for i in range(2)]
        rbuf2 = [sb(ph, f"rb2_{grp}_{i}", [128, 512], F32) for i in range(2)]
        lo = sb(ph, f"lo{grp}", [128, 128], F32)
        of = sb(ph, f"of{grp}", [128, 128], F32)
        ocf = sb(ph, f"ocf{grp}", [128, 128], F32)
        t1b = sb(ph, f"t1b{grp}", [128, 128], F32)
        memb = sb(ph, f"memb{grp}", [128, 128], F32)
        t2b = sb(ph, f"t2b{grp}", [128, 128], F32)
        mst = sb(ph, f"mst{grp}", [128, 128], BF16)
        dent = sb(ph, f"dent{grp}", [128, 1], F32)
        recl = sb(ph, f"recl{grp}", [128, 1], F32)
        pKV = [psum(ph, f"pKV{grp}_{i}", [128, 512]) for i in range(2)]
        pSC = psum(ph, f"pSC{grp}", [128, 128])
        pNM = [psum(ph, f"pNM{grp}_{i}", [128, 129]) for i in range(2)]
        pDS = psum(ph, f"pDS{grp}", [128, 129])
        pFN = psum(ph, f"pFN{grp}", [128, 512])
        pDEN = psum(ph, f"pDEN{grp}", [128, 1])

        NPRJ = 4 * NSUP + ST
        kp0 = s["sKP"].n
        kx0 = s["sKX"].n
        ke0 = s["sKE"].n
        pe0 = s["sPEb"].n
        vb0 = s["sVb"].n
        sb0 = s["sSb"].n
        gb0 = s["sGb"].n
        ocld0 = s["sOCld"].n
        skp0 = s["sSkp"].n
        skst0 = s["sSKst"].n
        yst0 = s["sYst"].n

        PE_C = 4
        V_C = 6
        G_C = 2

        def pe_scan(hl, c, k):
            return pe0 + (hl * ST + c) * PE_C + k

        def v_scan(hl, c, k):
            return vb0 + (hl * ST + c) * V_C + k

        def sb_cnt(hl, c):
            return sb0 + hl * (ST - 1) + c + 1

        with nc.Block(f"a2b{grp}", no_gpsimd_drain=True) as blk:
            @blk.sync
            def _(sy):
                bwait(sy)
                if grp > 0:
                    sy.wait_ge(s["sPEb"].h, pe0)
                sy.wait_ge(s["sWS"].h, 16 * PREP_WV)
                for ht in range(HT):
                    sy.dma_start(wkT[:, ht * 512:(ht + 1) * 512],
                                 wk_bf[grp * 512:(grp + 1) * 512, ht * 128:(ht + 1) * 128],
                                 transpose=True).then_inc(s["sKVw"].h, 16)
                    s["sKVw"].n += 16
                for ht in range(HT):
                    sy.dma_start(wvT[:, ht * 512:(ht + 1) * 512],
                                 wv_bf[grp * 512:(grp + 1) * 512, ht * 128:(ht + 1) * 128],
                                 transpose=True).then_inc(s["sKVw"].h, 16)
                    s["sKVw"].n += 16
                sy.wait_ge(s["sSQstr"].h, 16 * (grp + 1))
                for el in range(4):
                    sy.dma_start(sqTb[:, el * S:(el + 1) * S],
                                 sq_scr[grp * 512 + el * 128:grp * 512 + (el + 1) * 128, :]
                                 ).then_inc(s["sKVw"].h, 16)
                    s["sKVw"].n += 16
                sy.wait_ge(s["sKE"].h, ke0 + 4 * NSUP)
                sy.dma_start(
                    skT_scr[grp * 512:(grp + 1) * 512, :].rearrange("(el p) t -> p el t", p=128),
                    skTg.rearrange("p (el t) -> p el t", el=4)).then_inc(s["sSKst"].h, 16)
                s["sSKst"].n += 16
                sy.wait_ge(s["sSKst"].h, skst0 + 16)

                def load_skp(hl):
                    if hl >= 2:
                        sy.wait_ge(s["sPEb"].h, pe0 + (hl - 1) * ST * PE_C)
                    for c in range(ST):
                        sy.dma_start(
                            skp[:, ((hl % 2) * ST + c) * 128:((hl % 2) * ST + c + 1) * 128],
                            skT_scr[(grp * 4 + hl) * 128:(grp * 4 + hl + 1) * 128,
                                    c * 128:(c + 1) * 128],
                            transpose=True).then_inc(s["sSkp"].h, 16)
                        s["sSkp"].n += 16
                load_skp(0)
                load_skp(1)
                for hl in range(4):
                    sy.wait_ge(s["sVb"].h, v_scan(hl, ST - 1, V_C))
                    h = grp * 4 + hl
                    sy.dma_start(
                        y_scr[:, h * 128:(h + 1) * 128].rearrange("(c p) e -> p c e", p=128),
                        ybuf.rearrange("p (c e) -> p c e", e=128)).then_inc(s["sYst"].h, 16)
                    s["sYst"].n += 16
                    if hl + 2 < 4:
                        load_skp(hl + 2)
                bdone(sy)

            @blk.scalar
            def _(sc):
                bwait(sc)
                if grp > 0:
                    sc.wait_ge(s["sGb"].h, gb0)
                for el in range(4):
                    for j in range(NSUP):
                        gq = el * NSUP + j
                        sc.wait_ge(s["sKP"].h, kp0 + gq + 1)
                        if gq >= 2:
                            sc.wait_ge(s["sKE"].h, ke0 + gq - 1)
                        sc.activation(ebuf2[gq % 2][:], pKV[gq % 2][:], AF.Exp)
                        sc.activation(rbuf2[gq % 2][:], pKV[gq % 2][:], AF.Relu
                                      ).then_inc(s["sKX"].h, 1)
                        s["sKX"].n += 1
                for k0 in range(min(4, 4 * ST)):
                    hl0, c0 = k0 // ST, k0 % ST
                    h0 = grp * 4 + hl0
                    sc.dma_start(ocl[:, (k0 % 4) * 128:((k0 % 4) + 1) * 128],
                                 oc_scr[c0 * 128:(c0 + 1) * 128, h0 * 128:(h0 + 1) * 128]
                                 ).then_inc(s["sOCld"].h, 16)
                    s["sOCld"].n += 16
                for hl in range(4):
                    for c in range(ST):
                        k = hl * ST + c
                        if c < ST - 1:
                            sc.wait_ge(s["sVb"].h, v_scan(hl, c, 4))
                            sc.copy(S_bfs[:], S_sb[:]).then_inc(s["sSb"].h, 1)
                            s["sSb"].n += 1
                        kn = k + 4
                        if kn < 4 * ST:
                            hln, cn_ = kn // ST, kn % ST
                            hn = grp * 4 + hln
                            sc.wait_ge(s["sGb"].h, gb0 + k * 2 + 1)
                            sc.dma_start(ocl[:, (kn % 4) * 128:((kn % 4) + 1) * 128],
                                         oc_scr[cn_ * 128:(cn_ + 1) * 128,
                                                hn * 128:(hn + 1) * 128]
                                         ).then_inc(s["sOCld"].h, 16)
                            s["sOCld"].n += 16
                bdone(sc, drain=True)

            @blk.tensor
            def _(t):
                bwait(t)
                t.wait_ge(s["sKVw"].h, 16 * (2 * HT + 4) * (grp + 1))
                for el in range(4):
                    for j in range(NSUP):
                        gq = el * NSUP + j
                        pq = pKV[gq % 2]
                        if gq >= 2:
                            t.wait_ge(s["sKX"].h, kx0 + gq - 1)
                        mm = None
                        for ht in range(HT):
                            mm = t.matmul(pq[:],
                                          wkT[:, ht * 512 + el * 128: ht * 512 + (el + 1) * 128],
                                          xnT[:, ht * S + j * 512: ht * S + (j + 1) * 512],
                                          start=(ht == 0), stop=(ht == HT - 1))
                        mm.then_inc(s["sKP"].h, 1)
                        s["sKP"].n += 1
                for c in range(ST):
                    gq = 4 * NSUP + c
                    pq = pKV[gq % 2]
                    t.wait_ge(s["sKE"].h, ke0 + gq - 1)
                    mm = None
                    for ht in range(HT):
                        mm = t.matmul(pq[:],
                                      xnT[:, ht * S + c * 128: ht * S + (c + 1) * 128],
                                      wvT[:, ht * 512:(ht + 1) * 512],
                                      start=(ht == 0), stop=(ht == HT - 1))
                    mm.then_inc(s["sKP"].h, 1)
                    s["sKP"].n += 1
                for hl in range(4):
                    h = grp * 4 + hl
                    t.wait_ge(s["sKE"].h, ke0 + NPRJ)
                    t.wait_ge(s["sSkp"].h, skp0 + 16 * (hl + 1) * ST)
                    for c in range(ST):
                        k = hl * ST + c
                        if k >= 1:
                            pc, cc = (k - 1) // ST, (k - 1) % ST
                        if k >= 1:
                            t.wait_ge(s["sVb"].h, v_scan(pc, cc, 1))
                        t.matmul(pSC[:], skTg[:, hl * S + c * 128:hl * S + (c + 1) * 128],
                                 sqTb[:, hl * S + c * 128:hl * S + (c + 1) * 128],
                                 start=True, stop=True).then_inc(s["sPEb"].h, 1)
                        s["sPEb"].n += 1
                        if k >= 1:
                            t.wait_ge(s["sVb"].h, v_scan(pc, cc, 3))
                        mm = None
                        for m in range(NB):
                            mm = t.matmul(pFN[:, m * 128:(m + 1) * 128],
                                          sqTb[:, hl * S + c * 128:hl * S + (c + 1) * 128],
                                          F_bf[:, h * 512 + m * 128: h * 512 + (m + 1) * 128],
                                          start=True, stop=True)
                        mm.then_inc(s["sPEb"].h, 1)
                        s["sPEb"].n += 1
                        t.wait_ge(s["sVb"].h, v_scan(hl, c, 1))
                        if k >= 1:
                            t.wait_ge(s["sVb"].h, v_scan(pc, cc, 2))
                        pn = pNM[c % 2]
                        sqt = sqTb[:, hl * S + c * 128:hl * S + (c + 1) * 128]
                        if c == 0:
                            t.matmul(pn[:, 0:128], mst[:],
                                     vg[:, c * 512 + hl * 128:c * 512 + (hl + 1) * 128],
                                     start=True, stop=True)
                            mm = t.matmul(pDEN[:], mst[:], onc_bf[:],
                                          start=True, stop=True)
                        else:
                            t.matmul(pn[:, 0:128], mst[:],
                                     vg[:, c * 512 + hl * 128:c * 512 + (hl + 1) * 128],
                                     start=True, stop=False)
                            t.matmul(pDEN[:], mst[:], onc_bf[:], start=True, stop=False)
                            t.wait_ge(s["sSb"].h, sb_cnt(hl, c - 1))
                            t.matmul(pn[:, 0:128], sqt, S_bfs[:, 0:128],
                                     start=False, stop=True)
                            mm = t.matmul(pDEN[:], sqt, S_bfs[:, 128:129],
                                          start=False, stop=True)
                        mm.then_inc(s["sPEb"].h, 1)
                        s["sPEb"].n += 1
                        if c >= 1:
                            t.wait_ge(s["sVb"].h, v_scan(hl, c - 1, 4))
                        elif hl >= 1:
                            t.wait_ge(s["sVb"].h, v_scan(hl - 1, ST - 1, 4))
                        t.matmul(pDS[:, 0:128],
                                 skp[:, ((hl % 2) * ST + c) * 128:((hl % 2) * ST + c + 1) * 128],
                                 vg[:, c * 512 + hl * 128:c * 512 + (hl + 1) * 128],
                                 start=True, stop=True)
                        t.matmul(pDS[:, 128:129],
                                 skp[:, ((hl % 2) * ST + c) * 128:((hl % 2) * ST + c + 1) * 128],
                                 onc_bf[:], start=True, stop=True).then_inc(s["sPEb"].h, 1)
                        s["sPEb"].n += 1
                bdone(t)

            @blk.vector
            def _(v):
                bwait(v)
                for el in range(4):
                    for j in range(NSUP):
                        gq = el * NSUP + j
                        v.wait_ge(s["sKX"].h, kx0 + gq + 1)
                        v.scalar_tensor_tensor(
                            skTg[:, (el * S + j * 512):(el * S + (j + 1) * 512)],
                            ebuf2[gq % 2][:], 1.0, rbuf2[gq % 2][:], ALU.min, ALU.add
                        ).then_inc(s["sKE"].h, 1)
                        s["sKE"].n += 1
                for c in range(ST):
                    gq = 4 * NSUP + c
                    v.wait_ge(s["sKP"].h, kp0 + gq + 1)
                    v.tensor_copy(vg[:, c * 512:(c + 1) * 512], pKV[gq % 2][:]
                                  ).then_inc(s["sKE"].h, 1)
                    s["sKE"].n += 1
                for hl in range(4):
                    h = grp * 4 + hl
                    v.memset(S_sb[:], 0.0)
                    for c in range(ST):
                        k = hl * ST + c
                        base = h * ST * NB + c * NB
                        v.wait_ge(s["sPEb"].h, pe_scan(hl, c, 1))
                        if k >= 1:
                            pc, cc = (k - 1) // ST, (k - 1) % ST
                            v.wait_ge(s["sPEb"].h, pe_scan(pc, cc, 3))
                        v.tensor_tensor(mst[:], pSC[:], maskT[:], ALU.mult
                                        ).then_inc(s["sVb"].h, 1)
                        s["sVb"].n += 1
                        v.wait_ge(s["sPEb"].h, pe_scan(hl, c, 3))
                        v.tensor_scalar(dent[:], pDEN[:], EPS, None, ALU.max)
                        v.tensor_scalar(of[:], pFN[:, 0:128], rcw_all[:, base:base + 1],
                                        None, ALU.mult)
                        v.reciprocal(recl[:], dent[:])
                        v.scalar_tensor_tensor(of[:], pFN[:, 128:256],
                                               rcw_all[:, base + 1:base + 2], of[:],
                                               ALU.mult, ALU.add)
                        v.tensor_scalar(lo[:], pNM[c % 2][:, 0:128], recl[:, 0:1],
                                        None, ALU.mult).then_inc(s["sVb"].h, 1)
                        s["sVb"].n += 1
                        v.scalar_tensor_tensor(of[:], pFN[:, 256:384],
                                               rcw_all[:, base + 2:base + 3], of[:],
                                               ALU.mult, ALU.add)
                        v.scalar_tensor_tensor(of[:], pFN[:, 384:512],
                                               rcw_all[:, base + 3:base + 4], of[:],
                                               ALU.mult, ALU.add).then_inc(s["sVb"].h, 1)
                        s["sVb"].n += 1
                        v.wait_ge(s["sPEb"].h, pe_scan(hl, c, 4))
                        v.tensor_tensor(S_sb[:], pDS[:], S_sb[:], ALU.add
                                        ).then_inc(s["sVb"].h, 1)
                        s["sVb"].n += 1
                        v.wait_ge(s["sGb"].h, gb0 + k * G_C + 1)
                        v.scalar_tensor_tensor(memb[:], t1b[:], p_col[:, c:c + 1], ocf[:],
                                               ALU.mult, ALU.add).then_inc(s["sVb"].h, 1)
                        s["sVb"].n += 1
                        v.wait_ge(s["sGb"].h, gb0 + k * G_C + 2)
                        if c == 0:
                            if hl == 0:
                                if grp > 0:
                                    v.wait_ge(s["sYst"].h, yst0)
                            else:
                                v.wait_ge(s["sYst"].h, yst0 + 16 * hl)
                        v.scalar_tensor_tensor(ybuf[:, c * 128:(c + 1) * 128], t2b[:],
                                               g_bc[:, h:h + 1], lo[:], ALU.mult, ALU.add
                                               ).then_inc(s["sVb"].h, 1)
                        s["sVb"].n += 1
                bdone(v, drain=True)

            @blk.gpsimd
            def _(g):
                for hl in range(4):
                    for c in range(ST):
                        k = hl * ST + c
                        g.wait_ge(s["sOCld"].h, ocld0 + 16 * (k + 1))
                        g.wait_ge(s["sVb"].h, v_scan(hl, c, 3))
                        g.tensor_copy(ocf[:], ocl[:, (k % 4) * 128:((k % 4) + 1) * 128])
                        g.tensor_sub(t1b[:], of[:], ocf[:]).then_inc(s["sGb"].h, 1)
                        s["sGb"].n += 1
                        g.wait_ge(s["sVb"].h, v_scan(hl, c, 5))
                        g.wait_ge(s["sVb"].h, v_scan(hl, c, 2))
                        if k >= 1:
                            pc, cc = (k - 1) // ST, (k - 1) % ST
                            g.wait_ge(s["sVb"].h, v_scan(pc, cc, 6))
                        g.tensor_sub(t2b[:], memb[:], lo[:]).then_inc(s["sGb"].h, 1)
                        s["sGb"].n += 1
        ph.close()
        next_phase()
    attn.close()

    ph = ExitStack()
    woT = sb(ph, "woT", [128, HT * 2048], BF16, is_transpose_dest=True)
    yT = [sb(ph, f"yT{i}", [128, HT * 512], BF16, is_transpose_dest=True) for i in range(2)]
    xre = [sb(ph, f"xre{i}", [128, H], F32) for i in range(2)]
    x1t = [sb(ph, f"x1t{i}", [128, H], F32) for i in range(2)]
    xn2t = [sb(ph, f"xn2t{i}", [128, H], BF16) for i in range(2)]
    junk2 = sb(ph, "junk2", [128, H], BF16)
    ssq2 = [sb(ph, f"ssq2{i}", [128, 1], F32) for i in range(2)]
    s1b = [sb(ph, f"s1b{i}", [128, 1], F32) for i in range(2)]
    vtmp2 = sb(ph, "vtmp2", [128, 1], F32)
    mean2b = sb(ph, "mean2b", [128, 1], F32)
    varb = [sb(ph, f"varb{i}", [128, 1], F32) for i in range(2)]
    sqvb = [sb(ph, f"sqvb{i}", [128, 1], F32) for i in range(2)]
    rstdb = [sb(ph, f"rstdb{i}", [128, 1], F32) for i in range(2)]
    nbiasb = [sb(ph, f"nbiasb{i}", [128, 1], F32) for i in range(2)]
    pO = [psum(ph, f"pO{i}", [128, 2048]) for i in range(2)]

    with nc.Block("a3", no_gpsimd_drain=True) as blk:
        @blk.sync
        def _(sy):
            bwait(sy)
            sy.wait_ge(s["sWS"].h, 16 * PREP_WO)
            for et in range(HT):
                for hob in range(4):
                    sy.dma_start(woT[:, et * 2048 + hob * 512: et * 2048 + (hob + 1) * 512],
                                 wo_bf[hob * 512:(hob + 1) * 512, et * 128:(et + 1) * 128],
                                 transpose=True).then_inc(s["sWo"].h, 16)
                    s["sWo"].n += 16
            sy.wait_ge(s["sYst"].h, 16 * NH)

            def load_yT(j):
                if j >= 2:
                    sy.wait_ge(s["sOP"].h, (j - 1) * 4)
                for et in range(HT):
                    sy.dma_start(yT[j % 2][:, et * 512:(et + 1) * 512],
                                 y_scr[j * 512:(j + 1) * 512, et * 128:(et + 1) * 128],
                                 transpose=True).then_inc(s["sYT"].h, 16)
                    s["sYT"].n += 16
            load_yT(0)
            if NSUP > 1:
                load_yT(1)
            for i in range(ST):
                sy.wait_ge(s["sRv"].h, i + 1)
                sy.wait_ge(s["sXN2v"].h, i + 1)
                sy.dma_start(x1_scr[i * 128:(i + 1) * 128, :], x1t[i % 2][:]
                             ).then_inc(s["sX1st"].h, 16)
                s["sX1st"].n += 16
                sy.dma_start(xn2_scr[i * 128:(i + 1) * 128, :], xn2t[i % 2][:]
                             ).then_inc(s["sXN2st"].h, 16)
                s["sXN2st"].n += 16
                if i % 4 == 3 and (i // 4) + 2 < NSUP:
                    load_yT((i // 4) + 2)
            bdone(sy)

        @blk.scalar
        def _(sc):
            bwait(sc)
            for i in range(min(2, ST)):
                sc.dma_start(xre[i % 2][:], x_in[i * 128:(i + 1) * 128, :]
                             ).then_inc(s["sXre"].h, 16)
                s["sXre"].n += 16
            for i in range(ST):
                sc.wait_ge(s["sRv"].h, i + 1)
                if i >= 2:
                    sc.wait_ge(s["sVAR2"].h, i - 1)
                sc.activation(junk2[:], x1t[i % 2][:], AF.Square,
                              accum_out=ssq2[i % 2][:])
                sc.drain()
                sc.sem_inc(s["sSQ2"].h, 1)
                s["sSQ2"].n += 1
                sc.wait_ge(s["sVAR2"].h, i + 1)
                sc.activation(sqvb[i % 2][:], varb[i % 2][:], AF.Sqrt
                              ).then_inc(s["sSQT2"].h, 1)
                s["sSQT2"].n += 1
                if i + 2 < ST:
                    sc.dma_start(xre[i % 2][:], x_in[(i + 2) * 128:(i + 3) * 128, :]
                                 ).then_inc(s["sXre"].h, 16)
                    s["sXre"].n += 16
            bdone(sc, drain=True)

        @blk.tensor
        def _(t):
            bwait(t)
            t.wait_ge(s["sWo"].h, 16 * HT * 4)
            for j in range(NSUP):
                t.wait_ge(s["sYT"].h, 16 * HT * (j + 1))
                for cc in range(4):
                    c = j * 4 + cc
                    if c >= 2:
                        t.wait_ge(s["sRv"].h, c - 1)
                    mm = None
                    for et in range(HT):
                        for hob in range(4):
                            mm = t.matmul(pO[c % 2][:, hob * 512:(hob + 1) * 512],
                                          yT[j % 2][:, et * 512 + cc * 128: et * 512 + (cc + 1) * 128],
                                          woT[:, et * 2048 + hob * 512: et * 2048 + (hob + 1) * 512],
                                          start=(et == 0), stop=(et == HT - 1))
                    mm.then_inc(s["sOP"].h, 1)
                    s["sOP"].n += 1
            bdone(t)

        @blk.vector
        def _(v):
            bwait(v)
            for i in range(ST + 1):
                if i < ST:
                    v.wait_ge(s["sOP"].h, i + 1)
                    v.wait_ge(s["sXre"].h, 16 * (i + 1))
                    if i >= 2:
                        v.wait_ge(s["sX1st"].h, 16 * (i - 1))
                        v.wait_ge(s["sSQ2"].h, i - 1)
                        v.wait_ge(s["sXN2v"].h, i - 1)
                    v.tensor_tensor(x1t[i % 2][:], pO[i % 2][:], xre[i % 2][:], ALU.add
                                    ).then_inc(s["sRv"].h, 1)
                    s["sRv"].n += 1
                    v.reduce_sum(s1b[i % 2][:], x1t[i % 2][:], AX.X)
                if i >= 1:
                    j = i - 1
                    if i == ST:
                        v.drain()
                    v.scalar_tensor_tensor(nbiasb[j % 2][:], s1b[j % 2][:], -1.0 / H,
                                           rstdb[j % 2][:], ALU.mult, ALU.mult)
                if i < ST:
                    v.wait_ge(s["sSQ2"].h, i + 1)
                    v.tensor_scalar(vtmp2[:], ssq2[i % 2][:], 1.0 / H, LN_EPS,
                                    ALU.mult, ALU.add)
                    v.scalar_tensor_tensor(mean2b[:], s1b[i % 2][:], 1.0 / (H * float(H)),
                                           s1b[i % 2][:], ALU.mult, ALU.mult)
                if i >= 1:
                    j = i - 1
                    if j >= 2:
                        v.wait_ge(s["sXN2st"].h, 16 * (j - 1))
                    if i == ST:
                        v.drain()
                    v.tensor_scalar(xn2t[j % 2][:], x1t[j % 2][:], rstdb[j % 2][:, 0:1],
                                    nbiasb[j % 2][:, 0:1], ALU.mult, ALU.add
                                    ).then_inc(s["sXN2v"].h, 1)
                    s["sXN2v"].n += 1
                if i < ST:
                    if i == 0:
                        v.drain()
                    v.tensor_sub(varb[i % 2][:], vtmp2[:], mean2b[:]).then_inc(s["sVAR2"].h, 1)
                    s["sVAR2"].n += 1
                    v.wait_ge(s["sSQT2"].h, i + 1)
                    v.reciprocal(rstdb[i % 2][:], sqvb[i % 2][:])
            bdone(v, drain=True)
    ph.close()
    next_phase()

    ph = ExitStack()
    xn2T = sb(ph, "xn2T", [128, HT * 512], BF16, is_transpose_dest=True)
    hact = sb(ph, "hact", [128, IT * 512], BF16)
    wgT = [sb(ph, f"wgT{i}", [128, HT * 512], BF16, is_transpose_dest=True) for i in range(2)]
    wuT = [sb(ph, f"wuT{i}", [128, HT * 512], BF16, is_transpose_dest=True) for i in range(2)]
    wdT = [sb(ph, f"wdT{i}", [128, 512], BF16, is_transpose_dest=True) for i in range(4)]
    gs_t = [sb(ph, f"gs_t{i}", [128, 512], F32) for i in range(2)]
    x1l = sb(ph, "x1l", [128, 4 * 2048], F32)
    rr = sb(ph, "rr", [128, 4], F32)
    outb = [sb(ph, f"outb{i}", [128, 512], F32) for i in range(2)]
    pGf = psum(ph, "pGf", [128, 512])
    pUf = psum(ph, "pUf", [128, 512])
    pDN = [psum(ph, f"pDN{i}", [128, 512]) for i in range(4)]
    pR2 = psum(ph, "pR2", [128, 4])

    with nc.Block("ffn", no_gpsimd_drain=True) as blk:
        @blk.sync
        def _(sy):
            bwait(sy)
            sy.wait_ge(s["sWS"].h, 16 * N_PREP)
            for sbk in range(NFS):
                sy.wait_ge(s["sXN2st"].h, 16 * 4 * (sbk + 1))
                if sbk >= 1:
                    sy.wait_ge(s["sGU"].h, sbk * 2 * IT)
                for ht in range(HT):
                    sy.dma_start(xn2T[:, ht * 512:(ht + 1) * 512],
                                 xn2_scr[sbk * 512:(sbk + 1) * 512, ht * 128:(ht + 1) * 128],
                                 transpose=True).then_inc(s["sX2T"].h, 16)
                    s["sX2T"].n += 16
                for isup in range(16):
                    gi = sbk * 16 + isup
                    if gi >= 2:
                        sy.wait_ge(s["sGU"].h, (gi - 1) * 8)
                    for ht in range(HT):
                        sy.dma_start(wgT[gi % 2][:, ht * 512:(ht + 1) * 512],
                                     wg_bf[isup * 512:(isup + 1) * 512, ht * 128:(ht + 1) * 128],
                                     transpose=True).then_inc(s["sFW"].h, 16)
                        s["sFW"].n += 16
                    for ht in range(HT):
                        sy.dma_start(wuT[gi % 2][:, ht * 512:(ht + 1) * 512],
                                     wu_bf[isup * 512:(isup + 1) * 512, ht * 128:(ht + 1) * 128],
                                     transpose=True).then_inc(s["sFW"].h, 16)
                        s["sFW"].n += 16
                for hob in range(4):
                    for it in range(IT):
                        k = (sbk * 4 + hob) * IT + it
                        if k >= 4:
                            sy.wait_ge(s["sDP"].h, k - 3)
                        sy.dma_start(wdT[it % 4][:],
                                     wd_bf[hob * 512:(hob + 1) * 512, it * 128:(it + 1) * 128],
                                     transpose=True).then_inc(s["sDW"].h, 16)
                        s["sDW"].n += 16
            bdone(sy)

        @blk.scalar
        def _(sc):
            bwait(sc)
            for sbk in range(NFS):
                sc.wait_ge(s["sX1st"].h, 16 * 4 * (sbk + 1))
                if sbk >= 1:
                    sc.wait_ge(s["sDE"].h, sbk * 16)
                for ss in range(4):
                    sc.dma_start(x1l[:, ss * 2048:(ss + 1) * 2048],
                                 x1_scr[(sbk * 4 + ss) * 128:(sbk * 4 + ss + 1) * 128, :]
                                 ).then_inc(s["sXR1"].h, 16)
                    s["sXR1"].n += 16
                for it in range(IT):
                    gi = sbk * IT + it
                    sc.wait_ge(s["sGU"].h, gi * 2 + 1)
                    if gi >= 2:
                        sc.wait_ge(s["sHA"].h, gi - 1)
                    sc.activation(gs_t[it % 2][:], pGf[:], AF.Silu,
                                  scale=sg_col[:, it:it + 1]).then_inc(s["sSil"].h, 1)
                    s["sSil"].n += 1
                for hob in range(4):
                    for ss in range(4):
                        k = hob * 4 + ss
                        sc.wait_ge(s["sDE"].h, sbk * 16 + k + 1)
                        sc.dma_start(out[(sbk * 4 + ss) * 128:(sbk * 4 + ss + 1) * 128,
                                         hob * 512:(hob + 1) * 512],
                                     outb[k % 2][:]).then_inc(s["sOut"].h, 16)
                        s["sOut"].n += 16
            sc.wait_ge(s["sOut"].h, s["sOut"].n)
            bdone(sc, drain=True)

        @blk.tensor
        def _(t):
            bwait(t)
            for sbk in range(NFS):
                t.wait_ge(s["sX2T"].h, 16 * HT * (sbk + 1))
                for it in range(IT):
                    gi = sbk * IT + it
                    isup, itl = it // 4, it % 4
                    t.wait_ge(s["sFW"].h, 16 * 2 * HT * (sbk * 16 + isup + 1))
                    if gi >= 1:
                        t.wait_ge(s["sSil"].h, gi)
                    mm = None
                    for ht in range(HT):
                        mm = t.matmul(pGf[:],
                                      wgT[(sbk * 16 + isup) % 2][:, ht * 512 + itl * 128: ht * 512 + (itl + 1) * 128],
                                      xn2T[:, ht * 512:(ht + 1) * 512],
                                      start=(ht == 0), stop=(ht == HT - 1))
                    mm.then_inc(s["sGU"].h, 1)
                    s["sGU"].n += 1
                    if gi >= 1:
                        t.wait_ge(s["sHA"].h, gi)
                    mm = None
                    for ht in range(HT):
                        mm = t.matmul(pUf[:],
                                      wuT[(sbk * 16 + isup) % 2][:, ht * 512 + itl * 128: ht * 512 + (itl + 1) * 128],
                                      xn2T[:, ht * 512:(ht + 1) * 512],
                                      start=(ht == 0), stop=(ht == HT - 1))
                    mm.then_inc(s["sGU"].h, 1)
                    s["sGU"].n += 1
                t.wait_ge(s["sHA"].h, (sbk + 1) * IT)
                for ss in range(4):
                    mm = None
                    for it in range(IT):
                        mm = t.matmul(pR2[:, ss:ss + 1],
                                      hact[:, it * 512 + ss * 128: it * 512 + (ss + 1) * 128],
                                      onc_bf[:], start=(it == 0), stop=(it == IT - 1))
                    mm.then_inc(s["sR2"].h, 1)
                    s["sR2"].n += 1
                for hob in range(4):
                    for it in range(IT):
                        k = (sbk * 4 + hob) * IT + it
                        t.wait_ge(s["sDW"].h, 16 * (k + 1))
                        if hob >= 1 and it == 0:
                            t.wait_ge(s["sDE"].h, sbk * 16 + hob * 4)
                        mm = None
                        for ss in range(4):
                            mm = t.matmul(pDN[ss][:],
                                          hact[:, it * 512 + ss * 128: it * 512 + (ss + 1) * 128],
                                          wdT[it % 4][:],
                                          start=(it == 0), stop=(it == IT - 1))
                        mm.then_inc(s["sDP"].h, 1)
                        s["sDP"].n += 1
                if sbk < NFS - 1:
                    t.wait_ge(s["sDE"].h, (sbk + 1) * 16)
            bdone(t)

        @blk.vector
        def _(v):
            bwait(v)
            for sbk in range(NFS):
                for it in range(IT):
                    gi = sbk * IT + it
                    v.wait_ge(s["sSil"].h, gi + 1)
                    v.wait_ge(s["sGU"].h, gi * 2 + 2)
                    v.scalar_tensor_tensor(hact[:, it * 512:(it + 1) * 512],
                                           pUf[:], su_col[:, it:it + 1],
                                           gs_t[it % 2][:], ALU.mult, ALU.mult
                                           ).then_inc(s["sHA"].h, 1)
                    s["sHA"].n += 1
                v.wait_ge(s["sR2"].h, (sbk + 1) * 4)
                v.tensor_scalar(rr[:], pR2[:], 128.0, None, ALU.mult)
                v.drain()
                for hob in range(4):
                    v.wait_ge(s["sDP"].h, (sbk * 4 + hob + 1) * IT)
                    v.wait_ge(s["sXR1"].h, 16 * 4 * (sbk + 1))
                    for ss in range(4):
                        k = hob * 4 + ss
                        if sbk * 16 + k >= 2:
                            v.wait_ge(s["sOut"].h, 16 * (sbk * 16 + k - 1))
                        v.scalar_tensor_tensor(outb[k % 2][:], pDN[ss][:], rr[:, ss:ss + 1],
                                               sd_bc[:, hob * 512:(hob + 1) * 512],
                                               ALU.subtract, ALU.mult)
                        v.tensor_tensor(outb[k % 2][:], outb[k % 2][:],
                                        x1l[:, ss * 2048 + hob * 512: ss * 2048 + (hob + 1) * 512],
                                        ALU.add).then_inc(s["sDE"].h, 1)
                        s["sDE"].n += 1
            bdone(v, drain=True)
    ph.close()
    top.close()
    return nc


_NC_CACHE = {}


def get_nc(S=2048):
    if S not in _NC_CACHE:
        _NC_CACHE[S] = build_nc(S)
    return _NC_CACHE[S]


def kernel(**inputs):
    x = np.ascontiguousarray(inputs["L0_attn_hidden_states"])
    nc = get_nc(2048)
    common = {
        "wq": np.ascontiguousarray(inputs["L0_attn_w_q"]),
        "wk": np.ascontiguousarray(inputs["L0_attn_w_k"]),
        "wv": np.ascontiguousarray(inputs["L0_attn_w_v"]),
        "wo": np.ascontiguousarray(inputs["L0_attn_w_o"]),
        "egw1": np.ascontiguousarray(inputs["L0_attn_eg_w1"]),
        "egb1": np.ascontiguousarray(inputs["L0_attn_eg_b1"]),
        "egw2": np.ascontiguousarray(inputs["L0_attn_eg_w2"]),
        "egb2": np.ascontiguousarray(inputs["L0_attn_eg_b2"]),
        "mgate": np.ascontiguousarray(inputs["L0_attn_memory_gate"]),
        "fmem": np.ascontiguousarray(inputs["L0_attn_fine_memories"]),
        "fnorm": np.ascontiguousarray(inputs["L0_attn_fine_memory_norms"]),
        "gw": np.ascontiguousarray(inputs["L0_ffn_gate_w"]),
        "gs": np.ascontiguousarray(inputs["L0_ffn_gate_scale"]),
        "uw": np.ascontiguousarray(inputs["L0_ffn_up_w"]),
        "us": np.ascontiguousarray(inputs["L0_ffn_up_scale"]),
        "dw": np.ascontiguousarray(inputs["L0_ffn_down_w"]),
        "ds": np.ascontiguousarray(inputs["L0_ffn_down_scale"]),
    }
    in_maps = []
    for b in range(NCORES):
        m = dict(common)
        m["x"] = np.ascontiguousarray(x[b])
        in_maps.append(m)
    res = run_bass_kernel_spmd(nc, in_maps, core_ids=list(range(NCORES)))
    return np.stack([res.results[b]["out"] for b in range(NCORES)], axis=0)


# version 07
# speedup vs baseline: 1.0021x; 1.0000x over previous
from contextlib import ExitStack

import numpy as np
import ml_dtypes
import concourse.bass as bass
import concourse.mybir as mybir
from concourse.bass_utils import run_bass_kernel_spmd

F32 = mybir.dt.float32
BF16 = mybir.dt.bfloat16
I32 = mybir.dt.int32
AF = mybir.ActivationFunctionType
ALU = mybir.AluOpType
AX = mybir.AxisListType

H = 2048
NH = 16
HD = 128
NB = 4
HT = H // 128
II = 8192
IT = II // 128
EPS = 1e-6
LN_EPS = 1e-5
NCORES = 8


class Sem:
    def __init__(self, nc, name):
        self.h = nc.alloc_semaphore(name=name)
        self.n = 0


def build_nc(S=2048, dbg=False):
    ST = S // 128
    NSUP = S // 512
    NFS = S // 512

    nc = bass.Bass("TRN2", target_bir_lowering=False, debug=False,
                   detect_race_conditions=False)

    x_in = nc.dram_tensor("x", [S, H], F32, kind="ExternalInput").ap()
    wq_in = nc.dram_tensor("wq", [H, H], F32, kind="ExternalInput").ap()
    wk_in = nc.dram_tensor("wk", [H, H], F32, kind="ExternalInput").ap()
    wv_in = nc.dram_tensor("wv", [H, H], F32, kind="ExternalInput").ap()
    wo_in = nc.dram_tensor("wo", [H, H], F32, kind="ExternalInput").ap()
    egw1_in = nc.dram_tensor("egw1", [512, H], F32, kind="ExternalInput").ap()
    egb1_in = nc.dram_tensor("egb1", [512], F32, kind="ExternalInput").ap()
    egw2_in = nc.dram_tensor("egw2", [1, 512], F32, kind="ExternalInput").ap()
    egb2_in = nc.dram_tensor("egb2", [1], F32, kind="ExternalInput").ap()
    mg_in = nc.dram_tensor("mgate", [NH], F32, kind="ExternalInput").ap()
    fm_in = nc.dram_tensor("fmem", [NB, NH, HD, HD], F32, kind="ExternalInput").ap()
    fn_in = nc.dram_tensor("fnorm", [NB, NH, HD], F32, kind="ExternalInput").ap()
    wg_in = nc.dram_tensor("gw", [II, H], I32, kind="ExternalInput").ap()
    sg_in = nc.dram_tensor("gs", [II], F32, kind="ExternalInput").ap()
    wu_in = nc.dram_tensor("uw", [II, H], I32, kind="ExternalInput").ap()
    su_in = nc.dram_tensor("us", [II], F32, kind="ExternalInput").ap()
    wd_in = nc.dram_tensor("dw", [H, II], I32, kind="ExternalInput").ap()
    sd_in = nc.dram_tensor("ds", [H], F32, kind="ExternalInput").ap()
    out = nc.dram_tensor("out", [S, H], F32, kind="ExternalOutput").ap()

    wq_bf = nc.dram_tensor("wq_bf", [H, H], BF16, kind="Internal").ap()
    wk_bf = nc.dram_tensor("wk_bf", [H, H], BF16, kind="Internal").ap()
    wv_bf = nc.dram_tensor("wv_bf", [H, H], BF16, kind="Internal").ap()
    wo_bf = nc.dram_tensor("wo_bf", [H, H], BF16, kind="Internal").ap()
    egw1_bf = nc.dram_tensor("egw1_bf", [512, H], BF16, kind="Internal").ap()
    wg_bf = nc.dram_tensor("wg_bf", [II, H], BF16, kind="Internal").ap()
    wu_bf = nc.dram_tensor("wu_bf", [II, H], BF16, kind="Internal").ap()
    wd_bf = nc.dram_tensor("wd_bf", [H, II], BF16, kind="Internal").ap()
    kscr = "ExternalOutput" if dbg else "Internal"
    xn_scr = nc.dram_tensor("xn_scr", [S, H], BF16, kind=kscr).ap()
    sq_scr = nc.dram_tensor("sq_scr", [H, S], BF16, kind=kscr).ap()
    skT_scr = nc.dram_tensor("skT_scr", [H, S], BF16, kind=kscr).ap()
    oc_scr = nc.dram_tensor("oc_scr", [S, H], BF16, kind=kscr).ap()
    y_scr = nc.dram_tensor("y_scr", [S, H], BF16, kind=kscr).ap()
    x1_scr = nc.dram_tensor("x1_scr", [S, H], F32, kind=kscr).ap()
    xn2_scr = nc.dram_tensor("xn2_scr", [S, H], BF16, kind=kscr).ap()

    ident_c = nc.inline_tensor(np.eye(128, dtype=np.float32), "ident_c").ap()
    maskT_c = nc.inline_tensor(np.triu(np.ones((128, 128), dtype=np.float32)), "maskT_c").ap()
    onc_bf_c = nc.inline_tensor(np.ones((128, 1), dtype=ml_dtypes.bfloat16), "onc_bf_c").ap()
    onc_f_c = nc.inline_tensor(np.ones((128, 1), dtype=np.float32), "onc_f_c").ap()
    onr_f_c = nc.inline_tensor(np.ones((1, 128), dtype=np.float32), "onr_f_c").ap()

    sems = {}
    for nm in ["bar",
               "sXL", "sSQ", "sVAR", "sSQT", "sXN", "sXNst",
               "sSm", "sFM", "sFMV", "sPE0", "sV0", "sS0",
               "sWL", "sWS",
               "sXnT", "sWq", "sQP", "sQX", "sQE", "sCN", "sCV", "sRe", "sSx",
               "sOCst", "sSQstr",
               "sGW", "sCrT", "sGPE", "sGS",
               "sKVw", "sKP", "sKX", "sKE", "sSKst", "sSkp", "sOCld",
               "sPEb", "sVb", "sSb", "sGb", "sYst",
               "sWo", "sYT", "sOP", "sXre", "sRv", "sXN2v", "sSQ2", "sVAR2", "sSQT2",
               "sX1st", "sXN2st",
               "sX2T", "sFW", "sGU", "sSil", "sHA", "sR2", "sDW", "sDP", "sDE",
               "sXR1", "sOut"]:
        sems[nm] = Sem(nc, nm)
    s = sems
    BAR_ENG = 4
    cur_phase = [0]

    def bwait(eng):
        eng.wait_ge(s["bar"].h, BAR_ENG * cur_phase[0])

    def bdone(eng, drain=False):
        if drain:
            eng.drain()
        eng.sem_inc(s["bar"].h, 1)

    def next_phase():
        cur_phase[0] += 1

    top = ExitStack()

    def sb(stack, name, shape, dtype, **kw):
        return stack.enter_context(nc.sbuf_tensor(name, shape, dtype, **kw))

    def psum(stack, name, shape, dtype=F32):
        return stack.enter_context(nc.psum_tensor(name, shape, dtype))

    identf = sb(top, "identf", [128, 128], F32)
    maskT = sb(top, "maskTs", [128, 128], F32)
    onc_bf = sb(top, "onc_bfs", [128, 1], BF16)
    onc_f = sb(top, "onc_fs", [128, 1], F32)
    onr_f = sb(top, "onr_fs", [1, 128], F32)
    sg_col = sb(top, "sg_col", [128, 64], F32)
    su_col = sb(top, "su_col", [128, 64], F32)
    sd_bc = sb(top, "sd_bc", [128, H], F32)
    b2_bc = sb(top, "b2_bc", [128, 1], F32)
    p_col = sb(top, "p_col", [128, ST], F32)

    def emit_prep(g, lo, hi):
        for ci in range(lo, hi):
            src_, dst_ = prep_chunks[ci]
            buf = wstg[ci % 4]
            if ci >= 4:
                g.wait_ge(s["sWS"].h, 16 * (ci - 3))
            g.dma_start(buf[:], src_).then_inc(s["sWL"].h, 16)
            s["sWL"].n += 16
            g.wait_ge(s["sWL"].h, 16 * (ci + 1))
            g.dma_start(dst_, buf[:]).then_inc(s["sWS"].h, 16)
            s["sWS"].n += 16

    attn = ExitStack()
    wstg = [sb(attn, f"wstg{i}", [128, 2048], BF16) for i in range(4)]
    xnT = sb(attn, "xnT", [128, HT * S], BF16, is_transpose_dest=True)
    F_bf = sb(attn, "F_bf", [128, NH * NB * 128], BF16)
    Mp_bf = sb(attn, "Mp_bf", [128, NH * 129], BF16)
    norms_f = sb(attn, "norms_f", [128, 64], F32)
    norms_hm = sb(attn, "norms_hm", [128, 64], BF16)
    z_sb = sb(attn, "z_sb", [128, NH], F32)
    g_bc = sb(attn, "g_bc", [128, NH], F32)
    invg_bc = sb(attn, "invg_bc", [128, NH], F32)
    indz_bc = sb(attn, "indz_bc", [128, 1], F32)
    indb_row = sb(attn, "indb_row", [1, NB], F32)
    bsums = sb(attn, "bsums", [1, NB], F32)
    g_row = sb(attn, "g_row", [1, NH], F32)
    b1_col = sb(attn, "b1_col", [128, 4], F32)
    egw2_bf = sb(attn, "egw2_bf", [128, 4], BF16)
    rcw_all = sb(attn, "rcw_all", [128, NH * ST * NB], F32)

    SMALL_LOADS = 13

    ph = ExitStack()
    xbuf = [sb(ph, f"xb{i}", [128, H], F32) for i in range(2)]
    xnb = [sb(ph, f"xnb{i}", [128, H], BF16) for i in range(2)]
    junk = sb(ph, "junk", [128, H], BF16)
    ssq = [sb(ph, f"ssq{i}", [128, 1], F32) for i in range(2)]
    s1 = [sb(ph, f"s1_{i}", [128, 1], F32) for i in range(2)]
    vtmp = sb(ph, "vtmp", [128, 1], F32)
    mean2 = sb(ph, "mean2", [128, 1], F32)
    var = [sb(ph, f"var{i}", [128, 1], F32) for i in range(2)]
    sqv = [sb(ph, f"sqv{i}", [128, 1], F32) for i in range(2)]
    rstd = [sb(ph, f"rstd{i}", [128, 1], F32) for i in range(2)]
    nbias = [sb(ph, f"nbias{i}", [128, 1], F32) for i in range(2)]
    nrm_raw = sb(ph, "nrm_raw", [64, 128], F32)
    sgr = sb(ph, "sgr", [64, 128], F32)
    sur = sb(ph, "sur", [64, 128], F32)
    b1r = sb(ph, "b1r", [4, 128], F32)
    w2r = sb(ph, "w2r", [4, 128], F32)
    mg_row = sb(ph, "mg_row", [1, NH], F32)
    invg_row = sb(ph, "invg_row", [1, NH], F32)
    b2_row = sb(ph, "b2_row", [1, 1], F32)
    sd_row = sb(ph, "sd_row", [1, H], F32)
    indz_f = sb(ph, "indz_f", [1, 1], F32)
    zt = sb(ph, "zt", [1, 1], F32)
    fstage = [sb(ph, f"fstage{i}", [128, 512], F32) for i in range(2)]
    mstage = sb(ph, "mstage", [128, 128], F32)
    egw2f = sb(ph, "egw2f", [128, 4], F32)
    pA = psum(ph, "pA", [128, 64])
    pB = psum(ph, "pB", [128, 512])
    pC = psum(ph, "pC", [128, 16])
    pD = psum(ph, "pD", [1, 16])

    prep_chunks = []
    for r in range(16):
        prep_chunks.append((wq_in[r * 128:(r + 1) * 128, :], wq_bf[r * 128:(r + 1) * 128, :]))
    for r in range(16):
        prep_chunks.append((wk_in[r * 128:(r + 1) * 128, :], wk_bf[r * 128:(r + 1) * 128, :]))
    for r in range(16):
        prep_chunks.append((wv_in[r * 128:(r + 1) * 128, :], wv_bf[r * 128:(r + 1) * 128, :]))
    for r in range(16):
        prep_chunks.append((wo_in[r * 128:(r + 1) * 128, :], wo_bf[r * 128:(r + 1) * 128, :]))
    for r in range(4):
        prep_chunks.append((egw1_in[r * 128:(r + 1) * 128, :], egw1_bf[r * 128:(r + 1) * 128, :]))
    for r in range(64):
        prep_chunks.append((wg_in[r * 128:(r + 1) * 128, :], wg_bf[r * 128:(r + 1) * 128, :]))
    for r in range(64):
        prep_chunks.append((wu_in[r * 128:(r + 1) * 128, :], wu_bf[r * 128:(r + 1) * 128, :]))
    for r in range(16):
        for c in range(4):
            prep_chunks.append((wd_in[r * 128:(r + 1) * 128, c * 2048:(c + 1) * 2048],
                                wd_bf[r * 128:(r + 1) * 128, c * 2048:(c + 1) * 2048]))
    N_PREP = len(prep_chunks)
    PREP_WQ, PREP_WK, PREP_WV, PREP_WO, PREP_EG = 16, 32, 48, 64, 68

    with nc.Block("p0", no_gpsimd_drain=True) as blk:
        @blk.gpsimd
        def _(g):
            emit_prep(g, 0, N_PREP)

        @blk.sync
        def _(sy):
            for i in range(ST):
                if i >= 2:
                    sy.wait_ge(s["sXN"].h, i - 1)
                    sy.dma_start(xn_scr[(i - 2) * 128:(i - 1) * 128, :],
                                 xnb[i % 2][:]).then_inc(s["sXNst"].h, 16)
                    s["sXNst"].n += 16
                sy.dma_start(xbuf[i % 2][:], x_in[i * 128:(i + 1) * 128, :]
                             ).then_inc(s["sXL"].h, 16)
                s["sXL"].n += 16
            for i in range(ST - 2, ST):
                sy.wait_ge(s["sXN"].h, i + 1)
                sy.dma_start(xn_scr[i * 128:(i + 1) * 128, :],
                             xnb[i % 2][:]).then_inc(s["sXNst"].h, 16)
                s["sXNst"].n += 16
            bdone(sy)

        @blk.scalar
        def _(sc):
            small = [(ident_c, identf[:]), (maskT_c, maskT[:]),
                     (onc_bf_c, onc_bf[:]), (onc_f_c, onc_f[:]), (onr_f_c, onr_f[:]),
                     (fn_in.rearrange("m h d -> (m h) d"), nrm_raw[:]),
                     (sg_in.rearrange("(a b) -> a b", b=128), sgr[:]),
                     (su_in.rearrange("(a b) -> a b", b=128), sur[:]),
                     (egb1_in.rearrange("(a b) -> a b", b=128), b1r[:]),
                     (egw2_in.rearrange("a (b c) -> (a b) c", c=128), w2r[:]),
                     (mg_in.rearrange("(a b) -> a b", a=1), mg_row[:]),
                     (egb2_in.rearrange("(a b) -> a b", a=1), b2_row[:]),
                     (sd_in.rearrange("(a b) -> a b", a=1), sd_row[:])]
            assert len(small) == SMALL_LOADS
            for src, dst in small:
                sc.dma_start(dst, src).then_inc(s["sSm"].h, 16)
                s["sSm"].n += 16
            sc.wait_ge(s["sSm"].h, 16 * SMALL_LOADS)
            sc.activation(g_row[:], mg_row[:], AF.Sigmoid).then_inc(s["sS0"].h, 1)
            s["sS0"].n += 1
            for h in range(NH):
                if h >= 2:
                    sc.wait_ge(s["sFMV"].h, h - 1)
                for m in range(NB):
                    sc.dma_start(fstage[h % 2][:, m * 128:(m + 1) * 128],
                                 fm_in[m, h]).then_inc(s["sFM"].h, 16)
                    s["sFM"].n += 16
            for i in range(ST):
                sc.wait_ge(s["sXL"].h, 16 * (i + 1))
                if i >= 2:
                    sc.wait_ge(s["sVAR"].h, i - 1)
                sc.activation(junk[:], xbuf[i % 2][:], AF.Square,
                              accum_out=ssq[i % 2][:])
                sc.drain()
                sc.sem_inc(s["sSQ"].h, 1)
                s["sSQ"].n += 1
                sc.wait_ge(s["sVAR"].h, i + 1)
                sc.activation(sqv[i % 2][:], var[i % 2][:], AF.Sqrt
                              ).then_inc(s["sSQT"].h, 1)
                s["sSQT"].n += 1
            bdone(sc, drain=True)

        @blk.tensor
        def _(t):
            t.wait_ge(s["sSm"].h, 16 * SMALL_LOADS)
            t.transpose(pA[:], nrm_raw[:], identf[0:64, 0:64]).then_inc(s["sPE0"].h, 1)
            t.wait_ge(s["sV0"].h, 1)
            t.transpose(pA[:], sgr[:], identf[0:64, 0:64]).then_inc(s["sPE0"].h, 1)
            t.wait_ge(s["sV0"].h, 2)
            t.transpose(pA[:], sur[:], identf[0:64, 0:64]).then_inc(s["sPE0"].h, 1)
            t.wait_ge(s["sV0"].h, 3)
            t.transpose(pA[0:128, 0:4], b1r[:], identf[0:4, 0:4]).then_inc(s["sPE0"].h, 1)
            t.wait_ge(s["sV0"].h, 4)
            t.transpose(pA[0:128, 0:4], w2r[:], identf[0:4, 0:4]).then_inc(s["sPE0"].h, 1)
            for m in range(NB):
                t.wait_ge(s["sV0"].h, 5 + m)
                t.matmul(pD[:], onc_f[:], norms_f[:, m * 16:(m + 1) * 16],
                         start=True, stop=True).then_inc(s["sPE0"].h, 1)
            t.wait_ge(s["sV0"].h, 10)
            t.matmul(pC[:], onr_f[:], g_row[:], start=True, stop=True
                     ).then_inc(s["sPE0"].h, 1)
            t.wait_ge(s["sV0"].h, 11)
            t.matmul(pC[:], onr_f[:], invg_row[:], start=True, stop=True
                     ).then_inc(s["sPE0"].h, 1)
            t.wait_ge(s["sV0"].h, 12)
            t.matmul(pC[0:128, 0:1], onr_f[:], indz_f[:], start=True, stop=True
                     ).then_inc(s["sPE0"].h, 1)
            t.wait_ge(s["sV0"].h, 13)
            t.matmul(pC[0:128, 0:1], onr_f[:], b2_row[:], start=True, stop=True
                     ).then_inc(s["sPE0"].h, 1)
            for b in range(4):
                t.wait_ge(s["sV0"].h, 14 + b)
                t.matmul(pB[:], onr_f[:], sd_row[0:1, b * 512:(b + 1) * 512],
                         start=True, stop=True).then_inc(s["sPE0"].h, 1)
            s["sPE0"].n = 17
            bdone(t)

        @blk.vector
        def _(v):
            v.wait_ge(s["sPE0"].h, 1)
            v.tensor_copy(norms_f[:], pA[:])
            v.tensor_copy(norms_hm.rearrange("p (hh m) -> p hh m", m=NB),
                          pA.rearrange("p (m hh) -> p hh m", hh=NH))
            v.tensor_add(z_sb[:], norms_f[:, 0:16], norms_f[:, 16:32])
            v.drain()
            v.tensor_add(z_sb[:], z_sb[:], norms_f[:, 32:48])
            v.drain()
            v.tensor_add(z_sb[:], z_sb[:], norms_f[:, 48:64]).then_inc(s["sV0"].h, 1)
            v.wait_ge(s["sPE0"].h, 2)
            v.tensor_copy(sg_col[:], pA[:]).then_inc(s["sV0"].h, 1)
            v.wait_ge(s["sPE0"].h, 3)
            v.tensor_copy(su_col[:], pA[:]).then_inc(s["sV0"].h, 1)
            v.wait_ge(s["sPE0"].h, 4)
            v.tensor_copy(b1_col[:], pA[0:128, 0:4]).then_inc(s["sV0"].h, 1)
            v.wait_ge(s["sPE0"].h, 5)
            v.tensor_copy(egw2f[:], pA[0:128, 0:4])
            v.drain()
            v.tensor_copy(egw2_bf[:], egw2f[:]).then_inc(s["sV0"].h, 1)
            for m in range(NB):
                v.wait_ge(s["sPE0"].h, 6 + m)
                v.reduce_sum(bsums[0:1, m:m + 1], pD[:], AX.X).then_inc(s["sV0"].h, 1)
            v.drain()
            v.tensor_scalar(indb_row[:], bsums[:], EPS, None, ALU.is_ge)
            v.reduce_sum(zt[:], bsums[:], AX.X)
            v.drain()
            v.tensor_scalar(indz_f[:], zt[:], EPS, None, ALU.is_ge)
            v.drain()
            v.wait_ge(s["sS0"].h, 1)
            v.tensor_scalar(invg_row[:], g_row[:], -1.0, 1.0, ALU.mult, ALU.add
                            ).then_inc(s["sV0"].h, 1)
            v.wait_ge(s["sPE0"].h, 10)
            v.tensor_copy(g_bc[:], pC[:]).then_inc(s["sV0"].h, 1)
            v.wait_ge(s["sPE0"].h, 11)
            v.tensor_copy(invg_bc[:], pC[:]).then_inc(s["sV0"].h, 1)
            v.wait_ge(s["sPE0"].h, 12)
            v.tensor_copy(indz_bc[:], pC[0:128, 0:1]).then_inc(s["sV0"].h, 1)
            v.wait_ge(s["sPE0"].h, 13)
            v.tensor_copy(b2_bc[:], pC[0:128, 0:1]).then_inc(s["sV0"].h, 1)
            for b in range(4):
                v.wait_ge(s["sPE0"].h, 14 + b)
                v.tensor_copy(sd_bc[:, b * 512:(b + 1) * 512], pB[:]).then_inc(s["sV0"].h, 1)
            s["sV0"].n = 18
            for h in range(NH):
                v.wait_ge(s["sFM"].h, 16 * 4 * (h + 1))
                fs = fstage[h % 2]
                v.tensor_add(mstage[:], fs[:, 0:128], fs[:, 128:256])
                v.tensor_add(mstage[:], mstage[:], fs[:, 256:384])
                v.tensor_add(mstage[:], mstage[:], fs[:, 384:512])
                v.tensor_copy(Mp_bf[:, h * 129:h * 129 + 128], mstage[:])
                v.tensor_copy(Mp_bf[:, h * 129 + 128:h * 129 + 129], z_sb[:, h:h + 1])
                v.tensor_copy(F_bf[:, h * 512:(h + 1) * 512], fs[:]).then_inc(s["sFMV"].h, 1)
                s["sFMV"].n += 1
            for i in range(ST + 1):
                if i < ST:
                    v.wait_ge(s["sXL"].h, 16 * (i + 1))
                    v.reduce_sum(s1[i % 2][:], xbuf[i % 2][:], AX.X)
                if i >= 1:
                    j = i - 1
                    if i == ST:
                        v.drain()
                    v.scalar_tensor_tensor(nbias[j % 2][:], s1[j % 2][:], -1.0 / H,
                                           rstd[j % 2][:], ALU.mult, ALU.mult)
                if i < ST:
                    v.wait_ge(s["sSQ"].h, i + 1)
                    v.tensor_scalar(vtmp[:], ssq[i % 2][:], 1.0 / H, LN_EPS,
                                    ALU.mult, ALU.add)
                    v.scalar_tensor_tensor(mean2[:], s1[i % 2][:], 1.0 / (H * float(H)),
                                           s1[i % 2][:], ALU.mult, ALU.mult)
                if i >= 1:
                    j = i - 1
                    if j >= 2:
                        v.wait_ge(s["sXNst"].h, 16 * (j - 1))
                    if i == ST:
                        v.drain()
                    v.tensor_scalar(xnb[j % 2][:], xbuf[j % 2][:], rstd[j % 2][:, 0:1],
                                    nbias[j % 2][:, 0:1], ALU.mult, ALU.add
                                    ).then_inc(s["sXN"].h, 1)
                    s["sXN"].n += 1
                if i < ST:
                    if i == 0:
                        v.drain()
                    v.tensor_sub(var[i % 2][:], vtmp[:], mean2[:]).then_inc(s["sVAR"].h, 1)
                    s["sVAR"].n += 1
                    v.wait_ge(s["sSQT"].h, i + 1)
                    v.reciprocal(rstd[i % 2][:], sqv[i % 2][:])
            bdone(v, drain=True)
    ph.close()
    next_phase()

    CVH = ST + 2
    CNH = ST + 3

    def cv(h, what, c=0):
        base = h * CVH
        return {"nf": base + 1, "w": base + 2, "oc": base + 2 + c + 1}[what]

    def cn(h, what, c=0):
        base = h * CNH
        return {"nf": base + 1, "rel": base + 2, "wbc": base + 3, "co": base + 3 + c + 1}[what]

    ph = ExitStack()
    wqT = [sb(ph, f"wqT{i}", [128, HT * 512], BF16, is_transpose_dest=True) for i in range(2)]
    sqTg = [sb(ph, f"sqTg{i}", [128, 4 * S], BF16) for i in range(2)]
    ocb = sb(ph, "ocb", [128, S], BF16)
    nf_sb = sb(ph, "nf_sb", [128, NB * ST], F32)
    rc_sb = sb(ph, "rc_sb", [128, NB * ST], F32)
    rel_r = sb(ph, "rel_r", [1, NB], F32)
    rel_m = sb(ph, "rel_m", [1, 1], F32)
    rel_e = sb(ph, "rel_e", [1, NB], F32)
    rel_x = sb(ph, "rel_x", [1, NB], F32)
    rel_s = sb(ph, "rel_s", [1, 1], F32)
    w_row = sb(ph, "w_row", [1, NB], F32)
    w_pre = sb(ph, "w_pre", [1, NB], F32)
    wbc = sb(ph, "wbc", [128, NB], F32)
    ebuf = [sb(ph, f"ebuf{i}", [128, 512], F32) for i in range(2)]
    rbuf = [sb(ph, f"rbuf{i}", [128, 512], F32) for i in range(2)]
    nct = [sb(ph, f"nct{i}", [128, 1], F32) for i in range(2)]
    recc = [sb(ph, f"recc{i}", [128, 1], F32) for i in range(2)]
    pQ = [psum(ph, f"pQ{i}", [128, 512]) for i in range(2)]
    pAC = [psum(ph, f"pAC{i}", [128, 129]) for i in range(2)]
    pNF = psum(ph, "pNF", [128, NB * ST])
    pREL = psum(ph, "pREL", [1, NB])
    pWBC = psum(ph, "pWBC", [128, NB])

    with nc.Block("a2a", no_gpsimd_drain=True) as blk:
        @blk.sync
        def _(sy):
            bwait(sy)
            sy.wait_ge(s["sXNst"].h, 16 * ST)
            for ht in range(HT):
                for j in range(NSUP):
                    sy.dma_start(xnT[:, ht * S + j * 512: ht * S + (j + 1) * 512],
                                 xn_scr[j * 512:(j + 1) * 512, ht * 128:(ht + 1) * 128],
                                 transpose=True).then_inc(s["sXnT"].h, 16)
                    s["sXnT"].n += 16
            sy.wait_ge(s["sWS"].h, 16 * PREP_WQ)
            def load_wqT(es):
                for ht in range(HT):
                    sy.dma_start(wqT[es % 2][:, ht * 512:(ht + 1) * 512],
                                 wq_bf[es * 512:(es + 1) * 512, ht * 128:(ht + 1) * 128],
                                 transpose=True).then_inc(s["sWq"].h, 16)
                    s["sWq"].n += 16
            load_wqT(0)
            load_wqT(1)
            for es in range(4):
                for hl in range(4):
                    h = es * 4 + hl
                    sy.wait_ge(s["sCV"].h, cv(h, "oc", ST - 1))
                    sy.dma_start(
                        oc_scr[:, h * 128:(h + 1) * 128].rearrange("(c p) e -> p c e", p=128),
                        ocb.rearrange("p (c e) -> p c e", e=128)).then_inc(s["sOCst"].h, 16)
                    s["sOCst"].n += 16
                sy.wait_ge(s["sQE"].h, (es + 1) * 4 * NSUP)
                sy.dma_start(
                    sq_scr[es * 512:(es + 1) * 512, :].rearrange("(el p) t -> p el t", p=128),
                    sqTg[es % 2].rearrange("p (el t) -> p el t", el=4)).then_inc(s["sSQstr"].h, 16)
                s["sSQstr"].n += 16
                if es + 2 < 4:
                    sy.wait_ge(s["sQP"].h, (es + 1) * 4 * NSUP)
                    load_wqT(es + 2)
            bdone(sy)

        @blk.tensor
        def _(t):
            bwait(t)
            t.wait_ge(s["sXnT"].h, 16 * HT * NSUP)
            for es in range(4):
                t.wait_ge(s["sWq"].h, 16 * HT * (es + 1))
                if es >= 2:
                    t.wait_ge(s["sSQstr"].h, 16 * (es - 1))
                for el in range(4):
                    for j in range(NSUP):
                        gq = es * 4 * NSUP + el * NSUP + j
                        pq = pQ[gq % 2]
                        if gq >= 2:
                            t.wait_ge(s["sQX"].h, gq - 1)
                        mm = None
                        for ht in range(HT):
                            mm = t.matmul(pq[:],
                                          wqT[es % 2][:, ht * 512 + el * 128: ht * 512 + (el + 1) * 128],
                                          xnT[:, ht * S + j * 512: ht * S + (j + 1) * 512],
                                          start=(ht == 0), stop=(ht == HT - 1))
                        mm.then_inc(s["sQP"].h, 1)
                        s["sQP"].n += 1
                for hl in range(4):
                    h = es * 4 + hl
                    t.wait_ge(s["sQE"].h, es * 4 * NSUP + (hl + 1) * NSUP)
                    if h >= 1:
                        t.wait_ge(s["sCV"].h, cv(h - 1, "w"))
                    mm = None
                    for c in range(ST):
                        mm = t.matmul(pNF[:, c * NB:(c + 1) * NB],
                                      sqTg[es % 2][:, hl * S + c * 128:hl * S + (c + 1) * 128],
                                      norms_hm[:, h * NB:(h + 1) * NB],
                                      start=True, stop=True)
                    mm.then_inc(s["sCN"].h, 1)
                    s["sCN"].n += 1
                    t.wait_ge(s["sCV"].h, cv(h, "nf"))
                    mm = None
                    for c in range(ST):
                        mm = t.matmul(pREL[:], onc_f[:], nf_sb[:, c * NB:(c + 1) * NB],
                                      start=(c == 0), stop=(c == ST - 1))
                    mm.then_inc(s["sCN"].h, 1)
                    s["sCN"].n += 1
                    t.wait_ge(s["sCV"].h, cv(h, "w"))
                    t.matmul(pWBC[:], onr_f[:], w_row[:], start=True, stop=True
                             ).then_inc(s["sCN"].h, 1)
                    s["sCN"].n += 1
                    for c in range(ST):
                        k = h * ST + c
                        if k >= 2:
                            h2, c2 = (k - 2) // ST, (k - 2) % ST
                            t.wait_ge(s["sCV"].h, cv(h2, "oc", c2))
                        t.matmul(pAC[c % 2][:],
                                 sqTg[es % 2][:, hl * S + c * 128:hl * S + (c + 1) * 128],
                                 Mp_bf[:, h * 129:(h + 1) * 129],
                                 start=True, stop=True).then_inc(s["sCN"].h, 1)
                        s["sCN"].n += 1
            bdone(t)

        @blk.scalar
        def _(sc):
            bwait(sc)
            for es in range(4):
                for el in range(4):
                    for j in range(NSUP):
                        gq = es * 4 * NSUP + el * NSUP + j
                        pq = pQ[gq % 2]
                        sc.wait_ge(s["sQP"].h, gq + 1)
                        if gq >= 2:
                            sc.wait_ge(s["sQE"].h, gq - 1)
                        sc.activation(ebuf[gq % 2][:], pq[:], AF.Exp)
                        sc.activation(rbuf[gq % 2][:], pq[:], AF.Relu).then_inc(s["sQX"].h, 1)
                        s["sQX"].n += 1
                for hl in range(4):
                    h = es * 4 + hl
                    sc.wait_ge(s["sRe"].h, h + 1)
                    sc.activation(rel_x[:], rel_e[:], AF.Exp).then_inc(s["sSx"].h, 1)
                    s["sSx"].n += 1
            bdone(sc, drain=True)

        @blk.vector
        def _(v):
            bwait(v)
            for es in range(4):
                for el in range(4):
                    for j in range(NSUP):
                        gq = es * 4 * NSUP + el * NSUP + j
                        v.wait_ge(s["sQX"].h, gq + 1)
                        v.scalar_tensor_tensor(
                            sqTg[es % 2][:, (el * S + j * 512):(el * S + (j + 1) * 512)],
                            ebuf[gq % 2][:], 1.0, rbuf[gq % 2][:], ALU.min, ALU.add
                        ).then_inc(s["sQE"].h, 1)
                        s["sQE"].n += 1
                for hl in range(4):
                    h = es * 4 + hl
                    v.wait_ge(s["sCN"].h, cn(h, "nf"))
                    v.tensor_copy(nf_sb[:], pNF[:]).then_inc(s["sCV"].h, 1)
                    s["sCV"].n += 1
                    v.wait_ge(s["sCN"].h, cn(h, "rel"))
                    v.tensor_scalar(rel_r[:], pREL[:], 1.0 / S, None, ALU.mult)
                    v.tensor_scalar(rc_sb[:], nf_sb[:], EPS, None, ALU.max)
                    v.reduce_max(rel_m[:], rel_r[:], AX.X)
                    v.reciprocal(rc_sb[:], rc_sb[:])
                    v.tensor_single_scalar(rel_e[:], rel_r[:], rel_m[0:1, 0:1], ALU.subtract
                                           ).then_inc(s["sRe"].h, 1)
                    s["sRe"].n += 1
                    v.wait_ge(s["sSx"].h, h + 1)
                    v.tensor_mul(w_pre[:], rel_x[:], indb_row[:])
                    v.reduce_sum(rel_s[:], rel_x[:], AX.X)
                    v.drain()
                    v.reciprocal(rel_s[:], rel_s[:])
                    v.drain()
                    v.tensor_single_scalar(w_row[:], w_pre[:], rel_s[0:1, 0:1], ALU.mult
                                           ).then_inc(s["sCV"].h, 1)
                    s["sCV"].n += 1
                    v.wait_ge(s["sCN"].h, cn(h, "wbc"))
                    v.tensor_copy(wbc[:], pWBC[:])
                    v.drain()
                    rc3 = rc_sb.rearrange("p (c m) -> p c m", m=NB)
                    rw3 = rcw_all.rearrange("p (hh c m) -> p hh c m", hh=NH, m=NB)
                    for m in range(NB):
                        v.tensor_scalar(rw3[:, h, :, m], rc3[:, :, m], wbc[:, m:m + 1],
                                        None, ALU.mult)
                    if h >= 1:
                        v.wait_ge(s["sOCst"].h, 16 * h)
                    for c in range(ST + 1):
                        if c < ST:
                            v.wait_ge(s["sCN"].h, cn(h, "co", c))
                            v.tensor_scalar(nct[c % 2][:], pAC[c % 2][:, 128:129], EPS,
                                            None, ALU.max)
                        if c >= 1:
                            if c == ST:
                                v.drain()
                            v.tensor_scalar(ocb[:, (c - 1) * 128:c * 128],
                                            pAC[(c - 1) % 2][:, 0:128],
                                            recc[(c - 1) % 2][:, 0:1], indz_bc[:, 0:1],
                                            ALU.mult, ALU.mult).then_inc(s["sCV"].h, 1)
                            s["sCV"].n += 1
                        if c < ST:
                            if c == 0:
                                v.drain()
                            v.reciprocal(recc[c % 2][:], nct[c % 2][:])
            bdone(v, drain=True)
    ph.close()
    next_phase()

    ph = ExitStack()
    egw1T = sb(ph, "egw1T", [128, 16 * 512], BF16, is_transpose_dest=True)
    crT = [sb(ph, f"crT{i}", [128, 16 * 512], BF16, is_transpose_dest=True) for i in range(2)]
    hid_bf = sb(ph, "hid_bf", [128, 4 * 512], BF16)
    pHh = [psum(ph, f"pHh{i}", [128, 512]) for i in range(2)]
    pGP = psum(ph, "pGP", [128, 4])

    def hid_sgs(k_h):
        return (k_h // 4) * 8 + (k_h % 4) + 1

    with nc.Block("gprob", no_gpsimd_drain=True) as blk:
        @blk.sync
        def _(sy):
            bwait(sy)
            sy.wait_ge(s["sOCst"].h, 16 * NH)
            sy.wait_ge(s["sSQstr"].h, 16 * 4)
            sy.wait_ge(s["sWS"].h, 16 * PREP_EG)
            for et in range(HT):
                sy.dma_start(egw1T[:, et * 512:(et + 1) * 512],
                             egw1_bf[0:512, et * 128:(et + 1) * 128],
                             transpose=True).then_inc(s["sGW"].h, 16)
                s["sGW"].n += 16
            sy.wait_ge(s["sOCst"].h, 16 * NH)
            for j in range(NSUP):
                if j >= 2:
                    sy.wait_ge(s["sGPE"].h, (j - 1) * 8)
                for et in range(HT):
                    sy.dma_start(crT[j % 2][:, et * 512:(et + 1) * 512],
                                 oc_scr[j * 512:(j + 1) * 512, et * 128:(et + 1) * 128],
                                 transpose=True).then_inc(s["sCrT"].h, 16)
                    s["sCrT"].n += 16
            bdone(sy)

        @blk.tensor
        def _(t):
            bwait(t)
            t.wait_ge(s["sGW"].h, 16 * HT)
            for j in range(NSUP):
                t.wait_ge(s["sCrT"].h, 16 * HT * (j + 1))
                for ft in range(4):
                    k_h = j * 4 + ft
                    if k_h >= 2:
                        t.wait_ge(s["sGS"].h, hid_sgs(k_h - 2))
                    mm = None
                    for et in range(HT):
                        mm = t.matmul(pHh[ft % 2][:],
                                      egw1T[:, et * 512 + ft * 128: et * 512 + (ft + 1) * 128],
                                      crT[j % 2][:, et * 512:(et + 1) * 512],
                                      start=(et == 0), stop=(et == HT - 1))
                    mm.then_inc(s["sGPE"].h, 1)
                    s["sGPE"].n += 1
                t.wait_ge(s["sGS"].h, j * 8 + 4)
                for cc in range(4):
                    mm = None
                    for ft in range(4):
                        mm = t.matmul(pGP[:, cc:cc + 1],
                                      hid_bf[:, ft * 512 + cc * 128: ft * 512 + (cc + 1) * 128],
                                      egw2_bf[:, ft:ft + 1],
                                      start=(ft == 0), stop=(ft == 3))
                    mm.then_inc(s["sGPE"].h, 1)
                    s["sGPE"].n += 1
            bdone(t)

        @blk.scalar
        def _(sc):
            bwait(sc)
            for j in range(NSUP):
                for ft in range(4):
                    sc.wait_ge(s["sGPE"].h, j * 8 + ft + 1)
                    sc.activation(hid_bf[:, ft * 512:(ft + 1) * 512], pHh[ft % 2][:],
                                  AF.Relu, bias=b1_col[:, ft:ft + 1]).then_inc(s["sGS"].h, 1)
                    s["sGS"].n += 1
                for cc in range(4):
                    c = j * 4 + cc
                    sc.wait_ge(s["sGPE"].h, j * 8 + 4 + cc + 1)
                    sc.activation(p_col[:, c:c + 1], pGP[:, cc:cc + 1], AF.Sigmoid,
                                  bias=b2_bc[:, 0:1]).then_inc(s["sGS"].h, 1)
                    s["sGS"].n += 1
            bdone(sc, drain=True)

        @blk.vector
        def _(v):
            bwait(v)
            bdone(v)
    ph.close()
    next_phase()

    for grp in range(4):
        ph = ExitStack()
        wkvT = sb(ph, f"wkvT{grp}", [128, HT * 512], BF16, is_transpose_dest=True)
        wkT = wkvT
        wvT = wkvT
        sqTb = sb(ph, f"sqTb{grp}", [128, 4 * S], BF16)
        skTg = sb(ph, f"skTg{grp}", [128, 4 * S], BF16)
        vg = sb(ph, f"vg{grp}", [128, 4 * S], BF16)
        skp = sb(ph, f"skp{grp}", [128, 2 * S], BF16, is_transpose_dest=True)
        ocl = sb(ph, f"ocl{grp}", [128, 4 * 128], BF16)
        ybuf = sb(ph, f"ybuf{grp}", [128, S], BF16)
        S_sb = sb(ph, f"S_sb{grp}", [128, 129], F32)
        S_bfs = sb(ph, f"S_bfs{grp}", [128, 129], BF16)
        ebuf2 = [sb(ph, f"eb2_{grp}_{i}", [128, 512], F32) for i in range(2)]
        rbuf2 = [sb(ph, f"rb2_{grp}_{i}", [128, 512], F32) for i in range(2)]
        lo = sb(ph, f"lo{grp}", [128, 128], F32)
        of = sb(ph, f"of{grp}", [128, 128], F32)
        ocf = sb(ph, f"ocf{grp}", [128, 128], F32)
        t1b = sb(ph, f"t1b{grp}", [128, 128], F32)
        memb = sb(ph, f"memb{grp}", [128, 128], F32)
        t2b = sb(ph, f"t2b{grp}", [128, 128], F32)
        mst = sb(ph, f"mst{grp}", [128, 128], BF16)
        dent = sb(ph, f"dent{grp}", [128, 1], F32)
        recl = sb(ph, f"recl{grp}", [128, 1], F32)
        pKV = [psum(ph, f"pKV{grp}_{i}", [128, 512]) for i in range(2)]
        pSC = psum(ph, f"pSC{grp}", [128, 128])
        pNM = [psum(ph, f"pNM{grp}_{i}", [128, 129]) for i in range(2)]
        pDS = psum(ph, f"pDS{grp}", [128, 129])
        pFN = psum(ph, f"pFN{grp}", [128, 512])
        pDEN = psum(ph, f"pDEN{grp}", [128, 1])

        NPRJ = 4 * NSUP + ST
        kp0 = s["sKP"].n
        kx0 = s["sKX"].n
        ke0 = s["sKE"].n
        pe0 = s["sPEb"].n
        vb0 = s["sVb"].n
        sb0 = s["sSb"].n
        gb0 = s["sGb"].n
        ocld0 = s["sOCld"].n
        skp0 = s["sSkp"].n
        skst0 = s["sSKst"].n
        yst0 = s["sYst"].n

        PE_C = 4
        V_C = 6
        G_C = 2

        def pe_scan(hl, c, k):
            return pe0 + (hl * ST + c) * PE_C + k

        def v_scan(hl, c, k):
            return vb0 + (hl * ST + c) * V_C + k

        def sb_cnt(hl, c):
            return sb0 + hl * (ST - 1) + c + 1

        with nc.Block(f"a2b{grp}", no_gpsimd_drain=True) as blk:
            @blk.sync
            def _(sy):
                bwait(sy)
                if grp > 0:
                    sy.wait_ge(s["sPEb"].h, pe0)
                    sy.wait_ge(s["sYst"].h, yst0)
                sy.wait_ge(s["sWS"].h, 16 * PREP_WV)
                for ht in range(HT):
                    sy.dma_start(wkT[:, ht * 512:(ht + 1) * 512],
                                 wk_bf[grp * 512:(grp + 1) * 512, ht * 128:(ht + 1) * 128],
                                 transpose=True).then_inc(s["sKVw"].h, 16)
                    s["sKVw"].n += 16
                sy.wait_ge(s["sSQstr"].h, 16 * (grp + 1))
                for el in range(4):
                    sy.dma_start(sqTb[:, el * S:(el + 1) * S],
                                 sq_scr[grp * 512 + el * 128:grp * 512 + (el + 1) * 128, :]
                                 ).then_inc(s["sKVw"].h, 16)
                    s["sKVw"].n += 16
                sy.wait_ge(s["sKP"].h, kp0 + 4 * NSUP)
                for ht in range(HT):
                    sy.dma_start(wvT[:, ht * 512:(ht + 1) * 512],
                                 wv_bf[grp * 512:(grp + 1) * 512, ht * 128:(ht + 1) * 128],
                                 transpose=True).then_inc(s["sKVw"].h, 16)
                    s["sKVw"].n += 16
                sy.wait_ge(s["sKE"].h, ke0 + 4 * NSUP)
                sy.dma_start(
                    skT_scr[grp * 512:(grp + 1) * 512, :].rearrange("(el p) t -> p el t", p=128),
                    skTg.rearrange("p (el t) -> p el t", el=4)).then_inc(s["sSKst"].h, 16)
                s["sSKst"].n += 16
                sy.wait_ge(s["sSKst"].h, skst0 + 16)

                def load_skp(hl):
                    if hl >= 2:
                        sy.wait_ge(s["sPEb"].h, pe0 + (hl - 1) * ST * PE_C)
                    for c in range(ST):
                        sy.dma_start(
                            skp[:, ((hl % 2) * ST + c) * 128:((hl % 2) * ST + c + 1) * 128],
                            skT_scr[(grp * 4 + hl) * 128:(grp * 4 + hl + 1) * 128,
                                    c * 128:(c + 1) * 128],
                            transpose=True).then_inc(s["sSkp"].h, 16)
                        s["sSkp"].n += 16
                load_skp(0)
                load_skp(1)
                for hl in range(4):
                    sy.wait_ge(s["sVb"].h, v_scan(hl, ST - 1, V_C))
                    h = grp * 4 + hl
                    sy.dma_start(
                        y_scr[:, h * 128:(h + 1) * 128].rearrange("(c p) e -> p c e", p=128),
                        ybuf.rearrange("p (c e) -> p c e", e=128)).then_inc(s["sYst"].h, 16)
                    s["sYst"].n += 16
                    if hl + 2 < 4:
                        load_skp(hl + 2)
                bdone(sy)

            @blk.scalar
            def _(sc):
                bwait(sc)
                if grp > 0:
                    sc.wait_ge(s["sGb"].h, gb0)
                for el in range(4):
                    for j in range(NSUP):
                        gq = el * NSUP + j
                        sc.wait_ge(s["sKP"].h, kp0 + gq + 1)
                        if gq >= 2:
                            sc.wait_ge(s["sKE"].h, ke0 + gq - 1)
                        sc.activation(ebuf2[gq % 2][:], pKV[gq % 2][:], AF.Exp)
                        sc.activation(rbuf2[gq % 2][:], pKV[gq % 2][:], AF.Relu
                                      ).then_inc(s["sKX"].h, 1)
                        s["sKX"].n += 1
                for k0 in range(min(4, 4 * ST)):
                    hl0, c0 = k0 // ST, k0 % ST
                    h0 = grp * 4 + hl0
                    sc.dma_start(ocl[:, (k0 % 4) * 128:((k0 % 4) + 1) * 128],
                                 oc_scr[c0 * 128:(c0 + 1) * 128, h0 * 128:(h0 + 1) * 128]
                                 ).then_inc(s["sOCld"].h, 16)
                    s["sOCld"].n += 16
                for hl in range(4):
                    for c in range(ST):
                        k = hl * ST + c
                        if c < ST - 1:
                            sc.wait_ge(s["sVb"].h, v_scan(hl, c, 4))
                            sc.copy(S_bfs[:], S_sb[:]).then_inc(s["sSb"].h, 1)
                            s["sSb"].n += 1
                        kn = k + 4
                        if kn < 4 * ST:
                            hln, cn_ = kn // ST, kn % ST
                            hn = grp * 4 + hln
                            sc.wait_ge(s["sGb"].h, gb0 + k * 2 + 1)
                            sc.dma_start(ocl[:, (kn % 4) * 128:((kn % 4) + 1) * 128],
                                         oc_scr[cn_ * 128:(cn_ + 1) * 128,
                                                hn * 128:(hn + 1) * 128]
                                         ).then_inc(s["sOCld"].h, 16)
                            s["sOCld"].n += 16
                bdone(sc, drain=True)

            @blk.tensor
            def _(t):
                bwait(t)
                t.wait_ge(s["sKVw"].h, 16 * ((2 * HT + 4) * grp + HT + 4))
                for el in range(4):
                    for j in range(NSUP):
                        gq = el * NSUP + j
                        pq = pKV[gq % 2]
                        if gq >= 2:
                            t.wait_ge(s["sKX"].h, kx0 + gq - 1)
                        mm = None
                        for ht in range(HT):
                            mm = t.matmul(pq[:],
                                          wkT[:, ht * 512 + el * 128: ht * 512 + (el + 1) * 128],
                                          xnT[:, ht * S + j * 512: ht * S + (j + 1) * 512],
                                          start=(ht == 0), stop=(ht == HT - 1))
                        mm.then_inc(s["sKP"].h, 1)
                        s["sKP"].n += 1
                t.wait_ge(s["sKVw"].h, 16 * (2 * HT + 4) * (grp + 1))
                for c in range(ST):
                    gq = 4 * NSUP + c
                    pq = pKV[gq % 2]
                    t.wait_ge(s["sKE"].h, ke0 + gq - 1)
                    mm = None
                    for ht in range(HT):
                        mm = t.matmul(pq[:],
                                      xnT[:, ht * S + c * 128: ht * S + (c + 1) * 128],
                                      wvT[:, ht * 512:(ht + 1) * 512],
                                      start=(ht == 0), stop=(ht == HT - 1))
                    mm.then_inc(s["sKP"].h, 1)
                    s["sKP"].n += 1
                for hl in range(4):
                    h = grp * 4 + hl
                    t.wait_ge(s["sKE"].h, ke0 + NPRJ)
                    t.wait_ge(s["sSkp"].h, skp0 + 16 * (hl + 1) * ST)
                    for c in range(ST):
                        k = hl * ST + c
                        if k >= 1:
                            pc, cc = (k - 1) // ST, (k - 1) % ST
                        if k >= 1:
                            t.wait_ge(s["sVb"].h, v_scan(pc, cc, 1))
                        t.matmul(pSC[:], skTg[:, hl * S + c * 128:hl * S + (c + 1) * 128],
                                 sqTb[:, hl * S + c * 128:hl * S + (c + 1) * 128],
                                 start=True, stop=True).then_inc(s["sPEb"].h, 1)
                        s["sPEb"].n += 1
                        if k >= 1:
                            t.wait_ge(s["sVb"].h, v_scan(pc, cc, 3))
                        mm = None
                        for m in range(NB):
                            mm = t.matmul(pFN[:, m * 128:(m + 1) * 128],
                                          sqTb[:, hl * S + c * 128:hl * S + (c + 1) * 128],
                                          F_bf[:, h * 512 + m * 128: h * 512 + (m + 1) * 128],
                                          start=True, stop=True)
                        mm.then_inc(s["sPEb"].h, 1)
                        s["sPEb"].n += 1
                        t.wait_ge(s["sVb"].h, v_scan(hl, c, 1))
                        if k >= 1:
                            t.wait_ge(s["sVb"].h, v_scan(pc, cc, 2))
                        pn = pNM[c % 2]
                        sqt = sqTb[:, hl * S + c * 128:hl * S + (c + 1) * 128]
                        if c == 0:
                            t.matmul(pn[:, 0:128], mst[:],
                                     vg[:, c * 512 + hl * 128:c * 512 + (hl + 1) * 128],
                                     start=True, stop=True)
                            mm = t.matmul(pDEN[:], mst[:], onc_bf[:],
                                          start=True, stop=True)
                        else:
                            t.matmul(pn[:, 0:128], mst[:],
                                     vg[:, c * 512 + hl * 128:c * 512 + (hl + 1) * 128],
                                     start=True, stop=False)
                            t.matmul(pDEN[:], mst[:], onc_bf[:], start=True, stop=False)
                            t.wait_ge(s["sSb"].h, sb_cnt(hl, c - 1))
                            t.matmul(pn[:, 0:128], sqt, S_bfs[:, 0:128],
                                     start=False, stop=True)
                            mm = t.matmul(pDEN[:], sqt, S_bfs[:, 128:129],
                                          start=False, stop=True)
                        mm.then_inc(s["sPEb"].h, 1)
                        s["sPEb"].n += 1
                        if c >= 1:
                            t.wait_ge(s["sVb"].h, v_scan(hl, c - 1, 4))
                        elif hl >= 1:
                            t.wait_ge(s["sVb"].h, v_scan(hl - 1, ST - 1, 4))
                        t.matmul(pDS[:, 0:128],
                                 skp[:, ((hl % 2) * ST + c) * 128:((hl % 2) * ST + c + 1) * 128],
                                 vg[:, c * 512 + hl * 128:c * 512 + (hl + 1) * 128],
                                 start=True, stop=True)
                        t.matmul(pDS[:, 128:129],
                                 skp[:, ((hl % 2) * ST + c) * 128:((hl % 2) * ST + c + 1) * 128],
                                 onc_bf[:], start=True, stop=True).then_inc(s["sPEb"].h, 1)
                        s["sPEb"].n += 1
                bdone(t)

            @blk.vector
            def _(v):
                bwait(v)
                for el in range(4):
                    for j in range(NSUP):
                        gq = el * NSUP + j
                        v.wait_ge(s["sKX"].h, kx0 + gq + 1)
                        v.scalar_tensor_tensor(
                            skTg[:, (el * S + j * 512):(el * S + (j + 1) * 512)],
                            ebuf2[gq % 2][:], 1.0, rbuf2[gq % 2][:], ALU.min, ALU.add
                        ).then_inc(s["sKE"].h, 1)
                        s["sKE"].n += 1
                for c in range(ST):
                    gq = 4 * NSUP + c
                    v.wait_ge(s["sKP"].h, kp0 + gq + 1)
                    v.tensor_copy(vg[:, c * 512:(c + 1) * 512], pKV[gq % 2][:]
                                  ).then_inc(s["sKE"].h, 1)
                    s["sKE"].n += 1
                for hl in range(4):
                    h = grp * 4 + hl
                    v.memset(S_sb[:], 0.0)
                    for c in range(ST):
                        k = hl * ST + c
                        base = h * ST * NB + c * NB
                        v.wait_ge(s["sPEb"].h, pe_scan(hl, c, 1))
                        if k >= 1:
                            pc, cc = (k - 1) // ST, (k - 1) % ST
                            v.wait_ge(s["sPEb"].h, pe_scan(pc, cc, 3))
                        v.tensor_tensor(mst[:], pSC[:], maskT[:], ALU.mult
                                        ).then_inc(s["sVb"].h, 1)
                        s["sVb"].n += 1
                        v.wait_ge(s["sPEb"].h, pe_scan(hl, c, 3))
                        v.tensor_scalar(dent[:], pDEN[:], EPS, None, ALU.max)
                        v.tensor_scalar(of[:], pFN[:, 0:128], rcw_all[:, base:base + 1],
                                        None, ALU.mult)
                        v.reciprocal(recl[:], dent[:])
                        v.scalar_tensor_tensor(of[:], pFN[:, 128:256],
                                               rcw_all[:, base + 1:base + 2], of[:],
                                               ALU.mult, ALU.add)
                        v.tensor_scalar(lo[:], pNM[c % 2][:, 0:128], recl[:, 0:1],
                                        None, ALU.mult).then_inc(s["sVb"].h, 1)
                        s["sVb"].n += 1
                        v.scalar_tensor_tensor(of[:], pFN[:, 256:384],
                                               rcw_all[:, base + 2:base + 3], of[:],
                                               ALU.mult, ALU.add)
                        v.scalar_tensor_tensor(of[:], pFN[:, 384:512],
                                               rcw_all[:, base + 3:base + 4], of[:],
                                               ALU.mult, ALU.add).then_inc(s["sVb"].h, 1)
                        s["sVb"].n += 1
                        v.wait_ge(s["sPEb"].h, pe_scan(hl, c, 4))
                        v.tensor_tensor(S_sb[:], pDS[:], S_sb[:], ALU.add
                                        ).then_inc(s["sVb"].h, 1)
                        s["sVb"].n += 1
                        v.wait_ge(s["sGb"].h, gb0 + k * G_C + 1)
                        v.scalar_tensor_tensor(memb[:], t1b[:], p_col[:, c:c + 1], ocf[:],
                                               ALU.mult, ALU.add).then_inc(s["sVb"].h, 1)
                        s["sVb"].n += 1
                        v.wait_ge(s["sGb"].h, gb0 + k * G_C + 2)
                        if c == 0:
                            if hl == 0:
                                if grp > 0:
                                    v.wait_ge(s["sYst"].h, yst0)
                            else:
                                v.wait_ge(s["sYst"].h, yst0 + 16 * hl)
                        v.scalar_tensor_tensor(ybuf[:, c * 128:(c + 1) * 128], t2b[:],
                                               g_bc[:, h:h + 1], lo[:], ALU.mult, ALU.add
                                               ).then_inc(s["sVb"].h, 1)
                        s["sVb"].n += 1
                bdone(v, drain=True)

            @blk.gpsimd
            def _(g):
                for hl in range(4):
                    for c in range(ST):
                        k = hl * ST + c
                        g.wait_ge(s["sOCld"].h, ocld0 + 16 * (k + 1))
                        g.wait_ge(s["sVb"].h, v_scan(hl, c, 3))
                        g.tensor_copy(ocf[:], ocl[:, (k % 4) * 128:((k % 4) + 1) * 128])
                        g.tensor_sub(t1b[:], of[:], ocf[:]).then_inc(s["sGb"].h, 1)
                        s["sGb"].n += 1
                        g.wait_ge(s["sVb"].h, v_scan(hl, c, 5))
                        g.wait_ge(s["sVb"].h, v_scan(hl, c, 2))
                        if k >= 1:
                            pc, cc = (k - 1) // ST, (k - 1) % ST
                            g.wait_ge(s["sVb"].h, v_scan(pc, cc, 6))
                        g.tensor_sub(t2b[:], memb[:], lo[:]).then_inc(s["sGb"].h, 1)
                        s["sGb"].n += 1
        ph.close()
        next_phase()
    attn.close()

    ph = ExitStack()
    woT = sb(ph, "woT", [128, HT * 2048], BF16, is_transpose_dest=True)
    yT = [sb(ph, f"yT{i}", [128, HT * 512], BF16, is_transpose_dest=True) for i in range(2)]
    xre = [sb(ph, f"xre{i}", [128, H], F32) for i in range(2)]
    x1t = [sb(ph, f"x1t{i}", [128, H], F32) for i in range(2)]
    xn2t = [sb(ph, f"xn2t{i}", [128, H], BF16) for i in range(2)]
    junk2 = sb(ph, "junk2", [128, H], BF16)
    ssq2 = [sb(ph, f"ssq2{i}", [128, 1], F32) for i in range(2)]
    s1b = [sb(ph, f"s1b{i}", [128, 1], F32) for i in range(2)]
    vtmp2 = sb(ph, "vtmp2", [128, 1], F32)
    mean2b = sb(ph, "mean2b", [128, 1], F32)
    varb = [sb(ph, f"varb{i}", [128, 1], F32) for i in range(2)]
    sqvb = [sb(ph, f"sqvb{i}", [128, 1], F32) for i in range(2)]
    rstdb = [sb(ph, f"rstdb{i}", [128, 1], F32) for i in range(2)]
    nbiasb = [sb(ph, f"nbiasb{i}", [128, 1], F32) for i in range(2)]
    pO = [psum(ph, f"pO{i}", [128, 2048]) for i in range(2)]

    with nc.Block("a3", no_gpsimd_drain=True) as blk:
        @blk.sync
        def _(sy):
            bwait(sy)
            sy.wait_ge(s["sWS"].h, 16 * N_PREP)
            sy.wait_ge(s["sYst"].h, 16 * NH)
            sy.wait_ge(s["sSKst"].h, s["sSKst"].n)
            for et in range(HT):
                for hob in range(4):
                    sy.dma_start(woT[:, et * 2048 + hob * 512: et * 2048 + (hob + 1) * 512],
                                 wo_bf[hob * 512:(hob + 1) * 512, et * 128:(et + 1) * 128],
                                 transpose=True).then_inc(s["sWo"].h, 16)
                    s["sWo"].n += 16
            sy.wait_ge(s["sYst"].h, 16 * NH)

            def load_yT(j):
                if j >= 2:
                    sy.wait_ge(s["sOP"].h, (j - 1) * 4)
                for et in range(HT):
                    sy.dma_start(yT[j % 2][:, et * 512:(et + 1) * 512],
                                 y_scr[j * 512:(j + 1) * 512, et * 128:(et + 1) * 128],
                                 transpose=True).then_inc(s["sYT"].h, 16)
                    s["sYT"].n += 16
            load_yT(0)
            if NSUP > 1:
                load_yT(1)
            for i in range(ST):
                sy.wait_ge(s["sRv"].h, i + 1)
                sy.wait_ge(s["sXN2v"].h, i + 1)
                sy.dma_start(x1_scr[i * 128:(i + 1) * 128, :], x1t[i % 2][:]
                             ).then_inc(s["sX1st"].h, 16)
                s["sX1st"].n += 16
                sy.dma_start(xn2_scr[i * 128:(i + 1) * 128, :], xn2t[i % 2][:]
                             ).then_inc(s["sXN2st"].h, 16)
                s["sXN2st"].n += 16
                if i % 4 == 3 and (i // 4) + 2 < NSUP:
                    load_yT((i // 4) + 2)
            bdone(sy)

        @blk.scalar
        def _(sc):
            bwait(sc)
            for i in range(min(2, ST)):
                sc.dma_start(xre[i % 2][:], x_in[i * 128:(i + 1) * 128, :]
                             ).then_inc(s["sXre"].h, 16)
                s["sXre"].n += 16
            for i in range(ST):
                sc.wait_ge(s["sRv"].h, i + 1)
                if i >= 2:
                    sc.wait_ge(s["sVAR2"].h, i - 1)
                sc.activation(junk2[:], x1t[i % 2][:], AF.Square,
                              accum_out=ssq2[i % 2][:])
                sc.drain()
                sc.sem_inc(s["sSQ2"].h, 1)
                s["sSQ2"].n += 1
                sc.wait_ge(s["sVAR2"].h, i + 1)
                sc.activation(sqvb[i % 2][:], varb[i % 2][:], AF.Sqrt
                              ).then_inc(s["sSQT2"].h, 1)
                s["sSQT2"].n += 1
                if i + 2 < ST:
                    sc.dma_start(xre[i % 2][:], x_in[(i + 2) * 128:(i + 3) * 128, :]
                                 ).then_inc(s["sXre"].h, 16)
                    s["sXre"].n += 16
            bdone(sc, drain=True)

        @blk.tensor
        def _(t):
            bwait(t)
            t.wait_ge(s["sWo"].h, 16 * HT * 4)
            for j in range(NSUP):
                t.wait_ge(s["sYT"].h, 16 * HT * (j + 1))
                for cc in range(4):
                    c = j * 4 + cc
                    if c >= 2:
                        t.wait_ge(s["sRv"].h, c - 1)
                    mm = None
                    for et in range(HT):
                        for hob in range(4):
                            mm = t.matmul(pO[c % 2][:, hob * 512:(hob + 1) * 512],
                                          yT[j % 2][:, et * 512 + cc * 128: et * 512 + (cc + 1) * 128],
                                          woT[:, et * 2048 + hob * 512: et * 2048 + (hob + 1) * 512],
                                          start=(et == 0), stop=(et == HT - 1))
                    mm.then_inc(s["sOP"].h, 1)
                    s["sOP"].n += 1
            bdone(t)

        @blk.vector
        def _(v):
            bwait(v)
            for i in range(ST + 1):
                if i < ST:
                    v.wait_ge(s["sOP"].h, i + 1)
                    v.wait_ge(s["sXre"].h, 16 * (i + 1))
                    if i >= 2:
                        v.wait_ge(s["sX1st"].h, 16 * (i - 1))
                        v.wait_ge(s["sSQ2"].h, i - 1)
                        v.wait_ge(s["sXN2v"].h, i - 1)
                    v.tensor_tensor(x1t[i % 2][:], pO[i % 2][:], xre[i % 2][:], ALU.add
                                    ).then_inc(s["sRv"].h, 1)
                    s["sRv"].n += 1
                    v.reduce_sum(s1b[i % 2][:], x1t[i % 2][:], AX.X)
                if i >= 1:
                    j = i - 1
                    if i == ST:
                        v.drain()
                    v.scalar_tensor_tensor(nbiasb[j % 2][:], s1b[j % 2][:], -1.0 / H,
                                           rstdb[j % 2][:], ALU.mult, ALU.mult)
                if i < ST:
                    v.wait_ge(s["sSQ2"].h, i + 1)
                    v.tensor_scalar(vtmp2[:], ssq2[i % 2][:], 1.0 / H, LN_EPS,
                                    ALU.mult, ALU.add)
                    v.scalar_tensor_tensor(mean2b[:], s1b[i % 2][:], 1.0 / (H * float(H)),
                                           s1b[i % 2][:], ALU.mult, ALU.mult)
                if i >= 1:
                    j = i - 1
                    if j >= 2:
                        v.wait_ge(s["sXN2st"].h, 16 * (j - 1))
                    if i == ST:
                        v.drain()
                    v.tensor_scalar(xn2t[j % 2][:], x1t[j % 2][:], rstdb[j % 2][:, 0:1],
                                    nbiasb[j % 2][:, 0:1], ALU.mult, ALU.add
                                    ).then_inc(s["sXN2v"].h, 1)
                    s["sXN2v"].n += 1
                if i < ST:
                    if i == 0:
                        v.drain()
                    v.tensor_sub(varb[i % 2][:], vtmp2[:], mean2b[:]).then_inc(s["sVAR2"].h, 1)
                    s["sVAR2"].n += 1
                    v.wait_ge(s["sSQT2"].h, i + 1)
                    v.reciprocal(rstdb[i % 2][:], sqvb[i % 2][:])
            bdone(v, drain=True)
    ph.close()
    next_phase()

    ph = ExitStack()
    xn2T = sb(ph, "xn2T", [128, HT * 512], BF16, is_transpose_dest=True)
    hact = sb(ph, "hact", [128, IT * 512], BF16)
    wgT = [sb(ph, f"wgT{i}", [128, HT * 512], BF16, is_transpose_dest=True) for i in range(2)]
    wuT = [sb(ph, f"wuT{i}", [128, HT * 512], BF16, is_transpose_dest=True) for i in range(2)]
    wdT = [sb(ph, f"wdT{i}", [128, 512], BF16, is_transpose_dest=True) for i in range(4)]
    gs_t = [sb(ph, f"gs_t{i}", [128, 512], F32) for i in range(2)]
    x1l = sb(ph, "x1l", [128, 4 * 2048], F32)
    rr = sb(ph, "rr", [128, 4], F32)
    outb = [sb(ph, f"outb{i}", [128, 512], F32) for i in range(2)]
    pGf = psum(ph, "pGf", [128, 512])
    pUf = psum(ph, "pUf", [128, 512])
    pDN = [psum(ph, f"pDN{i}", [128, 512]) for i in range(4)]
    pR2 = psum(ph, "pR2", [128, 4])

    with nc.Block("ffn", no_gpsimd_drain=True) as blk:
        @blk.sync
        def _(sy):
            bwait(sy)
            sy.wait_ge(s["sWS"].h, 16 * N_PREP)
            sy.wait_ge(s["sX1st"].h, 16 * ST)
            sy.wait_ge(s["sXN2st"].h, 16 * ST)
            for sbk in range(NFS):
                if sbk >= 1:
                    sy.wait_ge(s["sGU"].h, sbk * 2 * IT)
                for ht in range(HT):
                    sy.dma_start(xn2T[:, ht * 512:(ht + 1) * 512],
                                 xn2_scr[sbk * 512:(sbk + 1) * 512, ht * 128:(ht + 1) * 128],
                                 transpose=True).then_inc(s["sX2T"].h, 16)
                    s["sX2T"].n += 16
                for isup in range(16):
                    gi = sbk * 16 + isup
                    if gi >= 2:
                        sy.wait_ge(s["sGU"].h, (gi - 1) * 8)
                    for ht in range(HT):
                        sy.dma_start(wgT[gi % 2][:, ht * 512:(ht + 1) * 512],
                                     wg_bf[isup * 512:(isup + 1) * 512, ht * 128:(ht + 1) * 128],
                                     transpose=True).then_inc(s["sFW"].h, 16)
                        s["sFW"].n += 16
                    for ht in range(HT):
                        sy.dma_start(wuT[gi % 2][:, ht * 512:(ht + 1) * 512],
                                     wu_bf[isup * 512:(isup + 1) * 512, ht * 128:(ht + 1) * 128],
                                     transpose=True).then_inc(s["sFW"].h, 16)
                        s["sFW"].n += 16
                for hob in range(4):
                    for it in range(IT):
                        k = (sbk * 4 + hob) * IT + it
                        if k >= 4:
                            sy.wait_ge(s["sDP"].h, k - 3)
                        sy.dma_start(wdT[it % 4][:],
                                     wd_bf[hob * 512:(hob + 1) * 512, it * 128:(it + 1) * 128],
                                     transpose=True).then_inc(s["sDW"].h, 16)
                        s["sDW"].n += 16
            bdone(sy)

        @blk.scalar
        def _(sc):
            bwait(sc)
            for sbk in range(NFS):
                sc.wait_ge(s["sX1st"].h, 16 * 4 * (sbk + 1))
                if sbk >= 1:
                    sc.wait_ge(s["sDE"].h, sbk * 16)
                for ss in range(4):
                    sc.dma_start(x1l[:, ss * 2048:(ss + 1) * 2048],
                                 x1_scr[(sbk * 4 + ss) * 128:(sbk * 4 + ss + 1) * 128, :]
                                 ).then_inc(s["sXR1"].h, 16)
                    s["sXR1"].n += 16
                for it in range(IT):
                    gi = sbk * IT + it
                    sc.wait_ge(s["sGU"].h, gi * 2 + 1)
                    if gi >= 2:
                        sc.wait_ge(s["sHA"].h, gi - 1)
                    sc.activation(gs_t[it % 2][:], pGf[:], AF.Silu,
                                  scale=sg_col[:, it:it + 1]).then_inc(s["sSil"].h, 1)
                    s["sSil"].n += 1
                for hob in range(4):
                    for ss in range(4):
                        k = hob * 4 + ss
                        sc.wait_ge(s["sDE"].h, sbk * 16 + k + 1)
                        sc.dma_start(out[(sbk * 4 + ss) * 128:(sbk * 4 + ss + 1) * 128,
                                         hob * 512:(hob + 1) * 512],
                                     outb[k % 2][:]).then_inc(s["sOut"].h, 16)
                        s["sOut"].n += 16
            sc.wait_ge(s["sOut"].h, s["sOut"].n)
            bdone(sc, drain=True)

        @blk.tensor
        def _(t):
            bwait(t)
            for sbk in range(NFS):
                t.wait_ge(s["sX2T"].h, 16 * HT * (sbk + 1))
                for it in range(IT):
                    gi = sbk * IT + it
                    isup, itl = it // 4, it % 4
                    t.wait_ge(s["sFW"].h, 16 * 2 * HT * (sbk * 16 + isup + 1))
                    if gi >= 1:
                        t.wait_ge(s["sSil"].h, gi)
                    mm = None
                    for ht in range(HT):
                        mm = t.matmul(pGf[:],
                                      wgT[(sbk * 16 + isup) % 2][:, ht * 512 + itl * 128: ht * 512 + (itl + 1) * 128],
                                      xn2T[:, ht * 512:(ht + 1) * 512],
                                      start=(ht == 0), stop=(ht == HT - 1))
                    mm.then_inc(s["sGU"].h, 1)
                    s["sGU"].n += 1
                    if gi >= 1:
                        t.wait_ge(s["sHA"].h, gi)
                    mm = None
                    for ht in range(HT):
                        mm = t.matmul(pUf[:],
                                      wuT[(sbk * 16 + isup) % 2][:, ht * 512 + itl * 128: ht * 512 + (itl + 1) * 128],
                                      xn2T[:, ht * 512:(ht + 1) * 512],
                                      start=(ht == 0), stop=(ht == HT - 1))
                    mm.then_inc(s["sGU"].h, 1)
                    s["sGU"].n += 1
                t.wait_ge(s["sHA"].h, (sbk + 1) * IT)
                for ss in range(4):
                    mm = None
                    for it in range(IT):
                        mm = t.matmul(pR2[:, ss:ss + 1],
                                      hact[:, it * 512 + ss * 128: it * 512 + (ss + 1) * 128],
                                      onc_bf[:], start=(it == 0), stop=(it == IT - 1))
                    mm.then_inc(s["sR2"].h, 1)
                    s["sR2"].n += 1
                for hob in range(4):
                    for it in range(IT):
                        k = (sbk * 4 + hob) * IT + it
                        t.wait_ge(s["sDW"].h, 16 * (k + 1))
                        if hob >= 1 and it == 0:
                            t.wait_ge(s["sDE"].h, sbk * 16 + hob * 4)
                        mm = None
                        for ss in range(4):
                            mm = t.matmul(pDN[ss][:],
                                          hact[:, it * 512 + ss * 128: it * 512 + (ss + 1) * 128],
                                          wdT[it % 4][:],
                                          start=(it == 0), stop=(it == IT - 1))
                        mm.then_inc(s["sDP"].h, 1)
                        s["sDP"].n += 1
                if sbk < NFS - 1:
                    t.wait_ge(s["sDE"].h, (sbk + 1) * 16)
            bdone(t)

        @blk.vector
        def _(v):
            bwait(v)
            for sbk in range(NFS):
                for it in range(IT):
                    gi = sbk * IT + it
                    v.wait_ge(s["sSil"].h, gi + 1)
                    v.wait_ge(s["sGU"].h, gi * 2 + 2)
                    v.scalar_tensor_tensor(hact[:, it * 512:(it + 1) * 512],
                                           pUf[:], su_col[:, it:it + 1],
                                           gs_t[it % 2][:], ALU.mult, ALU.mult
                                           ).then_inc(s["sHA"].h, 1)
                    s["sHA"].n += 1
                v.wait_ge(s["sR2"].h, (sbk + 1) * 4)
                v.tensor_scalar(rr[:], pR2[:], 128.0, None, ALU.mult)
                v.drain()
                for hob in range(4):
                    v.wait_ge(s["sDP"].h, (sbk * 4 + hob + 1) * IT)
                    v.wait_ge(s["sXR1"].h, 16 * 4 * (sbk + 1))
                    for ss in range(4):
                        k = hob * 4 + ss
                        if sbk * 16 + k >= 2:
                            v.wait_ge(s["sOut"].h, 16 * (sbk * 16 + k - 1))
                        v.scalar_tensor_tensor(outb[k % 2][:], pDN[ss][:], rr[:, ss:ss + 1],
                                               sd_bc[:, hob * 512:(hob + 1) * 512],
                                               ALU.subtract, ALU.mult)
                        v.tensor_tensor(outb[k % 2][:], outb[k % 2][:],
                                        x1l[:, ss * 2048 + hob * 512: ss * 2048 + (hob + 1) * 512],
                                        ALU.add).then_inc(s["sDE"].h, 1)
                        s["sDE"].n += 1
            bdone(v, drain=True)
    ph.close()
    top.close()
    return nc


_NC_CACHE = {}


def get_nc(S=2048):
    if S not in _NC_CACHE:
        _NC_CACHE[S] = build_nc(S)
    return _NC_CACHE[S]


def kernel(**inputs):
    x = np.ascontiguousarray(inputs["L0_attn_hidden_states"])
    nc = get_nc(2048)
    common = {
        "wq": np.ascontiguousarray(inputs["L0_attn_w_q"]),
        "wk": np.ascontiguousarray(inputs["L0_attn_w_k"]),
        "wv": np.ascontiguousarray(inputs["L0_attn_w_v"]),
        "wo": np.ascontiguousarray(inputs["L0_attn_w_o"]),
        "egw1": np.ascontiguousarray(inputs["L0_attn_eg_w1"]),
        "egb1": np.ascontiguousarray(inputs["L0_attn_eg_b1"]),
        "egw2": np.ascontiguousarray(inputs["L0_attn_eg_w2"]),
        "egb2": np.ascontiguousarray(inputs["L0_attn_eg_b2"]),
        "mgate": np.ascontiguousarray(inputs["L0_attn_memory_gate"]),
        "fmem": np.ascontiguousarray(inputs["L0_attn_fine_memories"]),
        "fnorm": np.ascontiguousarray(inputs["L0_attn_fine_memory_norms"]),
        "gw": np.ascontiguousarray(inputs["L0_ffn_gate_w"]),
        "gs": np.ascontiguousarray(inputs["L0_ffn_gate_scale"]),
        "uw": np.ascontiguousarray(inputs["L0_ffn_up_w"]),
        "us": np.ascontiguousarray(inputs["L0_ffn_up_scale"]),
        "dw": np.ascontiguousarray(inputs["L0_ffn_down_w"]),
        "ds": np.ascontiguousarray(inputs["L0_ffn_down_scale"]),
    }
    in_maps = []
    for b in range(NCORES):
        m = dict(common)
        m["x"] = np.ascontiguousarray(x[b])
        in_maps.append(m)
    res = run_bass_kernel_spmd(nc, in_maps, core_ids=list(range(NCORES)))
    return np.stack([res.results[b]["out"] for b in range(NCORES)], axis=0)


# version 08
# speedup vs baseline: 1.0608x; 1.0586x over previous
from contextlib import ExitStack

import numpy as np
import ml_dtypes
import concourse.bass as bass
import concourse.mybir as mybir
from concourse.bass_utils import run_bass_kernel_spmd

F32 = mybir.dt.float32
BF16 = mybir.dt.bfloat16
I32 = mybir.dt.int32
AF = mybir.ActivationFunctionType
ALU = mybir.AluOpType
AX = mybir.AxisListType

H = 2048
NH = 16
HD = 128
NB = 4
HT = H // 128
II = 8192
IT = II // 128
EPS = 1e-6
LN_EPS = 1e-5
NCORES = 8


class Sem:
    def __init__(self, nc, name):
        self.h = nc.alloc_semaphore(name=name)
        self.n = 0


def build_nc(S=2048, dbg=False):
    ST = S // 128
    NSUP = S // 512
    NFS = S // 512

    nc = bass.Bass("TRN2", target_bir_lowering=False, debug=False,
                   detect_race_conditions=False)

    x_in = nc.dram_tensor("x", [S, H], F32, kind="ExternalInput").ap()
    wq_in = nc.dram_tensor("wq", [H, H], F32, kind="ExternalInput").ap()
    wk_in = nc.dram_tensor("wk", [H, H], F32, kind="ExternalInput").ap()
    wv_in = nc.dram_tensor("wv", [H, H], F32, kind="ExternalInput").ap()
    wo_in = nc.dram_tensor("wo", [H, H], F32, kind="ExternalInput").ap()
    egw1_in = nc.dram_tensor("egw1", [512, H], F32, kind="ExternalInput").ap()
    egb1_in = nc.dram_tensor("egb1", [512], F32, kind="ExternalInput").ap()
    egw2_in = nc.dram_tensor("egw2", [1, 512], F32, kind="ExternalInput").ap()
    egb2_in = nc.dram_tensor("egb2", [1], F32, kind="ExternalInput").ap()
    mg_in = nc.dram_tensor("mgate", [NH], F32, kind="ExternalInput").ap()
    fm_in = nc.dram_tensor("fmem", [NB, NH, HD, HD], F32, kind="ExternalInput").ap()
    fn_in = nc.dram_tensor("fnorm", [NB, NH, HD], F32, kind="ExternalInput").ap()
    wg_in = nc.dram_tensor("gw", [II, H], I32, kind="ExternalInput").ap()
    sg_in = nc.dram_tensor("gs", [II], F32, kind="ExternalInput").ap()
    wu_in = nc.dram_tensor("uw", [II, H], I32, kind="ExternalInput").ap()
    su_in = nc.dram_tensor("us", [II], F32, kind="ExternalInput").ap()
    wd_in = nc.dram_tensor("dw", [H, II], I32, kind="ExternalInput").ap()
    sd_in = nc.dram_tensor("ds", [H], F32, kind="ExternalInput").ap()
    out = nc.dram_tensor("out", [S, H], F32, kind="ExternalOutput").ap()

    wq_bf = nc.dram_tensor("wq_bf", [H, H], BF16, kind="Internal").ap()
    wk_bf = nc.dram_tensor("wk_bf", [H, H], BF16, kind="Internal").ap()
    wv_bf = nc.dram_tensor("wv_bf", [H, H], BF16, kind="Internal").ap()
    wo_bf = nc.dram_tensor("wo_bf", [H, H], BF16, kind="Internal").ap()
    egw1_bf = nc.dram_tensor("egw1_bf", [512, H], BF16, kind="Internal").ap()
    wg_bf = nc.dram_tensor("wg_bf", [II, H], BF16, kind="Internal").ap()
    wu_bf = nc.dram_tensor("wu_bf", [II, H], BF16, kind="Internal").ap()
    wd_bf = nc.dram_tensor("wd_bf", [H, II], BF16, kind="Internal").ap()
    kscr = "ExternalOutput" if dbg else "Internal"
    xn_scr = nc.dram_tensor("xn_scr", [S, H], BF16, kind=kscr).ap()
    sq_scr = nc.dram_tensor("sq_scr", [H, S], BF16, kind=kscr).ap()
    skT_scr = nc.dram_tensor("skT_scr", [H, S], BF16, kind=kscr).ap()
    oc_scr = nc.dram_tensor("oc_scr", [S, H], BF16, kind=kscr).ap()
    y_scr = nc.dram_tensor("y_scr", [S, H], BF16, kind=kscr).ap()
    x1_scr = nc.dram_tensor("x1_scr", [S, H], F32, kind=kscr).ap()
    xn2_scr = nc.dram_tensor("xn2_scr", [S, H], BF16, kind=kscr).ap()

    ident_c = nc.inline_tensor(np.eye(128, dtype=np.float32), "ident_c").ap()
    maskT_c = nc.inline_tensor(np.triu(np.ones((128, 128), dtype=np.float32)), "maskT_c").ap()
    onc_bf_c = nc.inline_tensor(np.ones((128, 1), dtype=ml_dtypes.bfloat16), "onc_bf_c").ap()
    onc_f_c = nc.inline_tensor(np.ones((128, 1), dtype=np.float32), "onc_f_c").ap()
    onr_f_c = nc.inline_tensor(np.ones((1, 128), dtype=np.float32), "onr_f_c").ap()

    sems = {}
    for nm in ["bar",
               "sXL", "sSQ", "sVAR", "sSQT", "sXN", "sXNst",
               "sSm", "sFM", "sFMV", "sPE0", "sV0", "sS0",
               "sWL", "sWS",
               "sXnT", "sWq", "sQP", "sQX", "sQE", "sCN", "sCV", "sRe", "sSx",
               "sOCst", "sSQstr",
               "sGW", "sCrT", "sGPE", "sGS",
               "sKVw", "sKP", "sKX", "sKE", "sSKst", "sSkp", "sOCld",
               "sPEb", "sVb", "sSb", "sGb", "sYst",
               "sWo", "sYT", "sOP", "sXre", "sRv", "sXN2v", "sSQ2", "sVAR2", "sSQT2",
               "sX1st", "sXN2st",
               "sX2T", "sFW", "sGU", "sSil", "sHA", "sR2", "sDW", "sDP", "sDE",
               "sXR1", "sOut"]:
        sems[nm] = Sem(nc, nm)
    s = sems
    BAR_ENG = 4
    cur_phase = [0]

    def bwait(eng):
        eng.wait_ge(s["bar"].h, BAR_ENG * cur_phase[0])

    def bdone(eng, drain=False):
        if drain:
            eng.drain()
        eng.sem_inc(s["bar"].h, 1)

    def next_phase():
        cur_phase[0] += 1

    top = ExitStack()

    def sb(stack, name, shape, dtype, **kw):
        return stack.enter_context(nc.sbuf_tensor(name, shape, dtype, **kw))

    def psum(stack, name, shape, dtype=F32):
        return stack.enter_context(nc.psum_tensor(name, shape, dtype))

    identf = sb(top, "identf", [128, 128], F32)
    maskT = sb(top, "maskTs", [128, 128], F32)
    onc_bf = sb(top, "onc_bfs", [128, 1], BF16)
    onc_f = sb(top, "onc_fs", [128, 1], F32)
    onr_f = sb(top, "onr_fs", [1, 128], F32)
    sg_col = sb(top, "sg_col", [128, 64], F32)
    su_col = sb(top, "su_col", [128, 64], F32)
    sd_bc = sb(top, "sd_bc", [128, H], F32)
    b2_bc = sb(top, "b2_bc", [128, 1], F32)
    p_col = sb(top, "p_col", [128, ST], F32)

    RING = 6

    def emit_prep(g, lo, hi):
        for ci in range(lo, hi):
            src_, dst_ = prep_chunks[ci]
            buf = wstg[ci % RING][:].rearrange("p (a h) -> p a h", a=src_.shape[1]) \
                if len(src_.shape) == 3 else wstg[ci % RING][:]
            if ci >= RING:
                g.wait_ge(s["sWS"].h, 16 * (ci - RING + 1))
            g.dma_start(buf, src_).then_inc(s["sWL"].h, 16)
            s["sWL"].n += 16
            g.wait_ge(s["sWL"].h, 16 * (ci + 1))
            g.dma_start(dst_, buf).then_inc(s["sWS"].h, 16)
            s["sWS"].n += 16

    attn = ExitStack()
    xnT = sb(attn, "xnT", [128, HT * S], BF16, is_transpose_dest=True)
    F_bf = sb(attn, "F_bf", [128, NH * NB * 128], BF16)
    Mp_bf = sb(attn, "Mp_bf", [128, NH * 129], BF16)
    norms_f = sb(attn, "norms_f", [128, 64], F32)
    norms_hm = sb(attn, "norms_hm", [128, 64], BF16)
    z_sb = sb(attn, "z_sb", [128, NH], F32)
    g_bc = sb(attn, "g_bc", [128, NH], F32)
    invg_bc = sb(attn, "invg_bc", [128, NH], F32)
    indz_bc = sb(attn, "indz_bc", [128, 1], F32)
    indb_row = sb(attn, "indb_row", [1, NB], F32)
    bsums = sb(attn, "bsums", [1, NB], F32)
    g_row = sb(attn, "g_row", [1, NH], F32)
    b1_col = sb(attn, "b1_col", [128, 4], F32)
    egw2_bf = sb(attn, "egw2_bf", [128, 4], BF16)
    rcw_all = sb(attn, "rcw_all", [128, NH * ST * NB], F32)

    SMALL_LOADS = 13

    ph = ExitStack()
    xbuf = [sb(ph, f"xb{i}", [128, H], F32) for i in range(2)]
    xnb = [sb(ph, f"xnb{i}", [128, H], BF16) for i in range(2)]
    junk = sb(ph, "junk", [128, H], BF16)
    ssq = [sb(ph, f"ssq{i}", [128, 1], F32) for i in range(2)]
    s1 = [sb(ph, f"s1_{i}", [128, 1], F32) for i in range(2)]
    vtmp = sb(ph, "vtmp", [128, 1], F32)
    mean2 = sb(ph, "mean2", [128, 1], F32)
    var = [sb(ph, f"var{i}", [128, 1], F32) for i in range(2)]
    sqv = [sb(ph, f"sqv{i}", [128, 1], F32) for i in range(2)]
    rstd = [sb(ph, f"rstd{i}", [128, 1], F32) for i in range(2)]
    nbias = [sb(ph, f"nbias{i}", [128, 1], F32) for i in range(2)]
    wstg = [sb(ph, f"wstg{i}", [128, 4096], BF16) for i in range(RING)]
    nrm_raw = sb(ph, "nrm_raw", [64, 128], F32)
    sgr = sb(ph, "sgr", [64, 128], F32)
    sur = sb(ph, "sur", [64, 128], F32)
    b1r = sb(ph, "b1r", [4, 128], F32)
    w2r = sb(ph, "w2r", [4, 128], F32)
    mg_row = sb(ph, "mg_row", [1, NH], F32)
    invg_row = sb(ph, "invg_row", [1, NH], F32)
    b2_row = sb(ph, "b2_row", [1, 1], F32)
    sd_row = sb(ph, "sd_row", [1, H], F32)
    indz_f = sb(ph, "indz_f", [1, 1], F32)
    zt = sb(ph, "zt", [1, 1], F32)
    fstage = [sb(ph, f"fstage{i}", [128, 512], F32) for i in range(2)]
    mstage = sb(ph, "mstage", [128, 128], F32)
    egw2f = sb(ph, "egw2f", [128, 4], F32)
    pA = psum(ph, "pA", [128, 64])
    pB = psum(ph, "pB", [128, 512])
    pC = psum(ph, "pC", [128, 16])
    pD = psum(ph, "pD", [1, 16])

    prep_chunks = []

    def rows2(w_in, w_bf, r):
        return (w_in[r * 256:(r + 1) * 256, :].rearrange("(a p) h -> p a h", p=128),
                w_bf[r * 256:(r + 1) * 256, :].rearrange("(a p) h -> p a h", p=128))
    for r in range(8):
        prep_chunks.append(rows2(wq_in, wq_bf, r))
    for r in range(8):
        prep_chunks.append(rows2(wk_in, wk_bf, r))
    for r in range(8):
        prep_chunks.append(rows2(wv_in, wv_bf, r))
    for r in range(8):
        prep_chunks.append(rows2(wo_in, wo_bf, r))
    for r in range(2):
        prep_chunks.append(rows2(egw1_in, egw1_bf, r))
    for r in range(32):
        prep_chunks.append(rows2(wg_in, wg_bf, r))
    for r in range(32):
        prep_chunks.append(rows2(wu_in, wu_bf, r))
    for r in range(16):
        for c in range(2):
            prep_chunks.append((wd_in[r * 128:(r + 1) * 128, c * 4096:(c + 1) * 4096],
                                wd_bf[r * 128:(r + 1) * 128, c * 4096:(c + 1) * 4096]))
    N_PREP = len(prep_chunks)
    PREP_WQ, PREP_WK, PREP_WV, PREP_WO, PREP_EG = 8, 16, 24, 32, 34

    with nc.Block("p0", no_gpsimd_drain=True) as blk:
        @blk.gpsimd
        def _(g):
            emit_prep(g, 0, N_PREP)

        @blk.sync
        def _(sy):
            for i in range(ST):
                if i >= 2:
                    sy.wait_ge(s["sXN"].h, i - 1)
                    sy.dma_start(xn_scr[(i - 2) * 128:(i - 1) * 128, :],
                                 xnb[i % 2][:]).then_inc(s["sXNst"].h, 16)
                    s["sXNst"].n += 16
                sy.dma_start(xbuf[i % 2][:], x_in[i * 128:(i + 1) * 128, :]
                             ).then_inc(s["sXL"].h, 16)
                s["sXL"].n += 16
            for i in range(ST - 2, ST):
                sy.wait_ge(s["sXN"].h, i + 1)
                sy.dma_start(xn_scr[i * 128:(i + 1) * 128, :],
                             xnb[i % 2][:]).then_inc(s["sXNst"].h, 16)
                s["sXNst"].n += 16
            bdone(sy)

        @blk.scalar
        def _(sc):
            small = [(ident_c, identf[:]), (maskT_c, maskT[:]),
                     (onc_bf_c, onc_bf[:]), (onc_f_c, onc_f[:]), (onr_f_c, onr_f[:]),
                     (fn_in.rearrange("m h d -> (m h) d"), nrm_raw[:]),
                     (sg_in.rearrange("(a b) -> a b", b=128), sgr[:]),
                     (su_in.rearrange("(a b) -> a b", b=128), sur[:]),
                     (egb1_in.rearrange("(a b) -> a b", b=128), b1r[:]),
                     (egw2_in.rearrange("a (b c) -> (a b) c", c=128), w2r[:]),
                     (mg_in.rearrange("(a b) -> a b", a=1), mg_row[:]),
                     (egb2_in.rearrange("(a b) -> a b", a=1), b2_row[:]),
                     (sd_in.rearrange("(a b) -> a b", a=1), sd_row[:])]
            assert len(small) == SMALL_LOADS
            for src, dst in small:
                sc.dma_start(dst, src).then_inc(s["sSm"].h, 16)
                s["sSm"].n += 16
            sc.wait_ge(s["sSm"].h, 16 * SMALL_LOADS)
            sc.activation(g_row[:], mg_row[:], AF.Sigmoid).then_inc(s["sS0"].h, 1)
            s["sS0"].n += 1
            for h in range(NH):
                if h >= 2:
                    sc.wait_ge(s["sFMV"].h, h - 1)
                for m in range(NB):
                    sc.dma_start(fstage[h % 2][:, m * 128:(m + 1) * 128],
                                 fm_in[m, h]).then_inc(s["sFM"].h, 16)
                    s["sFM"].n += 16
            for i in range(ST):
                sc.wait_ge(s["sXL"].h, 16 * (i + 1))
                if i >= 2:
                    sc.wait_ge(s["sVAR"].h, i - 1)
                sc.activation(junk[:], xbuf[i % 2][:], AF.Square,
                              accum_out=ssq[i % 2][:])
                sc.drain()
                sc.sem_inc(s["sSQ"].h, 1)
                s["sSQ"].n += 1
                sc.wait_ge(s["sVAR"].h, i + 1)
                sc.activation(sqv[i % 2][:], var[i % 2][:], AF.Sqrt
                              ).then_inc(s["sSQT"].h, 1)
                s["sSQT"].n += 1
            bdone(sc, drain=True)

        @blk.tensor
        def _(t):
            t.wait_ge(s["sSm"].h, 16 * SMALL_LOADS)
            t.transpose(pA[:], nrm_raw[:], identf[0:64, 0:64]).then_inc(s["sPE0"].h, 1)
            t.wait_ge(s["sV0"].h, 1)
            t.transpose(pA[:], sgr[:], identf[0:64, 0:64]).then_inc(s["sPE0"].h, 1)
            t.wait_ge(s["sV0"].h, 2)
            t.transpose(pA[:], sur[:], identf[0:64, 0:64]).then_inc(s["sPE0"].h, 1)
            t.wait_ge(s["sV0"].h, 3)
            t.transpose(pA[0:128, 0:4], b1r[:], identf[0:4, 0:4]).then_inc(s["sPE0"].h, 1)
            t.wait_ge(s["sV0"].h, 4)
            t.transpose(pA[0:128, 0:4], w2r[:], identf[0:4, 0:4]).then_inc(s["sPE0"].h, 1)
            for m in range(NB):
                t.wait_ge(s["sV0"].h, 5 + m)
                t.matmul(pD[:], onc_f[:], norms_f[:, m * 16:(m + 1) * 16],
                         start=True, stop=True).then_inc(s["sPE0"].h, 1)
            t.wait_ge(s["sV0"].h, 10)
            t.matmul(pC[:], onr_f[:], g_row[:], start=True, stop=True
                     ).then_inc(s["sPE0"].h, 1)
            t.wait_ge(s["sV0"].h, 11)
            t.matmul(pC[:], onr_f[:], invg_row[:], start=True, stop=True
                     ).then_inc(s["sPE0"].h, 1)
            t.wait_ge(s["sV0"].h, 12)
            t.matmul(pC[0:128, 0:1], onr_f[:], indz_f[:], start=True, stop=True
                     ).then_inc(s["sPE0"].h, 1)
            t.wait_ge(s["sV0"].h, 13)
            t.matmul(pC[0:128, 0:1], onr_f[:], b2_row[:], start=True, stop=True
                     ).then_inc(s["sPE0"].h, 1)
            for b in range(4):
                t.wait_ge(s["sV0"].h, 14 + b)
                t.matmul(pB[:], onr_f[:], sd_row[0:1, b * 512:(b + 1) * 512],
                         start=True, stop=True).then_inc(s["sPE0"].h, 1)
            s["sPE0"].n = 17
            bdone(t)

        @blk.vector
        def _(v):
            v.wait_ge(s["sPE0"].h, 1)
            v.tensor_copy(norms_f[:], pA[:])
            v.tensor_copy(norms_hm.rearrange("p (hh m) -> p hh m", m=NB),
                          pA.rearrange("p (m hh) -> p hh m", hh=NH))
            v.tensor_add(z_sb[:], norms_f[:, 0:16], norms_f[:, 16:32])
            v.drain()
            v.tensor_add(z_sb[:], z_sb[:], norms_f[:, 32:48])
            v.drain()
            v.tensor_add(z_sb[:], z_sb[:], norms_f[:, 48:64]).then_inc(s["sV0"].h, 1)
            v.wait_ge(s["sPE0"].h, 2)
            v.tensor_copy(sg_col[:], pA[:]).then_inc(s["sV0"].h, 1)
            v.wait_ge(s["sPE0"].h, 3)
            v.tensor_copy(su_col[:], pA[:]).then_inc(s["sV0"].h, 1)
            v.wait_ge(s["sPE0"].h, 4)
            v.tensor_copy(b1_col[:], pA[0:128, 0:4]).then_inc(s["sV0"].h, 1)
            v.wait_ge(s["sPE0"].h, 5)
            v.tensor_copy(egw2f[:], pA[0:128, 0:4])
            v.drain()
            v.tensor_copy(egw2_bf[:], egw2f[:]).then_inc(s["sV0"].h, 1)
            for m in range(NB):
                v.wait_ge(s["sPE0"].h, 6 + m)
                v.reduce_sum(bsums[0:1, m:m + 1], pD[:], AX.X).then_inc(s["sV0"].h, 1)
            v.drain()
            v.tensor_scalar(indb_row[:], bsums[:], EPS, None, ALU.is_ge)
            v.reduce_sum(zt[:], bsums[:], AX.X)
            v.drain()
            v.tensor_scalar(indz_f[:], zt[:], EPS, None, ALU.is_ge)
            v.drain()
            v.wait_ge(s["sS0"].h, 1)
            v.tensor_scalar(invg_row[:], g_row[:], -1.0, 1.0, ALU.mult, ALU.add
                            ).then_inc(s["sV0"].h, 1)
            v.wait_ge(s["sPE0"].h, 10)
            v.tensor_copy(g_bc[:], pC[:]).then_inc(s["sV0"].h, 1)
            v.wait_ge(s["sPE0"].h, 11)
            v.tensor_copy(invg_bc[:], pC[:]).then_inc(s["sV0"].h, 1)
            v.wait_ge(s["sPE0"].h, 12)
            v.tensor_copy(indz_bc[:], pC[0:128, 0:1]).then_inc(s["sV0"].h, 1)
            v.wait_ge(s["sPE0"].h, 13)
            v.tensor_copy(b2_bc[:], pC[0:128, 0:1]).then_inc(s["sV0"].h, 1)
            for b in range(4):
                v.wait_ge(s["sPE0"].h, 14 + b)
                v.tensor_copy(sd_bc[:, b * 512:(b + 1) * 512], pB[:]).then_inc(s["sV0"].h, 1)
            s["sV0"].n = 18
            for h in range(NH):
                v.wait_ge(s["sFM"].h, 16 * 4 * (h + 1))
                fs = fstage[h % 2]
                v.tensor_add(mstage[:], fs[:, 0:128], fs[:, 128:256])
                v.tensor_add(mstage[:], mstage[:], fs[:, 256:384])
                v.tensor_add(mstage[:], mstage[:], fs[:, 384:512])
                v.tensor_copy(Mp_bf[:, h * 129:h * 129 + 128], mstage[:])
                v.tensor_copy(Mp_bf[:, h * 129 + 128:h * 129 + 129], z_sb[:, h:h + 1])
                v.tensor_copy(F_bf[:, h * 512:(h + 1) * 512], fs[:]).then_inc(s["sFMV"].h, 1)
                s["sFMV"].n += 1
            for i in range(ST + 1):
                if i < ST:
                    v.wait_ge(s["sXL"].h, 16 * (i + 1))
                    v.reduce_sum(s1[i % 2][:], xbuf[i % 2][:], AX.X)
                if i >= 1:
                    j = i - 1
                    if i == ST:
                        v.drain()
                    v.scalar_tensor_tensor(nbias[j % 2][:], s1[j % 2][:], -1.0 / H,
                                           rstd[j % 2][:], ALU.mult, ALU.mult)
                if i < ST:
                    v.wait_ge(s["sSQ"].h, i + 1)
                    v.tensor_scalar(vtmp[:], ssq[i % 2][:], 1.0 / H, LN_EPS,
                                    ALU.mult, ALU.add)
                    v.scalar_tensor_tensor(mean2[:], s1[i % 2][:], 1.0 / (H * float(H)),
                                           s1[i % 2][:], ALU.mult, ALU.mult)
                if i >= 1:
                    j = i - 1
                    if j >= 2:
                        v.wait_ge(s["sXNst"].h, 16 * (j - 1))
                    if i == ST:
                        v.drain()
                    v.tensor_scalar(xnb[j % 2][:], xbuf[j % 2][:], rstd[j % 2][:, 0:1],
                                    nbias[j % 2][:, 0:1], ALU.mult, ALU.add
                                    ).then_inc(s["sXN"].h, 1)
                    s["sXN"].n += 1
                if i < ST:
                    if i == 0:
                        v.drain()
                    v.tensor_sub(var[i % 2][:], vtmp[:], mean2[:]).then_inc(s["sVAR"].h, 1)
                    s["sVAR"].n += 1
                    v.wait_ge(s["sSQT"].h, i + 1)
                    v.reciprocal(rstd[i % 2][:], sqv[i % 2][:])
            bdone(v, drain=True)
    ph.close()
    next_phase()

    CVH = ST + 2
    CNH = ST + 3

    def cv(h, what, c=0):
        base = h * CVH
        return {"nf": base + 1, "w": base + 2, "oc": base + 2 + c + 1}[what]

    def cn(h, what, c=0):
        base = h * CNH
        return {"nf": base + 1, "rel": base + 2, "wbc": base + 3, "co": base + 3 + c + 1}[what]

    ph = ExitStack()
    wqT = [sb(ph, f"wqT{i}", [128, HT * 512], BF16, is_transpose_dest=True) for i in range(2)]
    sqTg = [sb(ph, f"sqTg{i}", [128, 4 * S], BF16) for i in range(2)]
    ocb = sb(ph, "ocb", [128, S], BF16)
    nf_sb = sb(ph, "nf_sb", [128, NB * ST], F32)
    rc_sb = sb(ph, "rc_sb", [128, NB * ST], F32)
    rel_r = sb(ph, "rel_r", [1, NB], F32)
    rel_m = sb(ph, "rel_m", [1, 1], F32)
    rel_e = sb(ph, "rel_e", [1, NB], F32)
    rel_x = sb(ph, "rel_x", [1, NB], F32)
    rel_s = sb(ph, "rel_s", [1, 1], F32)
    w_row = sb(ph, "w_row", [1, NB], F32)
    w_pre = sb(ph, "w_pre", [1, NB], F32)
    wbc = sb(ph, "wbc", [128, NB], F32)
    ebuf = [sb(ph, f"ebuf{i}", [128, 512], F32) for i in range(2)]
    rbuf = [sb(ph, f"rbuf{i}", [128, 512], F32) for i in range(2)]
    nct = [sb(ph, f"nct{i}", [128, 1], F32) for i in range(2)]
    recc = [sb(ph, f"recc{i}", [128, 1], F32) for i in range(2)]
    pQ = [psum(ph, f"pQ{i}", [128, 512]) for i in range(2)]
    pAC = [psum(ph, f"pAC{i}", [128, 129]) for i in range(2)]
    pNF = psum(ph, "pNF", [128, NB * ST])
    pREL = psum(ph, "pREL", [1, NB])
    pWBC = psum(ph, "pWBC", [128, NB])

    with nc.Block("a2a", no_gpsimd_drain=True) as blk:
        @blk.sync
        def _(sy):
            bwait(sy)
            sy.wait_ge(s["sXNst"].h, 16 * ST)
            for ht in range(HT):
                for j in range(NSUP):
                    sy.dma_start(xnT[:, ht * S + j * 512: ht * S + (j + 1) * 512],
                                 xn_scr[j * 512:(j + 1) * 512, ht * 128:(ht + 1) * 128],
                                 transpose=True).then_inc(s["sXnT"].h, 16)
                    s["sXnT"].n += 16
            sy.wait_ge(s["sWS"].h, 16 * PREP_WQ)
            def load_wqT(es):
                for ht in range(HT):
                    sy.dma_start(wqT[es % 2][:, ht * 512:(ht + 1) * 512],
                                 wq_bf[es * 512:(es + 1) * 512, ht * 128:(ht + 1) * 128],
                                 transpose=True).then_inc(s["sWq"].h, 16)
                    s["sWq"].n += 16
            load_wqT(0)
            load_wqT(1)
            for es in range(4):
                for hl in range(4):
                    h = es * 4 + hl
                    sy.wait_ge(s["sCV"].h, cv(h, "oc", ST - 1))
                    sy.dma_start(
                        oc_scr[:, h * 128:(h + 1) * 128].rearrange("(c p) e -> p c e", p=128),
                        ocb.rearrange("p (c e) -> p c e", e=128)).then_inc(s["sOCst"].h, 16)
                    s["sOCst"].n += 16
                sy.wait_ge(s["sQE"].h, (es + 1) * 4 * NSUP)
                sy.dma_start(
                    sq_scr[es * 512:(es + 1) * 512, :].rearrange("(el p) t -> p el t", p=128),
                    sqTg[es % 2].rearrange("p (el t) -> p el t", el=4)).then_inc(s["sSQstr"].h, 16)
                s["sSQstr"].n += 16
                if es + 2 < 4:
                    sy.wait_ge(s["sQP"].h, (es + 1) * 4 * NSUP)
                    load_wqT(es + 2)
            bdone(sy)

        @blk.tensor
        def _(t):
            bwait(t)
            t.wait_ge(s["sXnT"].h, 16 * HT * NSUP)
            for es in range(4):
                t.wait_ge(s["sWq"].h, 16 * HT * (es + 1))
                if es >= 2:
                    t.wait_ge(s["sSQstr"].h, 16 * (es - 1))
                for el in range(4):
                    for j in range(NSUP):
                        gq = es * 4 * NSUP + el * NSUP + j
                        pq = pQ[gq % 2]
                        if gq >= 2:
                            t.wait_ge(s["sQX"].h, gq - 1)
                        mm = None
                        for ht in range(HT):
                            mm = t.matmul(pq[:],
                                          wqT[es % 2][:, ht * 512 + el * 128: ht * 512 + (el + 1) * 128],
                                          xnT[:, ht * S + j * 512: ht * S + (j + 1) * 512],
                                          start=(ht == 0), stop=(ht == HT - 1))
                        mm.then_inc(s["sQP"].h, 1)
                        s["sQP"].n += 1
                for hl in range(4):
                    h = es * 4 + hl
                    t.wait_ge(s["sQE"].h, es * 4 * NSUP + (hl + 1) * NSUP)
                    if h >= 1:
                        t.wait_ge(s["sCV"].h, cv(h - 1, "w"))
                    mm = None
                    for c in range(ST):
                        mm = t.matmul(pNF[:, c * NB:(c + 1) * NB],
                                      sqTg[es % 2][:, hl * S + c * 128:hl * S + (c + 1) * 128],
                                      norms_hm[:, h * NB:(h + 1) * NB],
                                      start=True, stop=True)
                    mm.then_inc(s["sCN"].h, 1)
                    s["sCN"].n += 1
                    t.wait_ge(s["sCV"].h, cv(h, "nf"))
                    mm = None
                    for c in range(ST):
                        mm = t.matmul(pREL[:], onc_f[:], nf_sb[:, c * NB:(c + 1) * NB],
                                      start=(c == 0), stop=(c == ST - 1))
                    mm.then_inc(s["sCN"].h, 1)
                    s["sCN"].n += 1
                    t.wait_ge(s["sCV"].h, cv(h, "w"))
                    t.matmul(pWBC[:], onr_f[:], w_row[:], start=True, stop=True
                             ).then_inc(s["sCN"].h, 1)
                    s["sCN"].n += 1
                    for c in range(ST):
                        k = h * ST + c
                        if k >= 2:
                            h2, c2 = (k - 2) // ST, (k - 2) % ST
                            t.wait_ge(s["sCV"].h, cv(h2, "oc", c2))
                        t.matmul(pAC[c % 2][:],
                                 sqTg[es % 2][:, hl * S + c * 128:hl * S + (c + 1) * 128],
                                 Mp_bf[:, h * 129:(h + 1) * 129],
                                 start=True, stop=True).then_inc(s["sCN"].h, 1)
                        s["sCN"].n += 1
            bdone(t)

        @blk.scalar
        def _(sc):
            bwait(sc)
            for es in range(4):
                for el in range(4):
                    for j in range(NSUP):
                        gq = es * 4 * NSUP + el * NSUP + j
                        pq = pQ[gq % 2]
                        sc.wait_ge(s["sQP"].h, gq + 1)
                        if gq >= 2:
                            sc.wait_ge(s["sQE"].h, gq - 1)
                        sc.activation(ebuf[gq % 2][:], pq[:], AF.Exp)
                        sc.activation(rbuf[gq % 2][:], pq[:], AF.Relu).then_inc(s["sQX"].h, 1)
                        s["sQX"].n += 1
                for hl in range(4):
                    h = es * 4 + hl
                    sc.wait_ge(s["sRe"].h, h + 1)
                    sc.activation(rel_x[:], rel_e[:], AF.Exp).then_inc(s["sSx"].h, 1)
                    s["sSx"].n += 1
            bdone(sc, drain=True)

        @blk.vector
        def _(v):
            bwait(v)
            for es in range(4):
                for el in range(4):
                    for j in range(NSUP):
                        gq = es * 4 * NSUP + el * NSUP + j
                        v.wait_ge(s["sQX"].h, gq + 1)
                        v.scalar_tensor_tensor(
                            sqTg[es % 2][:, (el * S + j * 512):(el * S + (j + 1) * 512)],
                            ebuf[gq % 2][:], 1.0, rbuf[gq % 2][:], ALU.min, ALU.add
                        ).then_inc(s["sQE"].h, 1)
                        s["sQE"].n += 1
                for hl in range(4):
                    h = es * 4 + hl
                    v.wait_ge(s["sCN"].h, cn(h, "nf"))
                    v.tensor_copy(nf_sb[:], pNF[:]).then_inc(s["sCV"].h, 1)
                    s["sCV"].n += 1
                    v.wait_ge(s["sCN"].h, cn(h, "rel"))
                    v.tensor_scalar(rel_r[:], pREL[:], 1.0 / S, None, ALU.mult)
                    v.tensor_scalar(rc_sb[:], nf_sb[:], EPS, None, ALU.max)
                    v.reduce_max(rel_m[:], rel_r[:], AX.X)
                    v.reciprocal(rc_sb[:], rc_sb[:])
                    v.tensor_single_scalar(rel_e[:], rel_r[:], rel_m[0:1, 0:1], ALU.subtract
                                           ).then_inc(s["sRe"].h, 1)
                    s["sRe"].n += 1
                    v.wait_ge(s["sSx"].h, h + 1)
                    v.tensor_mul(w_pre[:], rel_x[:], indb_row[:])
                    v.reduce_sum(rel_s[:], rel_x[:], AX.X)
                    v.drain()
                    v.reciprocal(rel_s[:], rel_s[:])
                    v.drain()
                    v.tensor_single_scalar(w_row[:], w_pre[:], rel_s[0:1, 0:1], ALU.mult
                                           ).then_inc(s["sCV"].h, 1)
                    s["sCV"].n += 1
                    v.wait_ge(s["sCN"].h, cn(h, "wbc"))
                    v.tensor_copy(wbc[:], pWBC[:])
                    v.drain()
                    rc3 = rc_sb.rearrange("p (c m) -> p c m", m=NB)
                    rw3 = rcw_all.rearrange("p (hh c m) -> p hh c m", hh=NH, m=NB)
                    for m in range(NB):
                        v.tensor_scalar(rw3[:, h, :, m], rc3[:, :, m], wbc[:, m:m + 1],
                                        None, ALU.mult)
                    if h >= 1:
                        v.wait_ge(s["sOCst"].h, 16 * h)
                    for c in range(ST + 1):
                        if c < ST:
                            v.wait_ge(s["sCN"].h, cn(h, "co", c))
                            v.tensor_scalar(nct[c % 2][:], pAC[c % 2][:, 128:129], EPS,
                                            None, ALU.max)
                        if c >= 1:
                            if c == ST:
                                v.drain()
                            v.tensor_scalar(ocb[:, (c - 1) * 128:c * 128],
                                            pAC[(c - 1) % 2][:, 0:128],
                                            recc[(c - 1) % 2][:, 0:1], indz_bc[:, 0:1],
                                            ALU.mult, ALU.mult).then_inc(s["sCV"].h, 1)
                            s["sCV"].n += 1
                        if c < ST:
                            if c == 0:
                                v.drain()
                            v.reciprocal(recc[c % 2][:], nct[c % 2][:])
            bdone(v, drain=True)
    ph.close()
    next_phase()

    ph = ExitStack()
    egw1T = sb(ph, "egw1T", [128, 16 * 512], BF16, is_transpose_dest=True)
    crT = [sb(ph, f"crT{i}", [128, 16 * 512], BF16, is_transpose_dest=True) for i in range(2)]
    hid_bf = sb(ph, "hid_bf", [128, 4 * 512], BF16)
    pHh = [psum(ph, f"pHh{i}", [128, 512]) for i in range(2)]
    pGP = psum(ph, "pGP", [128, 4])

    def hid_sgs(k_h):
        return (k_h // 4) * 8 + (k_h % 4) + 1

    with nc.Block("gprob", no_gpsimd_drain=True) as blk:
        @blk.sync
        def _(sy):
            bwait(sy)
            sy.wait_ge(s["sOCst"].h, 16 * NH)
            sy.wait_ge(s["sSQstr"].h, 16 * 4)
            sy.wait_ge(s["sWS"].h, 16 * PREP_EG)
            for et in range(HT):
                sy.dma_start(egw1T[:, et * 512:(et + 1) * 512],
                             egw1_bf[0:512, et * 128:(et + 1) * 128],
                             transpose=True).then_inc(s["sGW"].h, 16)
                s["sGW"].n += 16
            sy.wait_ge(s["sOCst"].h, 16 * NH)
            for j in range(NSUP):
                if j >= 2:
                    sy.wait_ge(s["sGPE"].h, (j - 1) * 8)
                for et in range(HT):
                    sy.dma_start(crT[j % 2][:, et * 512:(et + 1) * 512],
                                 oc_scr[j * 512:(j + 1) * 512, et * 128:(et + 1) * 128],
                                 transpose=True).then_inc(s["sCrT"].h, 16)
                    s["sCrT"].n += 16
            bdone(sy)

        @blk.tensor
        def _(t):
            bwait(t)
            t.wait_ge(s["sGW"].h, 16 * HT)
            for j in range(NSUP):
                t.wait_ge(s["sCrT"].h, 16 * HT * (j + 1))
                for ft in range(4):
                    k_h = j * 4 + ft
                    if k_h >= 2:
                        t.wait_ge(s["sGS"].h, hid_sgs(k_h - 2))
                    mm = None
                    for et in range(HT):
                        mm = t.matmul(pHh[ft % 2][:],
                                      egw1T[:, et * 512 + ft * 128: et * 512 + (ft + 1) * 128],
                                      crT[j % 2][:, et * 512:(et + 1) * 512],
                                      start=(et == 0), stop=(et == HT - 1))
                    mm.then_inc(s["sGPE"].h, 1)
                    s["sGPE"].n += 1
                t.wait_ge(s["sGS"].h, j * 8 + 4)
                for cc in range(4):
                    mm = None
                    for ft in range(4):
                        mm = t.matmul(pGP[:, cc:cc + 1],
                                      hid_bf[:, ft * 512 + cc * 128: ft * 512 + (cc + 1) * 128],
                                      egw2_bf[:, ft:ft + 1],
                                      start=(ft == 0), stop=(ft == 3))
                    mm.then_inc(s["sGPE"].h, 1)
                    s["sGPE"].n += 1
            bdone(t)

        @blk.scalar
        def _(sc):
            bwait(sc)
            for j in range(NSUP):
                for ft in range(4):
                    sc.wait_ge(s["sGPE"].h, j * 8 + ft + 1)
                    sc.activation(hid_bf[:, ft * 512:(ft + 1) * 512], pHh[ft % 2][:],
                                  AF.Relu, bias=b1_col[:, ft:ft + 1]).then_inc(s["sGS"].h, 1)
                    s["sGS"].n += 1
                for cc in range(4):
                    c = j * 4 + cc
                    sc.wait_ge(s["sGPE"].h, j * 8 + 4 + cc + 1)
                    sc.activation(p_col[:, c:c + 1], pGP[:, cc:cc + 1], AF.Sigmoid,
                                  bias=b2_bc[:, 0:1]).then_inc(s["sGS"].h, 1)
                    s["sGS"].n += 1
            bdone(sc, drain=True)

        @blk.vector
        def _(v):
            bwait(v)
            bdone(v)
    ph.close()
    next_phase()

    for grp in range(4):
        ph = ExitStack()
        wkvT = sb(ph, f"wkvT{grp}", [128, HT * 512], BF16, is_transpose_dest=True)
        wkT = wkvT
        wvT = wkvT
        sqTb = sb(ph, f"sqTb{grp}", [128, 4 * S], BF16)
        skTg = sb(ph, f"skTg{grp}", [128, 4 * S], BF16)
        vg = sb(ph, f"vg{grp}", [128, 4 * S], BF16)
        skp = sb(ph, f"skp{grp}", [128, 2 * S], BF16, is_transpose_dest=True)
        ocl = sb(ph, f"ocl{grp}", [128, 4 * 128], BF16)
        ybuf = sb(ph, f"ybuf{grp}", [128, S], BF16)
        S_sb = sb(ph, f"S_sb{grp}", [128, 129], F32)
        S_bfs = sb(ph, f"S_bfs{grp}", [128, 129], BF16)
        ebuf2 = [sb(ph, f"eb2_{grp}_{i}", [128, 512], F32) for i in range(2)]
        rbuf2 = [sb(ph, f"rb2_{grp}_{i}", [128, 512], F32) for i in range(2)]
        lo = sb(ph, f"lo{grp}", [128, 128], F32)
        of = sb(ph, f"of{grp}", [128, 128], F32)
        ocf = sb(ph, f"ocf{grp}", [128, 128], F32)
        t1b = sb(ph, f"t1b{grp}", [128, 128], F32)
        memb = sb(ph, f"memb{grp}", [128, 128], F32)
        t2b = sb(ph, f"t2b{grp}", [128, 128], F32)
        mst = sb(ph, f"mst{grp}", [128, 128], BF16)
        dent = sb(ph, f"dent{grp}", [128, 1], F32)
        recl = sb(ph, f"recl{grp}", [128, 1], F32)
        pKV = [psum(ph, f"pKV{grp}_{i}", [128, 512]) for i in range(2)]
        pSC = psum(ph, f"pSC{grp}", [128, 128])
        pNM = [psum(ph, f"pNM{grp}_{i}", [128, 129]) for i in range(2)]
        pDS = psum(ph, f"pDS{grp}", [128, 129])
        pFN = psum(ph, f"pFN{grp}", [128, 512])
        pDEN = psum(ph, f"pDEN{grp}", [128, 1])

        NPRJ = 4 * NSUP + ST
        kp0 = s["sKP"].n
        kx0 = s["sKX"].n
        ke0 = s["sKE"].n
        pe0 = s["sPEb"].n
        vb0 = s["sVb"].n
        sb0 = s["sSb"].n
        gb0 = s["sGb"].n
        ocld0 = s["sOCld"].n
        skp0 = s["sSkp"].n
        skst0 = s["sSKst"].n
        yst0 = s["sYst"].n

        PE_C = 4
        V_C = 6
        G_C = 2

        def pe_scan(hl, c, k):
            return pe0 + (hl * ST + c) * PE_C + k

        def v_scan(hl, c, k):
            return vb0 + (hl * ST + c) * V_C + k

        def sb_cnt(hl, c):
            return sb0 + hl * (ST - 1) + c + 1

        with nc.Block(f"a2b{grp}", no_gpsimd_drain=True) as blk:
            @blk.sync
            def _(sy):
                bwait(sy)
                if grp > 0:
                    sy.wait_ge(s["sPEb"].h, pe0)
                    sy.wait_ge(s["sYst"].h, yst0)
                sy.wait_ge(s["sWS"].h, 16 * PREP_WV)
                for ht in range(HT):
                    sy.dma_start(wkT[:, ht * 512:(ht + 1) * 512],
                                 wk_bf[grp * 512:(grp + 1) * 512, ht * 128:(ht + 1) * 128],
                                 transpose=True).then_inc(s["sKVw"].h, 16)
                    s["sKVw"].n += 16
                sy.wait_ge(s["sSQstr"].h, 16 * (grp + 1))
                for el in range(4):
                    sy.dma_start(sqTb[:, el * S:(el + 1) * S],
                                 sq_scr[grp * 512 + el * 128:grp * 512 + (el + 1) * 128, :]
                                 ).then_inc(s["sKVw"].h, 16)
                    s["sKVw"].n += 16
                sy.wait_ge(s["sKP"].h, kp0 + 4 * NSUP)
                for ht in range(HT):
                    sy.dma_start(wvT[:, ht * 512:(ht + 1) * 512],
                                 wv_bf[grp * 512:(grp + 1) * 512, ht * 128:(ht + 1) * 128],
                                 transpose=True).then_inc(s["sKVw"].h, 16)
                    s["sKVw"].n += 16
                sy.wait_ge(s["sKE"].h, ke0 + 4 * NSUP)
                sy.dma_start(
                    skT_scr[grp * 512:(grp + 1) * 512, :].rearrange("(el p) t -> p el t", p=128),
                    skTg.rearrange("p (el t) -> p el t", el=4)).then_inc(s["sSKst"].h, 16)
                s["sSKst"].n += 16
                sy.wait_ge(s["sSKst"].h, skst0 + 16)

                def load_skp(hl):
                    if hl >= 2:
                        sy.wait_ge(s["sPEb"].h, pe0 + (hl - 1) * ST * PE_C)
                    for c in range(ST):
                        sy.dma_start(
                            skp[:, ((hl % 2) * ST + c) * 128:((hl % 2) * ST + c + 1) * 128],
                            skT_scr[(grp * 4 + hl) * 128:(grp * 4 + hl + 1) * 128,
                                    c * 128:(c + 1) * 128],
                            transpose=True).then_inc(s["sSkp"].h, 16)
                        s["sSkp"].n += 16
                load_skp(0)
                load_skp(1)
                for hl in range(4):
                    sy.wait_ge(s["sVb"].h, v_scan(hl, ST - 1, V_C))
                    h = grp * 4 + hl
                    sy.dma_start(
                        y_scr[:, h * 128:(h + 1) * 128].rearrange("(c p) e -> p c e", p=128),
                        ybuf.rearrange("p (c e) -> p c e", e=128)).then_inc(s["sYst"].h, 16)
                    s["sYst"].n += 16
                    if hl + 2 < 4:
                        load_skp(hl + 2)
                bdone(sy)

            @blk.scalar
            def _(sc):
                bwait(sc)
                if grp > 0:
                    sc.wait_ge(s["sGb"].h, gb0)
                for el in range(4):
                    for j in range(NSUP):
                        gq = el * NSUP + j
                        sc.wait_ge(s["sKP"].h, kp0 + gq + 1)
                        if gq >= 2:
                            sc.wait_ge(s["sKE"].h, ke0 + gq - 1)
                        sc.activation(ebuf2[gq % 2][:], pKV[gq % 2][:], AF.Exp)
                        sc.activation(rbuf2[gq % 2][:], pKV[gq % 2][:], AF.Relu
                                      ).then_inc(s["sKX"].h, 1)
                        s["sKX"].n += 1
                for k0 in range(min(4, 4 * ST)):
                    hl0, c0 = k0 // ST, k0 % ST
                    h0 = grp * 4 + hl0
                    sc.dma_start(ocl[:, (k0 % 4) * 128:((k0 % 4) + 1) * 128],
                                 oc_scr[c0 * 128:(c0 + 1) * 128, h0 * 128:(h0 + 1) * 128]
                                 ).then_inc(s["sOCld"].h, 16)
                    s["sOCld"].n += 16
                for hl in range(4):
                    for c in range(ST):
                        k = hl * ST + c
                        if c < ST - 1:
                            sc.wait_ge(s["sVb"].h, v_scan(hl, c, 4))
                            sc.copy(S_bfs[:], S_sb[:]).then_inc(s["sSb"].h, 1)
                            s["sSb"].n += 1
                        kn = k + 4
                        if kn < 4 * ST:
                            hln, cn_ = kn // ST, kn % ST
                            hn = grp * 4 + hln
                            sc.wait_ge(s["sGb"].h, gb0 + k * 2 + 1)
                            sc.dma_start(ocl[:, (kn % 4) * 128:((kn % 4) + 1) * 128],
                                         oc_scr[cn_ * 128:(cn_ + 1) * 128,
                                                hn * 128:(hn + 1) * 128]
                                         ).then_inc(s["sOCld"].h, 16)
                            s["sOCld"].n += 16
                bdone(sc, drain=True)

            @blk.tensor
            def _(t):
                bwait(t)
                t.wait_ge(s["sKVw"].h, 16 * ((2 * HT + 4) * grp + HT + 4))
                for el in range(4):
                    for j in range(NSUP):
                        gq = el * NSUP + j
                        pq = pKV[gq % 2]
                        if gq >= 2:
                            t.wait_ge(s["sKX"].h, kx0 + gq - 1)
                        mm = None
                        for ht in range(HT):
                            mm = t.matmul(pq[:],
                                          wkT[:, ht * 512 + el * 128: ht * 512 + (el + 1) * 128],
                                          xnT[:, ht * S + j * 512: ht * S + (j + 1) * 512],
                                          start=(ht == 0), stop=(ht == HT - 1))
                        mm.then_inc(s["sKP"].h, 1)
                        s["sKP"].n += 1
                t.wait_ge(s["sKVw"].h, 16 * (2 * HT + 4) * (grp + 1))
                for c in range(ST):
                    gq = 4 * NSUP + c
                    pq = pKV[gq % 2]
                    t.wait_ge(s["sKE"].h, ke0 + gq - 1)
                    mm = None
                    for ht in range(HT):
                        mm = t.matmul(pq[:],
                                      xnT[:, ht * S + c * 128: ht * S + (c + 1) * 128],
                                      wvT[:, ht * 512:(ht + 1) * 512],
                                      start=(ht == 0), stop=(ht == HT - 1))
                    mm.then_inc(s["sKP"].h, 1)
                    s["sKP"].n += 1
                for hl in range(4):
                    h = grp * 4 + hl
                    t.wait_ge(s["sKE"].h, ke0 + NPRJ)
                    t.wait_ge(s["sSkp"].h, skp0 + 16 * (hl + 1) * ST)
                    for c in range(ST):
                        k = hl * ST + c
                        if k >= 1:
                            pc, cc = (k - 1) // ST, (k - 1) % ST
                        if k >= 1:
                            t.wait_ge(s["sVb"].h, v_scan(pc, cc, 1))
                        t.matmul(pSC[:], skTg[:, hl * S + c * 128:hl * S + (c + 1) * 128],
                                 sqTb[:, hl * S + c * 128:hl * S + (c + 1) * 128],
                                 start=True, stop=True).then_inc(s["sPEb"].h, 1)
                        s["sPEb"].n += 1
                        if k >= 1:
                            t.wait_ge(s["sVb"].h, v_scan(pc, cc, 3))
                        mm = None
                        for m in range(NB):
                            mm = t.matmul(pFN[:, m * 128:(m + 1) * 128],
                                          sqTb[:, hl * S + c * 128:hl * S + (c + 1) * 128],
                                          F_bf[:, h * 512 + m * 128: h * 512 + (m + 1) * 128],
                                          start=True, stop=True)
                        mm.then_inc(s["sPEb"].h, 1)
                        s["sPEb"].n += 1
                        t.wait_ge(s["sVb"].h, v_scan(hl, c, 1))
                        if k >= 1:
                            t.wait_ge(s["sVb"].h, v_scan(pc, cc, 2))
                        pn = pNM[c % 2]
                        sqt = sqTb[:, hl * S + c * 128:hl * S + (c + 1) * 128]
                        if c == 0:
                            t.matmul(pn[:, 0:128], mst[:],
                                     vg[:, c * 512 + hl * 128:c * 512 + (hl + 1) * 128],
                                     start=True, stop=True)
                            mm = t.matmul(pDEN[:], mst[:], onc_bf[:],
                                          start=True, stop=True)
                        else:
                            t.matmul(pn[:, 0:128], mst[:],
                                     vg[:, c * 512 + hl * 128:c * 512 + (hl + 1) * 128],
                                     start=True, stop=False)
                            t.matmul(pDEN[:], mst[:], onc_bf[:], start=True, stop=False)
                            t.wait_ge(s["sSb"].h, sb_cnt(hl, c - 1))
                            t.matmul(pn[:, 0:128], sqt, S_bfs[:, 0:128],
                                     start=False, stop=True)
                            mm = t.matmul(pDEN[:], sqt, S_bfs[:, 128:129],
                                          start=False, stop=True)
                        mm.then_inc(s["sPEb"].h, 1)
                        s["sPEb"].n += 1
                        if c >= 1:
                            t.wait_ge(s["sVb"].h, v_scan(hl, c - 1, 4))
                        elif hl >= 1:
                            t.wait_ge(s["sVb"].h, v_scan(hl - 1, ST - 1, 4))
                        t.matmul(pDS[:, 0:128],
                                 skp[:, ((hl % 2) * ST + c) * 128:((hl % 2) * ST + c + 1) * 128],
                                 vg[:, c * 512 + hl * 128:c * 512 + (hl + 1) * 128],
                                 start=True, stop=True)
                        t.matmul(pDS[:, 128:129],
                                 skp[:, ((hl % 2) * ST + c) * 128:((hl % 2) * ST + c + 1) * 128],
                                 onc_bf[:], start=True, stop=True).then_inc(s["sPEb"].h, 1)
                        s["sPEb"].n += 1
                bdone(t)

            @blk.vector
            def _(v):
                bwait(v)
                for el in range(4):
                    for j in range(NSUP):
                        gq = el * NSUP + j
                        v.wait_ge(s["sKX"].h, kx0 + gq + 1)
                        v.scalar_tensor_tensor(
                            skTg[:, (el * S + j * 512):(el * S + (j + 1) * 512)],
                            ebuf2[gq % 2][:], 1.0, rbuf2[gq % 2][:], ALU.min, ALU.add
                        ).then_inc(s["sKE"].h, 1)
                        s["sKE"].n += 1
                for c in range(ST):
                    gq = 4 * NSUP + c
                    v.wait_ge(s["sKP"].h, kp0 + gq + 1)
                    v.tensor_copy(vg[:, c * 512:(c + 1) * 512], pKV[gq % 2][:]
                                  ).then_inc(s["sKE"].h, 1)
                    s["sKE"].n += 1
                for hl in range(4):
                    h = grp * 4 + hl
                    v.memset(S_sb[:], 0.0)
                    for c in range(ST):
                        k = hl * ST + c
                        base = h * ST * NB + c * NB
                        v.wait_ge(s["sPEb"].h, pe_scan(hl, c, 1))
                        if k >= 1:
                            pc, cc = (k - 1) // ST, (k - 1) % ST
                            v.wait_ge(s["sPEb"].h, pe_scan(pc, cc, 3))
                        v.tensor_tensor(mst[:], pSC[:], maskT[:], ALU.mult
                                        ).then_inc(s["sVb"].h, 1)
                        s["sVb"].n += 1
                        v.wait_ge(s["sPEb"].h, pe_scan(hl, c, 3))
                        v.tensor_scalar(dent[:], pDEN[:], EPS, None, ALU.max)
                        v.tensor_scalar(of[:], pFN[:, 0:128], rcw_all[:, base:base + 1],
                                        None, ALU.mult)
                        v.reciprocal(recl[:], dent[:])
                        v.scalar_tensor_tensor(of[:], pFN[:, 128:256],
                                               rcw_all[:, base + 1:base + 2], of[:],
                                               ALU.mult, ALU.add)
                        v.tensor_scalar(lo[:], pNM[c % 2][:, 0:128], recl[:, 0:1],
                                        None, ALU.mult).then_inc(s["sVb"].h, 1)
                        s["sVb"].n += 1
                        v.scalar_tensor_tensor(of[:], pFN[:, 256:384],
                                               rcw_all[:, base + 2:base + 3], of[:],
                                               ALU.mult, ALU.add)
                        v.scalar_tensor_tensor(of[:], pFN[:, 384:512],
                                               rcw_all[:, base + 3:base + 4], of[:],
                                               ALU.mult, ALU.add).then_inc(s["sVb"].h, 1)
                        s["sVb"].n += 1
                        v.wait_ge(s["sPEb"].h, pe_scan(hl, c, 4))
                        v.tensor_tensor(S_sb[:], pDS[:], S_sb[:], ALU.add
                                        ).then_inc(s["sVb"].h, 1)
                        s["sVb"].n += 1
                        v.wait_ge(s["sGb"].h, gb0 + k * G_C + 1)
                        v.scalar_tensor_tensor(memb[:], t1b[:], p_col[:, c:c + 1], ocf[:],
                                               ALU.mult, ALU.add).then_inc(s["sVb"].h, 1)
                        s["sVb"].n += 1
                        v.wait_ge(s["sGb"].h, gb0 + k * G_C + 2)
                        if c == 0:
                            if hl == 0:
                                if grp > 0:
                                    v.wait_ge(s["sYst"].h, yst0)
                            else:
                                v.wait_ge(s["sYst"].h, yst0 + 16 * hl)
                        v.scalar_tensor_tensor(ybuf[:, c * 128:(c + 1) * 128], t2b[:],
                                               g_bc[:, h:h + 1], lo[:], ALU.mult, ALU.add
                                               ).then_inc(s["sVb"].h, 1)
                        s["sVb"].n += 1
                bdone(v, drain=True)

            @blk.gpsimd
            def _(g):
                for hl in range(4):
                    for c in range(ST):
                        k = hl * ST + c
                        g.wait_ge(s["sOCld"].h, ocld0 + 16 * (k + 1))
                        g.wait_ge(s["sVb"].h, v_scan(hl, c, 3))
                        g.tensor_copy(ocf[:], ocl[:, (k % 4) * 128:((k % 4) + 1) * 128])
                        g.tensor_sub(t1b[:], of[:], ocf[:]).then_inc(s["sGb"].h, 1)
                        s["sGb"].n += 1
                        g.wait_ge(s["sVb"].h, v_scan(hl, c, 5))
                        g.wait_ge(s["sVb"].h, v_scan(hl, c, 2))
                        if k >= 1:
                            pc, cc = (k - 1) // ST, (k - 1) % ST
                            g.wait_ge(s["sVb"].h, v_scan(pc, cc, 6))
                        g.tensor_sub(t2b[:], memb[:], lo[:]).then_inc(s["sGb"].h, 1)
                        s["sGb"].n += 1
        ph.close()
        next_phase()
    attn.close()

    ph = ExitStack()
    woT = sb(ph, "woT", [128, HT * 2048], BF16, is_transpose_dest=True)
    yT = [sb(ph, f"yT{i}", [128, HT * 512], BF16, is_transpose_dest=True) for i in range(2)]
    xre = [sb(ph, f"xre{i}", [128, H], F32) for i in range(2)]
    x1t = [sb(ph, f"x1t{i}", [128, H], F32) for i in range(2)]
    xn2t = [sb(ph, f"xn2t{i}", [128, H], BF16) for i in range(2)]
    junk2 = sb(ph, "junk2", [128, H], BF16)
    ssq2 = [sb(ph, f"ssq2{i}", [128, 1], F32) for i in range(2)]
    s1b = [sb(ph, f"s1b{i}", [128, 1], F32) for i in range(2)]
    vtmp2 = sb(ph, "vtmp2", [128, 1], F32)
    mean2b = sb(ph, "mean2b", [128, 1], F32)
    varb = [sb(ph, f"varb{i}", [128, 1], F32) for i in range(2)]
    sqvb = [sb(ph, f"sqvb{i}", [128, 1], F32) for i in range(2)]
    rstdb = [sb(ph, f"rstdb{i}", [128, 1], F32) for i in range(2)]
    nbiasb = [sb(ph, f"nbiasb{i}", [128, 1], F32) for i in range(2)]
    pO = [psum(ph, f"pO{i}", [128, 2048]) for i in range(2)]

    with nc.Block("a3", no_gpsimd_drain=True) as blk:
        @blk.sync
        def _(sy):
            bwait(sy)
            sy.wait_ge(s["sWS"].h, 16 * N_PREP)
            sy.wait_ge(s["sYst"].h, 16 * NH)
            sy.wait_ge(s["sSKst"].h, s["sSKst"].n)
            for et in range(HT):
                for hob in range(4):
                    sy.dma_start(woT[:, et * 2048 + hob * 512: et * 2048 + (hob + 1) * 512],
                                 wo_bf[hob * 512:(hob + 1) * 512, et * 128:(et + 1) * 128],
                                 transpose=True).then_inc(s["sWo"].h, 16)
                    s["sWo"].n += 16
            sy.wait_ge(s["sYst"].h, 16 * NH)

            def load_yT(j):
                if j >= 2:
                    sy.wait_ge(s["sOP"].h, (j - 1) * 4)
                for et in range(HT):
                    sy.dma_start(yT[j % 2][:, et * 512:(et + 1) * 512],
                                 y_scr[j * 512:(j + 1) * 512, et * 128:(et + 1) * 128],
                                 transpose=True).then_inc(s["sYT"].h, 16)
                    s["sYT"].n += 16
            load_yT(0)
            if NSUP > 1:
                load_yT(1)
            for i in range(ST):
                sy.wait_ge(s["sRv"].h, i + 1)
                sy.wait_ge(s["sXN2v"].h, i + 1)
                sy.dma_start(x1_scr[i * 128:(i + 1) * 128, :], x1t[i % 2][:]
                             ).then_inc(s["sX1st"].h, 16)
                s["sX1st"].n += 16
                sy.dma_start(xn2_scr[i * 128:(i + 1) * 128, :], xn2t[i % 2][:]
                             ).then_inc(s["sXN2st"].h, 16)
                s["sXN2st"].n += 16
                if i % 4 == 3 and (i // 4) + 2 < NSUP:
                    load_yT((i // 4) + 2)
            bdone(sy)

        @blk.scalar
        def _(sc):
            bwait(sc)
            for i in range(min(2, ST)):
                sc.dma_start(xre[i % 2][:], x_in[i * 128:(i + 1) * 128, :]
                             ).then_inc(s["sXre"].h, 16)
                s["sXre"].n += 16
            for i in range(ST):
                sc.wait_ge(s["sRv"].h, i + 1)
                if i >= 2:
                    sc.wait_ge(s["sVAR2"].h, i - 1)
                sc.activation(junk2[:], x1t[i % 2][:], AF.Square,
                              accum_out=ssq2[i % 2][:])
                sc.drain()
                sc.sem_inc(s["sSQ2"].h, 1)
                s["sSQ2"].n += 1
                sc.wait_ge(s["sVAR2"].h, i + 1)
                sc.activation(sqvb[i % 2][:], varb[i % 2][:], AF.Sqrt
                              ).then_inc(s["sSQT2"].h, 1)
                s["sSQT2"].n += 1
                if i + 2 < ST:
                    sc.dma_start(xre[i % 2][:], x_in[(i + 2) * 128:(i + 3) * 128, :]
                                 ).then_inc(s["sXre"].h, 16)
                    s["sXre"].n += 16
            bdone(sc, drain=True)

        @blk.tensor
        def _(t):
            bwait(t)
            t.wait_ge(s["sWo"].h, 16 * HT * 4)
            for j in range(NSUP):
                t.wait_ge(s["sYT"].h, 16 * HT * (j + 1))
                for cc in range(4):
                    c = j * 4 + cc
                    if c >= 2:
                        t.wait_ge(s["sRv"].h, c - 1)
                    mm = None
                    for et in range(HT):
                        for hob in range(4):
                            mm = t.matmul(pO[c % 2][:, hob * 512:(hob + 1) * 512],
                                          yT[j % 2][:, et * 512 + cc * 128: et * 512 + (cc + 1) * 128],
                                          woT[:, et * 2048 + hob * 512: et * 2048 + (hob + 1) * 512],
                                          start=(et == 0), stop=(et == HT - 1))
                    mm.then_inc(s["sOP"].h, 1)
                    s["sOP"].n += 1
            bdone(t)

        @blk.vector
        def _(v):
            bwait(v)
            for i in range(ST + 1):
                if i < ST:
                    v.wait_ge(s["sOP"].h, i + 1)
                    v.wait_ge(s["sXre"].h, 16 * (i + 1))
                    if i >= 2:
                        v.wait_ge(s["sX1st"].h, 16 * (i - 1))
                        v.wait_ge(s["sSQ2"].h, i - 1)
                        v.wait_ge(s["sXN2v"].h, i - 1)
                    v.tensor_tensor(x1t[i % 2][:], pO[i % 2][:], xre[i % 2][:], ALU.add
                                    ).then_inc(s["sRv"].h, 1)
                    s["sRv"].n += 1
                    v.reduce_sum(s1b[i % 2][:], x1t[i % 2][:], AX.X)
                if i >= 1:
                    j = i - 1
                    if i == ST:
                        v.drain()
                    v.scalar_tensor_tensor(nbiasb[j % 2][:], s1b[j % 2][:], -1.0 / H,
                                           rstdb[j % 2][:], ALU.mult, ALU.mult)
                if i < ST:
                    v.wait_ge(s["sSQ2"].h, i + 1)
                    v.tensor_scalar(vtmp2[:], ssq2[i % 2][:], 1.0 / H, LN_EPS,
                                    ALU.mult, ALU.add)
                    v.scalar_tensor_tensor(mean2b[:], s1b[i % 2][:], 1.0 / (H * float(H)),
                                           s1b[i % 2][:], ALU.mult, ALU.mult)
                if i >= 1:
                    j = i - 1
                    if j >= 2:
                        v.wait_ge(s["sXN2st"].h, 16 * (j - 1))
                    if i == ST:
                        v.drain()
                    v.tensor_scalar(xn2t[j % 2][:], x1t[j % 2][:], rstdb[j % 2][:, 0:1],
                                    nbiasb[j % 2][:, 0:1], ALU.mult, ALU.add
                                    ).then_inc(s["sXN2v"].h, 1)
                    s["sXN2v"].n += 1
                if i < ST:
                    if i == 0:
                        v.drain()
                    v.tensor_sub(varb[i % 2][:], vtmp2[:], mean2b[:]).then_inc(s["sVAR2"].h, 1)
                    s["sVAR2"].n += 1
                    v.wait_ge(s["sSQT2"].h, i + 1)
                    v.reciprocal(rstdb[i % 2][:], sqvb[i % 2][:])
            bdone(v, drain=True)
    ph.close()
    next_phase()

    ph = ExitStack()
    xn2T = sb(ph, "xn2T", [128, HT * 512], BF16, is_transpose_dest=True)
    hact = sb(ph, "hact", [128, IT * 512], BF16)
    wgT = [sb(ph, f"wgT{i}", [128, HT * 512], BF16, is_transpose_dest=True) for i in range(2)]
    wuT = [sb(ph, f"wuT{i}", [128, HT * 512], BF16, is_transpose_dest=True) for i in range(2)]
    wdT = [sb(ph, f"wdT{i}", [128, 512], BF16, is_transpose_dest=True) for i in range(4)]
    gs_t = [sb(ph, f"gs_t{i}", [128, 512], F32) for i in range(2)]
    x1l = sb(ph, "x1l", [128, 4 * 2048], F32)
    rr = sb(ph, "rr", [128, 4], F32)
    outb = [sb(ph, f"outb{i}", [128, 512], F32) for i in range(2)]
    pGf = psum(ph, "pGf", [128, 512])
    pUf = psum(ph, "pUf", [128, 512])
    pDN = [psum(ph, f"pDN{i}", [128, 512]) for i in range(4)]
    pR2 = psum(ph, "pR2", [128, 4])

    with nc.Block("ffn", no_gpsimd_drain=True) as blk:
        @blk.sync
        def _(sy):
            bwait(sy)
            sy.wait_ge(s["sWS"].h, 16 * N_PREP)
            sy.wait_ge(s["sX1st"].h, 16 * ST)
            sy.wait_ge(s["sXN2st"].h, 16 * ST)
            for sbk in range(NFS):
                if sbk >= 1:
                    sy.wait_ge(s["sGU"].h, sbk * 2 * IT)
                for ht in range(HT):
                    sy.dma_start(xn2T[:, ht * 512:(ht + 1) * 512],
                                 xn2_scr[sbk * 512:(sbk + 1) * 512, ht * 128:(ht + 1) * 128],
                                 transpose=True).then_inc(s["sX2T"].h, 16)
                    s["sX2T"].n += 16
                for isup in range(16):
                    gi = sbk * 16 + isup
                    if gi >= 2:
                        sy.wait_ge(s["sGU"].h, (gi - 1) * 8)
                    for ht in range(HT):
                        sy.dma_start(wgT[gi % 2][:, ht * 512:(ht + 1) * 512],
                                     wg_bf[isup * 512:(isup + 1) * 512, ht * 128:(ht + 1) * 128],
                                     transpose=True).then_inc(s["sFW"].h, 16)
                        s["sFW"].n += 16
                    for ht in range(HT):
                        sy.dma_start(wuT[gi % 2][:, ht * 512:(ht + 1) * 512],
                                     wu_bf[isup * 512:(isup + 1) * 512, ht * 128:(ht + 1) * 128],
                                     transpose=True).then_inc(s["sFW"].h, 16)
                        s["sFW"].n += 16
                for hob in range(4):
                    for it in range(IT):
                        k = (sbk * 4 + hob) * IT + it
                        if k >= 4:
                            sy.wait_ge(s["sDP"].h, k - 3)
                        sy.dma_start(wdT[it % 4][:],
                                     wd_bf[hob * 512:(hob + 1) * 512, it * 128:(it + 1) * 128],
                                     transpose=True).then_inc(s["sDW"].h, 16)
                        s["sDW"].n += 16
            bdone(sy)

        @blk.scalar
        def _(sc):
            bwait(sc)
            for sbk in range(NFS):
                sc.wait_ge(s["sX1st"].h, 16 * 4 * (sbk + 1))
                if sbk >= 1:
                    sc.wait_ge(s["sDE"].h, sbk * 16)
                for ss in range(4):
                    sc.dma_start(x1l[:, ss * 2048:(ss + 1) * 2048],
                                 x1_scr[(sbk * 4 + ss) * 128:(sbk * 4 + ss + 1) * 128, :]
                                 ).then_inc(s["sXR1"].h, 16)
                    s["sXR1"].n += 16
                for it in range(IT):
                    gi = sbk * IT + it
                    sc.wait_ge(s["sGU"].h, gi * 2 + 1)
                    if gi >= 2:
                        sc.wait_ge(s["sHA"].h, gi - 1)
                    sc.activation(gs_t[it % 2][:], pGf[:], AF.Silu,
                                  scale=sg_col[:, it:it + 1]).then_inc(s["sSil"].h, 1)
                    s["sSil"].n += 1
                for hob in range(4):
                    for ss in range(4):
                        k = hob * 4 + ss
                        sc.wait_ge(s["sDE"].h, sbk * 16 + k + 1)
                        sc.dma_start(out[(sbk * 4 + ss) * 128:(sbk * 4 + ss + 1) * 128,
                                         hob * 512:(hob + 1) * 512],
                                     outb[k % 2][:]).then_inc(s["sOut"].h, 16)
                        s["sOut"].n += 16
            sc.wait_ge(s["sOut"].h, s["sOut"].n)
            bdone(sc, drain=True)

        @blk.tensor
        def _(t):
            bwait(t)
            for sbk in range(NFS):
                t.wait_ge(s["sX2T"].h, 16 * HT * (sbk + 1))
                for it in range(IT):
                    gi = sbk * IT + it
                    isup, itl = it // 4, it % 4
                    t.wait_ge(s["sFW"].h, 16 * 2 * HT * (sbk * 16 + isup + 1))
                    if gi >= 1:
                        t.wait_ge(s["sSil"].h, gi)
                    mm = None
                    for ht in range(HT):
                        mm = t.matmul(pGf[:],
                                      wgT[(sbk * 16 + isup) % 2][:, ht * 512 + itl * 128: ht * 512 + (itl + 1) * 128],
                                      xn2T[:, ht * 512:(ht + 1) * 512],
                                      start=(ht == 0), stop=(ht == HT - 1))
                    mm.then_inc(s["sGU"].h, 1)
                    s["sGU"].n += 1
                    if gi >= 1:
                        t.wait_ge(s["sHA"].h, gi)
                    mm = None
                    for ht in range(HT):
                        mm = t.matmul(pUf[:],
                                      wuT[(sbk * 16 + isup) % 2][:, ht * 512 + itl * 128: ht * 512 + (itl + 1) * 128],
                                      xn2T[:, ht * 512:(ht + 1) * 512],
                                      start=(ht == 0), stop=(ht == HT - 1))
                    mm.then_inc(s["sGU"].h, 1)
                    s["sGU"].n += 1
                t.wait_ge(s["sHA"].h, (sbk + 1) * IT)
                for ss in range(4):
                    mm = None
                    for it in range(IT):
                        mm = t.matmul(pR2[:, ss:ss + 1],
                                      hact[:, it * 512 + ss * 128: it * 512 + (ss + 1) * 128],
                                      onc_bf[:], start=(it == 0), stop=(it == IT - 1))
                    mm.then_inc(s["sR2"].h, 1)
                    s["sR2"].n += 1
                for hob in range(4):
                    for it in range(IT):
                        k = (sbk * 4 + hob) * IT + it
                        t.wait_ge(s["sDW"].h, 16 * (k + 1))
                        if hob >= 1 and it == 0:
                            t.wait_ge(s["sDE"].h, sbk * 16 + hob * 4)
                        mm = None
                        for ss in range(4):
                            mm = t.matmul(pDN[ss][:],
                                          hact[:, it * 512 + ss * 128: it * 512 + (ss + 1) * 128],
                                          wdT[it % 4][:],
                                          start=(it == 0), stop=(it == IT - 1))
                        mm.then_inc(s["sDP"].h, 1)
                        s["sDP"].n += 1
                if sbk < NFS - 1:
                    t.wait_ge(s["sDE"].h, (sbk + 1) * 16)
            bdone(t)

        @blk.vector
        def _(v):
            bwait(v)
            for sbk in range(NFS):
                for it in range(IT):
                    gi = sbk * IT + it
                    v.wait_ge(s["sSil"].h, gi + 1)
                    v.wait_ge(s["sGU"].h, gi * 2 + 2)
                    v.scalar_tensor_tensor(hact[:, it * 512:(it + 1) * 512],
                                           pUf[:], su_col[:, it:it + 1],
                                           gs_t[it % 2][:], ALU.mult, ALU.mult
                                           ).then_inc(s["sHA"].h, 1)
                    s["sHA"].n += 1
                v.wait_ge(s["sR2"].h, (sbk + 1) * 4)
                v.tensor_scalar(rr[:], pR2[:], 128.0, None, ALU.mult)
                v.drain()
                for hob in range(4):
                    v.wait_ge(s["sDP"].h, (sbk * 4 + hob + 1) * IT)
                    v.wait_ge(s["sXR1"].h, 16 * 4 * (sbk + 1))
                    for ss in range(4):
                        k = hob * 4 + ss
                        if sbk * 16 + k >= 2:
                            v.wait_ge(s["sOut"].h, 16 * (sbk * 16 + k - 1))
                        v.scalar_tensor_tensor(outb[k % 2][:], pDN[ss][:], rr[:, ss:ss + 1],
                                               sd_bc[:, hob * 512:(hob + 1) * 512],
                                               ALU.subtract, ALU.mult)
                        v.tensor_tensor(outb[k % 2][:], outb[k % 2][:],
                                        x1l[:, ss * 2048 + hob * 512: ss * 2048 + (hob + 1) * 512],
                                        ALU.add).then_inc(s["sDE"].h, 1)
                        s["sDE"].n += 1
            bdone(v, drain=True)
    ph.close()
    top.close()
    return nc


_NC_CACHE = {}


def get_nc(S=2048):
    if S not in _NC_CACHE:
        _NC_CACHE[S] = build_nc(S)
    return _NC_CACHE[S]


def kernel(**inputs):
    x = np.ascontiguousarray(inputs["L0_attn_hidden_states"])
    nc = get_nc(2048)
    common = {
        "wq": np.ascontiguousarray(inputs["L0_attn_w_q"]),
        "wk": np.ascontiguousarray(inputs["L0_attn_w_k"]),
        "wv": np.ascontiguousarray(inputs["L0_attn_w_v"]),
        "wo": np.ascontiguousarray(inputs["L0_attn_w_o"]),
        "egw1": np.ascontiguousarray(inputs["L0_attn_eg_w1"]),
        "egb1": np.ascontiguousarray(inputs["L0_attn_eg_b1"]),
        "egw2": np.ascontiguousarray(inputs["L0_attn_eg_w2"]),
        "egb2": np.ascontiguousarray(inputs["L0_attn_eg_b2"]),
        "mgate": np.ascontiguousarray(inputs["L0_attn_memory_gate"]),
        "fmem": np.ascontiguousarray(inputs["L0_attn_fine_memories"]),
        "fnorm": np.ascontiguousarray(inputs["L0_attn_fine_memory_norms"]),
        "gw": np.ascontiguousarray(inputs["L0_ffn_gate_w"]),
        "gs": np.ascontiguousarray(inputs["L0_ffn_gate_scale"]),
        "uw": np.ascontiguousarray(inputs["L0_ffn_up_w"]),
        "us": np.ascontiguousarray(inputs["L0_ffn_up_scale"]),
        "dw": np.ascontiguousarray(inputs["L0_ffn_down_w"]),
        "ds": np.ascontiguousarray(inputs["L0_ffn_down_scale"]),
    }
    in_maps = []
    for b in range(NCORES):
        m = dict(common)
        m["x"] = np.ascontiguousarray(x[b])
        in_maps.append(m)
    res = run_bass_kernel_spmd(nc, in_maps, core_ids=list(range(NCORES)))
    return np.stack([res.results[b]["out"] for b in range(NCORES)], axis=0)


# version 09 (final)
# speedup vs baseline: 1.0772x; 1.0154x over previous
from contextlib import ExitStack

import numpy as np
import ml_dtypes
import concourse.bass as bass
import concourse.mybir as mybir
from concourse.bass_utils import run_bass_kernel_spmd

F32 = mybir.dt.float32
BF16 = mybir.dt.bfloat16
I32 = mybir.dt.int32
AF = mybir.ActivationFunctionType
ALU = mybir.AluOpType
AX = mybir.AxisListType

H = 2048
NH = 16
HD = 128
NB = 4
HT = H // 128
II = 8192
IT = II // 128
EPS = 1e-6
LN_EPS = 1e-5
NCORES = 8


class Sem:
    def __init__(self, nc, name):
        self.h = nc.alloc_semaphore(name=name)
        self.n = 0


def build_nc(S=2048, dbg=False):
    ST = S // 128
    NSUP = S // 512
    NFS = S // 512

    nc = bass.Bass("TRN2", target_bir_lowering=False, debug=False,
                   detect_race_conditions=False)

    x_in = nc.dram_tensor("x", [S, H], F32, kind="ExternalInput").ap()
    wq_in = nc.dram_tensor("wq", [H, H], F32, kind="ExternalInput").ap()
    wk_in = nc.dram_tensor("wk", [H, H], F32, kind="ExternalInput").ap()
    wv_in = nc.dram_tensor("wv", [H, H], F32, kind="ExternalInput").ap()
    wo_in = nc.dram_tensor("wo", [H, H], F32, kind="ExternalInput").ap()
    egw1_in = nc.dram_tensor("egw1", [512, H], F32, kind="ExternalInput").ap()
    egb1_in = nc.dram_tensor("egb1", [512], F32, kind="ExternalInput").ap()
    egw2_in = nc.dram_tensor("egw2", [1, 512], F32, kind="ExternalInput").ap()
    egb2_in = nc.dram_tensor("egb2", [1], F32, kind="ExternalInput").ap()
    mg_in = nc.dram_tensor("mgate", [NH], F32, kind="ExternalInput").ap()
    fm_in = nc.dram_tensor("fmem", [NB, NH, HD, HD], F32, kind="ExternalInput").ap()
    fn_in = nc.dram_tensor("fnorm", [NB, NH, HD], F32, kind="ExternalInput").ap()
    wg_in = nc.dram_tensor("gw", [II, H], I32, kind="ExternalInput").ap()
    sg_in = nc.dram_tensor("gs", [II], F32, kind="ExternalInput").ap()
    wu_in = nc.dram_tensor("uw", [II, H], I32, kind="ExternalInput").ap()
    su_in = nc.dram_tensor("us", [II], F32, kind="ExternalInput").ap()
    wd_in = nc.dram_tensor("dw", [H, II], I32, kind="ExternalInput").ap()
    sd_in = nc.dram_tensor("ds", [H], F32, kind="ExternalInput").ap()
    out = nc.dram_tensor("out", [S, H], F32, kind="ExternalOutput").ap()

    wq_bf = nc.dram_tensor("wq_bf", [H, H], BF16, kind="Internal").ap()
    wk_bf = nc.dram_tensor("wk_bf", [H, H], BF16, kind="Internal").ap()
    wv_bf = nc.dram_tensor("wv_bf", [H, H], BF16, kind="Internal").ap()
    wo_bf = nc.dram_tensor("wo_bf", [H, H], BF16, kind="Internal").ap()
    egw1_bf = nc.dram_tensor("egw1_bf", [512, H], BF16, kind="Internal").ap()
    wg_bf = nc.dram_tensor("wg_bf", [II, H], BF16, kind="Internal").ap()
    wu_bf = nc.dram_tensor("wu_bf", [II, H], BF16, kind="Internal").ap()
    wd_bf = nc.dram_tensor("wd_bf", [H, II], BF16, kind="Internal").ap()
    kscr = "ExternalOutput" if dbg else "Internal"
    xn_scr = nc.dram_tensor("xn_scr", [S, H], BF16, kind=kscr).ap()
    sq_scr = nc.dram_tensor("sq_scr", [H, S], BF16, kind=kscr).ap()
    skT_scr = nc.dram_tensor("skT_scr", [H, S], BF16, kind=kscr).ap()
    oc_scr = nc.dram_tensor("oc_scr", [S, H], BF16, kind=kscr).ap()
    y_scr = nc.dram_tensor("y_scr", [S, H], BF16, kind=kscr).ap()
    x1_scr = nc.dram_tensor("x1_scr", [S, H], F32, kind=kscr).ap()
    xn2_scr = nc.dram_tensor("xn2_scr", [S, H], BF16, kind=kscr).ap()

    ident_c = nc.inline_tensor(np.eye(128, dtype=np.float32), "ident_c").ap()
    maskT_c = nc.inline_tensor(np.triu(np.ones((128, 128), dtype=np.float32)), "maskT_c").ap()
    onc_bf_c = nc.inline_tensor(np.ones((128, 1), dtype=ml_dtypes.bfloat16), "onc_bf_c").ap()
    onc_f_c = nc.inline_tensor(np.ones((128, 1), dtype=np.float32), "onc_f_c").ap()
    onr_f_c = nc.inline_tensor(np.ones((1, 128), dtype=np.float32), "onr_f_c").ap()

    sems = {}
    for nm in ["bar",
               "sXL", "sSQ", "sVAR", "sSQT", "sXN", "sXNst",
               "sSm", "sFM", "sFMV", "sPE0", "sV0", "sS0",
               "sWL", "sWS",
               "sXnT", "sWq", "sQP", "sQX", "sQE", "sCN", "sCV", "sRe", "sSx",
               "sOCst", "sSQstr",
               "sGW", "sCrT", "sGPE", "sGS",
               "sKVw", "sKP", "sKX", "sKE", "sSKst", "sSkp", "sOCld",
               "sPEb", "sVb", "sSb", "sGb", "sYst",
               "sWo", "sYT", "sOP", "sXre", "sRv", "sXN2v", "sSQ2", "sVAR2", "sSQT2",
               "sX1st", "sXN2st",
               "sX2T", "sFW", "sGU", "sSil", "sHA", "sR2", "sDW", "sDP", "sDE",
               "sXR1", "sOut"]:
        sems[nm] = Sem(nc, nm)
    s = sems
    BAR_ENG = 4
    cur_phase = [0]

    def bwait(eng):
        eng.wait_ge(s["bar"].h, BAR_ENG * cur_phase[0])

    def bdone(eng, drain=False):
        if drain:
            eng.drain()
        eng.sem_inc(s["bar"].h, 1)

    def next_phase():
        cur_phase[0] += 1

    top = ExitStack()

    def sb(stack, name, shape, dtype, **kw):
        return stack.enter_context(nc.sbuf_tensor(name, shape, dtype, **kw))

    def psum(stack, name, shape, dtype=F32):
        return stack.enter_context(nc.psum_tensor(name, shape, dtype))

    identf = sb(top, "identf", [128, 128], F32)
    maskT = sb(top, "maskTs", [128, 128], F32)
    onc_bf = sb(top, "onc_bfs", [128, 1], BF16)
    onc_f = sb(top, "onc_fs", [128, 1], F32)
    onr_f = sb(top, "onr_fs", [1, 128], F32)
    sg_col = sb(top, "sg_col", [128, 64], F32)
    su_col = sb(top, "su_col", [128, 64], F32)
    sd_bc = sb(top, "sd_bc", [128, H], F32)
    b2_bc = sb(top, "b2_bc", [128, 1], F32)
    p_col = sb(top, "p_col", [128, ST], F32)

    RING = 6

    def emit_prep(g, lo, hi):
        for ci in range(lo, hi):
            src_, dst_ = prep_chunks[ci]
            buf = wstg[ci % RING][:].rearrange("p (a h) -> p a h", a=src_.shape[1]) \
                if len(src_.shape) == 3 else wstg[ci % RING][:]
            if ci >= RING:
                g.wait_ge(s["sWS"].h, 16 * (ci - RING + 1))
            g.dma_start(buf, src_).then_inc(s["sWL"].h, 16)
            s["sWL"].n += 16
            g.wait_ge(s["sWL"].h, 16 * (ci + 1))
            g.dma_start(dst_, buf).then_inc(s["sWS"].h, 16)
            s["sWS"].n += 16

    attn = ExitStack()
    xnT = sb(attn, "xnT", [128, HT * S], BF16, is_transpose_dest=True)
    F_bf = sb(attn, "F_bf", [128, NH * NB * 128], BF16)
    Mp_bf = sb(attn, "Mp_bf", [128, NH * 129], BF16)
    norms_f = sb(attn, "norms_f", [128, 64], F32)
    norms_hm = sb(attn, "norms_hm", [128, 64], BF16)
    z_sb = sb(attn, "z_sb", [128, NH], F32)
    g_bc = sb(attn, "g_bc", [128, NH], F32)
    invg_bc = sb(attn, "invg_bc", [128, NH], F32)
    indz_bc = sb(attn, "indz_bc", [128, 1], F32)
    indb_row = sb(attn, "indb_row", [1, NB], F32)
    bsums = sb(attn, "bsums", [1, NB], F32)
    g_row = sb(attn, "g_row", [1, NH], F32)
    b1_col = sb(attn, "b1_col", [128, 4], F32)
    egw2_bf = sb(attn, "egw2_bf", [128, 4], BF16)
    rcw_all = sb(attn, "rcw_all", [128, NH * ST * NB], F32)

    SMALL_LOADS = 13

    ph = ExitStack()
    xbuf = [sb(ph, f"xb{i}", [128, H], F32) for i in range(2)]
    xnb = [sb(ph, f"xnb{i}", [128, H], BF16) for i in range(2)]
    junk = sb(ph, "junk", [128, H], BF16)
    ssq = [sb(ph, f"ssq{i}", [128, 1], F32) for i in range(2)]
    s1 = [sb(ph, f"s1_{i}", [128, 1], F32) for i in range(2)]
    vtmp = sb(ph, "vtmp", [128, 1], F32)
    mean2 = sb(ph, "mean2", [128, 1], F32)
    var = [sb(ph, f"var{i}", [128, 1], F32) for i in range(2)]
    sqv = [sb(ph, f"sqv{i}", [128, 1], F32) for i in range(2)]
    rstd = [sb(ph, f"rstd{i}", [128, 1], F32) for i in range(2)]
    nbias = [sb(ph, f"nbias{i}", [128, 1], F32) for i in range(2)]
    wstg = [sb(ph, f"wstg{i}", [128, 4096], BF16) for i in range(RING)]
    nrm_raw = sb(ph, "nrm_raw", [64, 128], F32)
    sgr = sb(ph, "sgr", [64, 128], F32)
    sur = sb(ph, "sur", [64, 128], F32)
    b1r = sb(ph, "b1r", [4, 128], F32)
    w2r = sb(ph, "w2r", [4, 128], F32)
    mg_row = sb(ph, "mg_row", [1, NH], F32)
    invg_row = sb(ph, "invg_row", [1, NH], F32)
    b2_row = sb(ph, "b2_row", [1, 1], F32)
    sd_row = sb(ph, "sd_row", [1, H], F32)
    indz_f = sb(ph, "indz_f", [1, 1], F32)
    zt = sb(ph, "zt", [1, 1], F32)
    fstage = [sb(ph, f"fstage{i}", [128, 512], F32) for i in range(2)]
    mstage = sb(ph, "mstage", [128, 128], F32)
    egw2f = sb(ph, "egw2f", [128, 4], F32)
    pA = psum(ph, "pA", [128, 64])
    pB = psum(ph, "pB", [128, 512])
    pC = psum(ph, "pC", [128, 16])
    pD = psum(ph, "pD", [1, 16])

    prep_chunks = []

    def rows2(w_in, w_bf, r):
        return (w_in[r * 256:(r + 1) * 256, :].rearrange("(a p) h -> p a h", p=128),
                w_bf[r * 256:(r + 1) * 256, :].rearrange("(a p) h -> p a h", p=128))
    for r in range(8):
        prep_chunks.append(rows2(wq_in, wq_bf, r))
    for r in range(8):
        prep_chunks.append(rows2(wk_in, wk_bf, r))
    for r in range(8):
        prep_chunks.append(rows2(wv_in, wv_bf, r))
    for r in range(8):
        prep_chunks.append(rows2(wo_in, wo_bf, r))
    for r in range(2):
        prep_chunks.append(rows2(egw1_in, egw1_bf, r))
    for r in range(32):
        prep_chunks.append(rows2(wg_in, wg_bf, r))
    for r in range(32):
        prep_chunks.append(rows2(wu_in, wu_bf, r))
    for r in range(16):
        for c in range(2):
            prep_chunks.append((wd_in[r * 128:(r + 1) * 128, c * 4096:(c + 1) * 4096],
                                wd_bf[r * 128:(r + 1) * 128, c * 4096:(c + 1) * 4096]))
    N_PREP = len(prep_chunks)
    PREP_WQ, PREP_WK, PREP_WV, PREP_WO, PREP_EG = 8, 16, 24, 32, 34

    with nc.Block("p0", no_gpsimd_drain=True) as blk:
        @blk.gpsimd
        def _(g):
            emit_prep(g, 0, N_PREP)

        @blk.sync
        def _(sy):
            for i in range(ST):
                if i >= 2:
                    sy.wait_ge(s["sXN"].h, i - 1)
                    sy.dma_start(xn_scr[(i - 2) * 128:(i - 1) * 128, :],
                                 xnb[i % 2][:]).then_inc(s["sXNst"].h, 16)
                    s["sXNst"].n += 16
                sy.dma_start(xbuf[i % 2][:], x_in[i * 128:(i + 1) * 128, :]
                             ).then_inc(s["sXL"].h, 16)
                s["sXL"].n += 16
            for i in range(ST - 2, ST):
                sy.wait_ge(s["sXN"].h, i + 1)
                sy.dma_start(xn_scr[i * 128:(i + 1) * 128, :],
                             xnb[i % 2][:]).then_inc(s["sXNst"].h, 16)
                s["sXNst"].n += 16
            bdone(sy)

        @blk.scalar
        def _(sc):
            small = [(ident_c, identf[:]), (maskT_c, maskT[:]),
                     (onc_bf_c, onc_bf[:]), (onc_f_c, onc_f[:]), (onr_f_c, onr_f[:]),
                     (fn_in.rearrange("m h d -> (m h) d"), nrm_raw[:]),
                     (sg_in.rearrange("(a b) -> a b", b=128), sgr[:]),
                     (su_in.rearrange("(a b) -> a b", b=128), sur[:]),
                     (egb1_in.rearrange("(a b) -> a b", b=128), b1r[:]),
                     (egw2_in.rearrange("a (b c) -> (a b) c", c=128), w2r[:]),
                     (mg_in.rearrange("(a b) -> a b", a=1), mg_row[:]),
                     (egb2_in.rearrange("(a b) -> a b", a=1), b2_row[:]),
                     (sd_in.rearrange("(a b) -> a b", a=1), sd_row[:])]
            assert len(small) == SMALL_LOADS
            for src, dst in small:
                sc.dma_start(dst, src).then_inc(s["sSm"].h, 16)
                s["sSm"].n += 16
            sc.wait_ge(s["sSm"].h, 16 * SMALL_LOADS)
            sc.activation(g_row[:], mg_row[:], AF.Sigmoid).then_inc(s["sS0"].h, 1)
            s["sS0"].n += 1
            for h in range(NH):
                if h >= 2:
                    sc.wait_ge(s["sFMV"].h, h - 1)
                for m in range(NB):
                    sc.dma_start(fstage[h % 2][:, m * 128:(m + 1) * 128],
                                 fm_in[m, h]).then_inc(s["sFM"].h, 16)
                    s["sFM"].n += 16
            for i in range(ST):
                sc.wait_ge(s["sXL"].h, 16 * (i + 1))
                if i >= 2:
                    sc.wait_ge(s["sVAR"].h, i - 1)
                sc.activation(junk[:], xbuf[i % 2][:], AF.Square,
                              accum_out=ssq[i % 2][:])
                sc.drain()
                sc.sem_inc(s["sSQ"].h, 1)
                s["sSQ"].n += 1
                sc.wait_ge(s["sVAR"].h, i + 1)
                sc.activation(sqv[i % 2][:], var[i % 2][:], AF.Sqrt
                              ).then_inc(s["sSQT"].h, 1)
                s["sSQT"].n += 1
            bdone(sc, drain=True)

        @blk.tensor
        def _(t):
            t.wait_ge(s["sSm"].h, 16 * SMALL_LOADS)
            t.transpose(pA[:], nrm_raw[:], identf[0:64, 0:64]).then_inc(s["sPE0"].h, 1)
            t.wait_ge(s["sV0"].h, 1)
            t.transpose(pA[:], sgr[:], identf[0:64, 0:64]).then_inc(s["sPE0"].h, 1)
            t.wait_ge(s["sV0"].h, 2)
            t.transpose(pA[:], sur[:], identf[0:64, 0:64]).then_inc(s["sPE0"].h, 1)
            t.wait_ge(s["sV0"].h, 3)
            t.transpose(pA[0:128, 0:4], b1r[:], identf[0:4, 0:4]).then_inc(s["sPE0"].h, 1)
            t.wait_ge(s["sV0"].h, 4)
            t.transpose(pA[0:128, 0:4], w2r[:], identf[0:4, 0:4]).then_inc(s["sPE0"].h, 1)
            for m in range(NB):
                t.wait_ge(s["sV0"].h, 5 + m)
                t.matmul(pD[:], onc_f[:], norms_f[:, m * 16:(m + 1) * 16],
                         start=True, stop=True).then_inc(s["sPE0"].h, 1)
            t.wait_ge(s["sV0"].h, 10)
            t.matmul(pC[:], onr_f[:], g_row[:], start=True, stop=True
                     ).then_inc(s["sPE0"].h, 1)
            t.wait_ge(s["sV0"].h, 11)
            t.matmul(pC[:], onr_f[:], invg_row[:], start=True, stop=True
                     ).then_inc(s["sPE0"].h, 1)
            t.wait_ge(s["sV0"].h, 12)
            t.matmul(pC[0:128, 0:1], onr_f[:], indz_f[:], start=True, stop=True
                     ).then_inc(s["sPE0"].h, 1)
            t.wait_ge(s["sV0"].h, 13)
            t.matmul(pC[0:128, 0:1], onr_f[:], b2_row[:], start=True, stop=True
                     ).then_inc(s["sPE0"].h, 1)
            for b in range(4):
                t.wait_ge(s["sV0"].h, 14 + b)
                t.matmul(pB[:], onr_f[:], sd_row[0:1, b * 512:(b + 1) * 512],
                         start=True, stop=True).then_inc(s["sPE0"].h, 1)
            s["sPE0"].n = 17
            bdone(t)

        @blk.vector
        def _(v):
            v.wait_ge(s["sPE0"].h, 1)
            v.tensor_copy(norms_f[:], pA[:])
            v.tensor_copy(norms_hm.rearrange("p (hh m) -> p hh m", m=NB),
                          pA.rearrange("p (m hh) -> p hh m", hh=NH))
            v.tensor_add(z_sb[:], norms_f[:, 0:16], norms_f[:, 16:32])
            v.drain()
            v.tensor_add(z_sb[:], z_sb[:], norms_f[:, 32:48])
            v.drain()
            v.tensor_add(z_sb[:], z_sb[:], norms_f[:, 48:64]).then_inc(s["sV0"].h, 1)
            v.wait_ge(s["sPE0"].h, 2)
            v.tensor_copy(sg_col[:], pA[:]).then_inc(s["sV0"].h, 1)
            v.wait_ge(s["sPE0"].h, 3)
            v.tensor_copy(su_col[:], pA[:]).then_inc(s["sV0"].h, 1)
            v.wait_ge(s["sPE0"].h, 4)
            v.tensor_copy(b1_col[:], pA[0:128, 0:4]).then_inc(s["sV0"].h, 1)
            v.wait_ge(s["sPE0"].h, 5)
            v.tensor_copy(egw2f[:], pA[0:128, 0:4])
            v.drain()
            v.tensor_copy(egw2_bf[:], egw2f[:]).then_inc(s["sV0"].h, 1)
            for m in range(NB):
                v.wait_ge(s["sPE0"].h, 6 + m)
                v.reduce_sum(bsums[0:1, m:m + 1], pD[:], AX.X).then_inc(s["sV0"].h, 1)
            v.drain()
            v.tensor_scalar(indb_row[:], bsums[:], EPS, None, ALU.is_ge)
            v.reduce_sum(zt[:], bsums[:], AX.X)
            v.drain()
            v.tensor_scalar(indz_f[:], zt[:], EPS, None, ALU.is_ge)
            v.drain()
            v.wait_ge(s["sS0"].h, 1)
            v.tensor_scalar(invg_row[:], g_row[:], -1.0, 1.0, ALU.mult, ALU.add
                            ).then_inc(s["sV0"].h, 1)
            v.wait_ge(s["sPE0"].h, 10)
            v.tensor_copy(g_bc[:], pC[:]).then_inc(s["sV0"].h, 1)
            v.wait_ge(s["sPE0"].h, 11)
            v.tensor_copy(invg_bc[:], pC[:]).then_inc(s["sV0"].h, 1)
            v.wait_ge(s["sPE0"].h, 12)
            v.tensor_copy(indz_bc[:], pC[0:128, 0:1]).then_inc(s["sV0"].h, 1)
            v.wait_ge(s["sPE0"].h, 13)
            v.tensor_copy(b2_bc[:], pC[0:128, 0:1]).then_inc(s["sV0"].h, 1)
            for b in range(4):
                v.wait_ge(s["sPE0"].h, 14 + b)
                v.tensor_copy(sd_bc[:, b * 512:(b + 1) * 512], pB[:]).then_inc(s["sV0"].h, 1)
            s["sV0"].n = 18
            for h in range(NH):
                v.wait_ge(s["sFM"].h, 16 * 4 * (h + 1))
                fs = fstage[h % 2]
                v.tensor_add(mstage[:], fs[:, 0:128], fs[:, 128:256])
                v.tensor_add(mstage[:], mstage[:], fs[:, 256:384])
                v.tensor_add(mstage[:], mstage[:], fs[:, 384:512])
                v.tensor_copy(Mp_bf[:, h * 129:h * 129 + 128], mstage[:])
                v.tensor_copy(Mp_bf[:, h * 129 + 128:h * 129 + 129], z_sb[:, h:h + 1])
                v.tensor_copy(F_bf[:, h * 512:(h + 1) * 512], fs[:]).then_inc(s["sFMV"].h, 1)
                s["sFMV"].n += 1
            for i in range(ST + 1):
                if i < ST:
                    v.wait_ge(s["sXL"].h, 16 * (i + 1))
                    v.reduce_sum(s1[i % 2][:], xbuf[i % 2][:], AX.X)
                if i >= 1:
                    j = i - 1
                    if i == ST:
                        v.drain()
                    v.scalar_tensor_tensor(nbias[j % 2][:], s1[j % 2][:], -1.0 / H,
                                           rstd[j % 2][:], ALU.mult, ALU.mult)
                if i < ST:
                    v.wait_ge(s["sSQ"].h, i + 1)
                    v.tensor_scalar(vtmp[:], ssq[i % 2][:], 1.0 / H, LN_EPS,
                                    ALU.mult, ALU.add)
                    v.scalar_tensor_tensor(mean2[:], s1[i % 2][:], 1.0 / (H * float(H)),
                                           s1[i % 2][:], ALU.mult, ALU.mult)
                if i >= 1:
                    j = i - 1
                    if j >= 2:
                        v.wait_ge(s["sXNst"].h, 16 * (j - 1))
                    if i == ST:
                        v.drain()
                    v.tensor_scalar(xnb[j % 2][:], xbuf[j % 2][:], rstd[j % 2][:, 0:1],
                                    nbias[j % 2][:, 0:1], ALU.mult, ALU.add
                                    ).then_inc(s["sXN"].h, 1)
                    s["sXN"].n += 1
                if i < ST:
                    if i == 0:
                        v.drain()
                    v.tensor_sub(var[i % 2][:], vtmp[:], mean2[:]).then_inc(s["sVAR"].h, 1)
                    s["sVAR"].n += 1
                    v.wait_ge(s["sSQT"].h, i + 1)
                    v.reciprocal(rstd[i % 2][:], sqv[i % 2][:])
            bdone(v, drain=True)
    ph.close()
    next_phase()

    CVH = ST + 2
    CNH = ST + 3

    def cv(h, what, c=0):
        base = h * CVH
        return {"nf": base + 1, "w": base + 2, "oc": base + 2 + c + 1}[what]

    def cn(h, what, c=0):
        base = h * CNH
        return {"nf": base + 1, "rel": base + 2, "wbc": base + 3, "co": base + 3 + c + 1}[what]

    ph = ExitStack()
    wqT = [sb(ph, f"wqT{i}", [128, HT * 512], BF16, is_transpose_dest=True) for i in range(2)]
    sqTg = [sb(ph, f"sqTg{i}", [128, 4 * S], BF16) for i in range(2)]
    ocb = sb(ph, "ocb", [128, S], BF16)
    nf_sb = sb(ph, "nf_sb", [128, NB * ST], F32)
    rc_sb = sb(ph, "rc_sb", [128, NB * ST], F32)
    rel_r = sb(ph, "rel_r", [1, NB], F32)
    rel_m = sb(ph, "rel_m", [1, 1], F32)
    rel_e = sb(ph, "rel_e", [1, NB], F32)
    rel_x = sb(ph, "rel_x", [1, NB], F32)
    rel_s = sb(ph, "rel_s", [1, 1], F32)
    w_row = sb(ph, "w_row", [1, NB], F32)
    w_pre = sb(ph, "w_pre", [1, NB], F32)
    wbc = sb(ph, "wbc", [128, NB], F32)
    ebuf = [sb(ph, f"ebuf{i}", [128, 512], F32) for i in range(2)]
    rbuf = [sb(ph, f"rbuf{i}", [128, 512], F32) for i in range(2)]
    nct = [sb(ph, f"nct{i}", [128, 1], F32) for i in range(2)]
    recc = [sb(ph, f"recc{i}", [128, 1], F32) for i in range(2)]
    pQ = [psum(ph, f"pQ{i}", [128, 512]) for i in range(2)]
    pAC = [psum(ph, f"pAC{i}", [128, 129]) for i in range(2)]
    pNF = psum(ph, "pNF", [128, NB * ST])
    pREL = psum(ph, "pREL", [1, NB])
    pWBC = psum(ph, "pWBC", [128, NB])

    with nc.Block("a2a", no_gpsimd_drain=True) as blk:
        @blk.sync
        def _(sy):
            bwait(sy)
            sy.wait_ge(s["sXNst"].h, 16 * ST)
            for ht in range(HT):
                for j in range(NSUP):
                    sy.dma_start(xnT[:, ht * S + j * 512: ht * S + (j + 1) * 512],
                                 xn_scr[j * 512:(j + 1) * 512, ht * 128:(ht + 1) * 128],
                                 transpose=True).then_inc(s["sXnT"].h, 16)
                    s["sXnT"].n += 16
            sy.wait_ge(s["sWS"].h, 16 * PREP_WQ)
            def load_wqT(es):
                for ht in range(HT):
                    sy.dma_start(wqT[es % 2][:, ht * 512:(ht + 1) * 512],
                                 wq_bf[es * 512:(es + 1) * 512, ht * 128:(ht + 1) * 128],
                                 transpose=True).then_inc(s["sWq"].h, 16)
                    s["sWq"].n += 16
            load_wqT(0)
            load_wqT(1)
            for es in range(4):
                for hl in range(4):
                    h = es * 4 + hl
                    sy.wait_ge(s["sCV"].h, cv(h, "oc", ST - 1))
                    sy.dma_start(
                        oc_scr[:, h * 128:(h + 1) * 128].rearrange("(c p) e -> p c e", p=128),
                        ocb.rearrange("p (c e) -> p c e", e=128)).then_inc(s["sOCst"].h, 16)
                    s["sOCst"].n += 16
                sy.wait_ge(s["sQE"].h, (es + 1) * 4 * NSUP)
                sy.dma_start(
                    sq_scr[es * 512:(es + 1) * 512, :].rearrange("(el p) t -> p el t", p=128),
                    sqTg[es % 2].rearrange("p (el t) -> p el t", el=4)).then_inc(s["sSQstr"].h, 16)
                s["sSQstr"].n += 16
                if es + 2 < 4:
                    sy.wait_ge(s["sQP"].h, (es + 1) * 4 * NSUP)
                    load_wqT(es + 2)
            bdone(sy)

        @blk.tensor
        def _(t):
            bwait(t)
            t.wait_ge(s["sXnT"].h, 16 * HT * NSUP)
            for es in range(4):
                t.wait_ge(s["sWq"].h, 16 * HT * (es + 1))
                if es >= 2:
                    t.wait_ge(s["sSQstr"].h, 16 * (es - 1))
                for el in range(4):
                    for j in range(NSUP):
                        gq = es * 4 * NSUP + el * NSUP + j
                        pq = pQ[gq % 2]
                        if gq >= 2:
                            t.wait_ge(s["sQX"].h, gq - 1)
                        mm = None
                        for ht in range(HT):
                            mm = t.matmul(pq[:],
                                          wqT[es % 2][:, ht * 512 + el * 128: ht * 512 + (el + 1) * 128],
                                          xnT[:, ht * S + j * 512: ht * S + (j + 1) * 512],
                                          start=(ht == 0), stop=(ht == HT - 1))
                        mm.then_inc(s["sQP"].h, 1)
                        s["sQP"].n += 1
                for hl in range(4):
                    h = es * 4 + hl
                    t.wait_ge(s["sQE"].h, es * 4 * NSUP + (hl + 1) * NSUP)
                    if h >= 1:
                        t.wait_ge(s["sCV"].h, cv(h - 1, "w"))
                    mm = None
                    for c in range(ST):
                        mm = t.matmul(pNF[:, c * NB:(c + 1) * NB],
                                      sqTg[es % 2][:, hl * S + c * 128:hl * S + (c + 1) * 128],
                                      norms_hm[:, h * NB:(h + 1) * NB],
                                      start=True, stop=True)
                    mm.then_inc(s["sCN"].h, 1)
                    s["sCN"].n += 1
                    t.wait_ge(s["sCV"].h, cv(h, "nf"))
                    mm = None
                    for c in range(ST):
                        mm = t.matmul(pREL[:], onc_f[:], nf_sb[:, c * NB:(c + 1) * NB],
                                      start=(c == 0), stop=(c == ST - 1))
                    mm.then_inc(s["sCN"].h, 1)
                    s["sCN"].n += 1
                    t.wait_ge(s["sCV"].h, cv(h, "w"))
                    t.matmul(pWBC[:], onr_f[:], w_row[:], start=True, stop=True
                             ).then_inc(s["sCN"].h, 1)
                    s["sCN"].n += 1
                    for c in range(ST):
                        k = h * ST + c
                        if k >= 2:
                            h2, c2 = (k - 2) // ST, (k - 2) % ST
                            t.wait_ge(s["sCV"].h, cv(h2, "oc", c2))
                        t.matmul(pAC[c % 2][:],
                                 sqTg[es % 2][:, hl * S + c * 128:hl * S + (c + 1) * 128],
                                 Mp_bf[:, h * 129:(h + 1) * 129],
                                 start=True, stop=True).then_inc(s["sCN"].h, 1)
                        s["sCN"].n += 1
            bdone(t)

        @blk.scalar
        def _(sc):
            bwait(sc)
            for es in range(4):
                for el in range(4):
                    for j in range(NSUP):
                        gq = es * 4 * NSUP + el * NSUP + j
                        pq = pQ[gq % 2]
                        sc.wait_ge(s["sQP"].h, gq + 1)
                        if gq >= 2:
                            sc.wait_ge(s["sQE"].h, gq - 1)
                        sc.activation(ebuf[gq % 2][:], pq[:], AF.Exp)
                        sc.activation(rbuf[gq % 2][:], pq[:], AF.Relu).then_inc(s["sQX"].h, 1)
                        s["sQX"].n += 1
                for hl in range(4):
                    h = es * 4 + hl
                    sc.wait_ge(s["sRe"].h, h + 1)
                    sc.activation(rel_x[:], rel_e[:], AF.Exp).then_inc(s["sSx"].h, 1)
                    s["sSx"].n += 1
            bdone(sc, drain=True)

        @blk.vector
        def _(v):
            bwait(v)
            for es in range(4):
                for el in range(4):
                    for j in range(NSUP):
                        gq = es * 4 * NSUP + el * NSUP + j
                        v.wait_ge(s["sQX"].h, gq + 1)
                        v.scalar_tensor_tensor(
                            sqTg[es % 2][:, (el * S + j * 512):(el * S + (j + 1) * 512)],
                            ebuf[gq % 2][:], 1.0, rbuf[gq % 2][:], ALU.min, ALU.add
                        ).then_inc(s["sQE"].h, 1)
                        s["sQE"].n += 1
                for hl in range(4):
                    h = es * 4 + hl
                    v.wait_ge(s["sCN"].h, cn(h, "nf"))
                    v.tensor_copy(nf_sb[:], pNF[:]).then_inc(s["sCV"].h, 1)
                    s["sCV"].n += 1
                    v.wait_ge(s["sCN"].h, cn(h, "rel"))
                    v.tensor_scalar(rel_r[:], pREL[:], 1.0 / S, None, ALU.mult)
                    v.tensor_scalar(rc_sb[:], nf_sb[:], EPS, None, ALU.max)
                    v.reduce_max(rel_m[:], rel_r[:], AX.X)
                    v.reciprocal(rc_sb[:], rc_sb[:])
                    v.tensor_single_scalar(rel_e[:], rel_r[:], rel_m[0:1, 0:1], ALU.subtract
                                           ).then_inc(s["sRe"].h, 1)
                    s["sRe"].n += 1
                    v.wait_ge(s["sSx"].h, h + 1)
                    v.tensor_mul(w_pre[:], rel_x[:], indb_row[:])
                    v.reduce_sum(rel_s[:], rel_x[:], AX.X)
                    v.drain()
                    v.reciprocal(rel_s[:], rel_s[:])
                    v.drain()
                    v.tensor_single_scalar(w_row[:], w_pre[:], rel_s[0:1, 0:1], ALU.mult
                                           ).then_inc(s["sCV"].h, 1)
                    s["sCV"].n += 1
                    v.wait_ge(s["sCN"].h, cn(h, "wbc"))
                    v.tensor_copy(wbc[:], pWBC[:])
                    v.drain()
                    rc3 = rc_sb.rearrange("p (c m) -> p c m", m=NB)
                    rw3 = rcw_all.rearrange("p (hh c m) -> p hh c m", hh=NH, m=NB)
                    for m in range(NB):
                        v.tensor_scalar(rw3[:, h, :, m], rc3[:, :, m], wbc[:, m:m + 1],
                                        None, ALU.mult)
                    if h >= 1:
                        v.wait_ge(s["sOCst"].h, 16 * h)
                    for c in range(ST + 1):
                        if c < ST:
                            v.wait_ge(s["sCN"].h, cn(h, "co", c))
                            v.tensor_scalar(nct[c % 2][:], pAC[c % 2][:, 128:129], EPS,
                                            None, ALU.max)
                        if c >= 1:
                            if c == ST:
                                v.drain()
                            v.tensor_scalar(ocb[:, (c - 1) * 128:c * 128],
                                            pAC[(c - 1) % 2][:, 0:128],
                                            recc[(c - 1) % 2][:, 0:1], indz_bc[:, 0:1],
                                            ALU.mult, ALU.mult).then_inc(s["sCV"].h, 1)
                            s["sCV"].n += 1
                        if c < ST:
                            if c == 0:
                                v.drain()
                            v.reciprocal(recc[c % 2][:], nct[c % 2][:])
            bdone(v, drain=True)
    ph.close()
    next_phase()

    ph = ExitStack()
    egw1T = sb(ph, "egw1T", [128, 16 * 512], BF16, is_transpose_dest=True)
    crT = [sb(ph, f"crT{i}", [128, 16 * 512], BF16, is_transpose_dest=True) for i in range(2)]
    hid_bf = sb(ph, "hid_bf", [128, 4 * 512], BF16)
    pHh = [psum(ph, f"pHh{i}", [128, 512]) for i in range(2)]
    pGP = psum(ph, "pGP", [128, 4])

    def hid_sgs(k_h):
        return (k_h // 4) * 8 + (k_h % 4) + 1

    with nc.Block("gprob", no_gpsimd_drain=True) as blk:
        @blk.sync
        def _(sy):
            bwait(sy)
            sy.wait_ge(s["sOCst"].h, 16 * NH)
            sy.wait_ge(s["sSQstr"].h, 16 * 4)
            sy.wait_ge(s["sWS"].h, 16 * PREP_EG)
            for et in range(HT):
                sy.dma_start(egw1T[:, et * 512:(et + 1) * 512],
                             egw1_bf[0:512, et * 128:(et + 1) * 128],
                             transpose=True).then_inc(s["sGW"].h, 16)
                s["sGW"].n += 16
            sy.wait_ge(s["sOCst"].h, 16 * NH)
            for j in range(NSUP):
                if j >= 2:
                    sy.wait_ge(s["sGPE"].h, (j - 1) * 8)
                for et in range(HT):
                    sy.dma_start(crT[j % 2][:, et * 512:(et + 1) * 512],
                                 oc_scr[j * 512:(j + 1) * 512, et * 128:(et + 1) * 128],
                                 transpose=True).then_inc(s["sCrT"].h, 16)
                    s["sCrT"].n += 16
            bdone(sy)

        @blk.tensor
        def _(t):
            bwait(t)
            t.wait_ge(s["sGW"].h, 16 * HT)
            for j in range(NSUP):
                t.wait_ge(s["sCrT"].h, 16 * HT * (j + 1))
                for ft in range(4):
                    k_h = j * 4 + ft
                    if k_h >= 2:
                        t.wait_ge(s["sGS"].h, hid_sgs(k_h - 2))
                    mm = None
                    for et in range(HT):
                        mm = t.matmul(pHh[ft % 2][:],
                                      egw1T[:, et * 512 + ft * 128: et * 512 + (ft + 1) * 128],
                                      crT[j % 2][:, et * 512:(et + 1) * 512],
                                      start=(et == 0), stop=(et == HT - 1))
                    mm.then_inc(s["sGPE"].h, 1)
                    s["sGPE"].n += 1
                t.wait_ge(s["sGS"].h, j * 8 + 4)
                for cc in range(4):
                    mm = None
                    for ft in range(4):
                        mm = t.matmul(pGP[:, cc:cc + 1],
                                      hid_bf[:, ft * 512 + cc * 128: ft * 512 + (cc + 1) * 128],
                                      egw2_bf[:, ft:ft + 1],
                                      start=(ft == 0), stop=(ft == 3))
                    mm.then_inc(s["sGPE"].h, 1)
                    s["sGPE"].n += 1
            bdone(t)

        @blk.scalar
        def _(sc):
            bwait(sc)
            for j in range(NSUP):
                for ft in range(4):
                    sc.wait_ge(s["sGPE"].h, j * 8 + ft + 1)
                    sc.activation(hid_bf[:, ft * 512:(ft + 1) * 512], pHh[ft % 2][:],
                                  AF.Relu, bias=b1_col[:, ft:ft + 1]).then_inc(s["sGS"].h, 1)
                    s["sGS"].n += 1
                for cc in range(4):
                    c = j * 4 + cc
                    sc.wait_ge(s["sGPE"].h, j * 8 + 4 + cc + 1)
                    sc.activation(p_col[:, c:c + 1], pGP[:, cc:cc + 1], AF.Sigmoid,
                                  bias=b2_bc[:, 0:1]).then_inc(s["sGS"].h, 1)
                    s["sGS"].n += 1
            bdone(sc, drain=True)

        @blk.vector
        def _(v):
            bwait(v)
            bdone(v)
    ph.close()
    next_phase()

    for grp in range(4):
        ph = ExitStack()
        wkvT = sb(ph, f"wkvT{grp}", [128, HT * 512], BF16, is_transpose_dest=True)
        wkT = wkvT
        wvT = wkvT
        sqTb = sb(ph, f"sqTb{grp}", [128, 4 * S], BF16)
        skTg = sb(ph, f"skTg{grp}", [128, 4 * S], BF16)
        vg1 = sb(ph, f"vg1{grp}", [128, ST * 4 * 129], BF16)
        skp = sb(ph, f"skp{grp}", [128, 2 * S], BF16, is_transpose_dest=True)
        ocl = sb(ph, f"ocl{grp}", [128, 4 * 128], BF16)
        ybt = [sb(ph, f"ybt{grp}_{i}", [128, 128], BF16) for i in range(4)]
        S_sb = [sb(ph, f"S_sb{grp}_{a}", [128, 129], F32) for a in range(2)]
        S_bfs = [sb(ph, f"S_bfs{grp}_{a}", [128, 129], BF16) for a in range(2)]
        ebuf2 = sb(ph, f"eb2_{grp}", [128, 512], F32)
        rbuf2 = sb(ph, f"rb2_{grp}", [128, 512], F32)
        lo = [sb(ph, f"lo{grp}_{a}", [128, 128], F32) for a in range(2)]
        of = [sb(ph, f"of{grp}_{a}", [128, 128], F32) for a in range(2)]
        ocf = [sb(ph, f"ocf{grp}_{a}", [128, 128], F32) for a in range(2)]
        t1b = [sb(ph, f"t1b{grp}_{a}", [128, 128], F32) for a in range(2)]
        memb = [sb(ph, f"memb{grp}_{a}", [128, 128], F32) for a in range(2)]
        t2b = [sb(ph, f"t2b{grp}_{a}", [128, 128], F32) for a in range(2)]
        mst = [sb(ph, f"mst{grp}_{a}", [128, 128], BF16) for a in range(2)]
        dent = [sb(ph, f"dent{grp}_{a}", [128, 1], F32) for a in range(2)]
        recl = [sb(ph, f"recl{grp}_{a}", [128, 1], F32) for a in range(2)]
        pSC = [psum(ph, f"pSC{grp}_{a}", [128, 128]) for a in range(2)]
        pNM = [psum(ph, f"pNM{grp}_{a}", [128, 129]) for a in range(2)]
        pDS = [psum(ph, f"pDS{grp}_{a}", [128, 129]) for a in range(2)]
        pFN = [psum(ph, f"pFN{grp}_{a}", [128, 512]) for a in range(2)]
        pKV = pFN

        NPRJ = 4 * NSUP + ST
        kp0 = s["sKP"].n
        kx0 = s["sKX"].n
        ke0 = s["sKE"].n
        ocld0 = s["sOCld"].n
        skp0 = s["sSkp"].n
        skst0 = s["sSKst"].n
        yst0 = s["sYst"].n

        def vtile(c, hl):
            return vg1[:, (c * 4 + hl) * 129:(c * 4 + hl) * 129 + 129]

        def vtile_v(c, hl):
            return vg1[:, (c * 4 + hl) * 129:(c * 4 + hl) * 129 + 128]

        ORDER = [(2 * pair + a, c) for pair in range(2) for c in range(ST)
                 for a in range(2)]
        pe_d, vb_d, sb_d, g_d = {}, {}, {}, {}
        pe_n, vb_n, sb_n, g_n = s["sPEb"].n, s["sVb"].n, s["sSb"].n, s["sGb"].n
        for pair in range(2):
            for c in range(ST):
                for a in range(2):
                    hl = 2 * pair + a
                    pe_n += 1; pe_d[(hl, c, "sc")] = pe_n
                    pe_n += 1; pe_d[(hl, c, "fn")] = pe_n
                for a in range(2):
                    hl = 2 * pair + a
                    pe_n += 1; pe_d[(hl, c, "num")] = pe_n
                    pe_n += 1; pe_d[(hl, c, "ds")] = pe_n
        for pair in range(2):
            for c in range(ST):
                for a in range(2):
                    hl = 2 * pair + a
                    vb_n += 1; vb_d[(hl, c, "mst")] = vb_n
                for a in range(2):
                    hl = 2 * pair + a
                    vb_n += 1; vb_d[(hl, c, "lo")] = vb_n
                    vb_n += 1; vb_d[(hl, c, "of")] = vb_n
                for a in range(2):
                    hl = 2 * pair + a
                    vb_n += 1; vb_d[(hl, c, "sadd")] = vb_n
                for a in range(2):
                    hl = 2 * pair + a
                    vb_n += 1; vb_d[(hl, c, "memb")] = vb_n
                    vb_n += 1; vb_d[(hl, c, "y")] = vb_n
        for pair in range(2):
            for c in range(ST):
                for a in range(2):
                    hl = 2 * pair + a
                    if c < ST - 1:
                        sb_n += 1; sb_d[(hl, c)] = sb_n
        for q, (hl, c) in enumerate(ORDER):
            g_n += 1; g_d[(hl, c, "t1")] = g_n
            g_n += 1; g_d[(hl, c, "t2")] = g_n

        with nc.Block(f"a2b{grp}", no_gpsimd_drain=True) as blk:
            @blk.sync
            def _(sy):
                bwait(sy)
                if grp > 0:
                    sy.wait_ge(s["sPEb"].h, s["sPEb"].n)
                    sy.wait_ge(s["sYst"].h, yst0)
                sy.wait_ge(s["sWS"].h, 16 * PREP_WV)
                for ht in range(HT):
                    sy.dma_start(wkT[:, ht * 512:(ht + 1) * 512],
                                 wk_bf[grp * 512:(grp + 1) * 512, ht * 128:(ht + 1) * 128],
                                 transpose=True).then_inc(s["sKVw"].h, 16)
                    s["sKVw"].n += 16
                sy.wait_ge(s["sSQstr"].h, 16 * (grp + 1))
                for el in range(4):
                    sy.dma_start(sqTb[:, el * S:(el + 1) * S],
                                 sq_scr[grp * 512 + el * 128:grp * 512 + (el + 1) * 128, :]
                                 ).then_inc(s["sKVw"].h, 16)
                    s["sKVw"].n += 16
                sy.wait_ge(s["sKP"].h, kp0 + 4 * NSUP)
                for ht in range(HT):
                    sy.dma_start(wvT[:, ht * 512:(ht + 1) * 512],
                                 wv_bf[grp * 512:(grp + 1) * 512, ht * 128:(ht + 1) * 128],
                                 transpose=True).then_inc(s["sKVw"].h, 16)
                    s["sKVw"].n += 16
                sy.wait_ge(s["sKE"].h, ke0 + 4 * NSUP)
                sy.dma_start(
                    skT_scr[grp * 512:(grp + 1) * 512, :].rearrange("(el p) t -> p el t", p=128),
                    skTg.rearrange("p (el t) -> p el t", el=4)).then_inc(s["sSKst"].h, 16)
                s["sSKst"].n += 16
                sy.wait_ge(s["sSKst"].h, skst0 + 16)

                def load_skp(hl):
                    if hl >= 2:
                        sy.wait_ge(s["sPEb"].h, pe_d[(hl - 2, ST - 1, "ds")])
                    for c in range(ST):
                        sy.dma_start(
                            skp[:, ((hl % 2) * ST + c) * 128:((hl % 2) * ST + c + 1) * 128],
                            skT_scr[(grp * 4 + hl) * 128:(grp * 4 + hl + 1) * 128,
                                    c * 128:(c + 1) * 128],
                            transpose=True).then_inc(s["sSkp"].h, 16)
                        s["sSkp"].n += 16
                load_skp(0)
                load_skp(1)
                for q, (hl, c) in enumerate(ORDER):
                    if q == 2 * ST:
                        load_skp(2)
                        load_skp(3)
                    sy.wait_ge(s["sVb"].h, vb_d[(hl, c, "y")])
                    h = grp * 4 + hl
                    sy.dma_start(y_scr[c * 128:(c + 1) * 128, h * 128:(h + 1) * 128],
                                 ybt[q % 4][:]).then_inc(s["sYst"].h, 16)
                    s["sYst"].n += 16
                bdone(sy)

            @blk.scalar
            def _(sc):
                bwait(sc)
                if grp > 0:
                    sc.wait_ge(s["sGb"].h, s["sGb"].n)
                for el in range(4):
                    for j in range(NSUP):
                        gq = el * NSUP + j
                        sc.wait_ge(s["sKP"].h, kp0 + gq + 1)
                        if gq >= 1:
                            sc.wait_ge(s["sKE"].h, ke0 + gq)
                        sc.activation(ebuf2[:], pKV[gq % 2][:], AF.Exp)
                        sc.activation(rbuf2[:], pKV[gq % 2][:], AF.Relu
                                      ).then_inc(s["sKX"].h, 1)
                        s["sKX"].n += 1
                for q0 in range(min(4, len(ORDER))):
                    hl0, c0 = ORDER[q0]
                    h0 = grp * 4 + hl0
                    sc.dma_start(ocl[:, (q0 % 4) * 128:((q0 % 4) + 1) * 128],
                                 oc_scr[c0 * 128:(c0 + 1) * 128, h0 * 128:(h0 + 1) * 128]
                                 ).then_inc(s["sOCld"].h, 16)
                    s["sOCld"].n += 16
                for q, (hl, c) in enumerate(ORDER):
                    if c < ST - 1:
                        sc.wait_ge(s["sVb"].h, vb_d[(hl, c, "sadd")])
                        sc.copy(S_bfs[hl % 2][:], S_sb[hl % 2][:]).then_inc(s["sSb"].h, 1)
                        s["sSb"].n += 1
                    qn = q + 4
                    if qn < len(ORDER):
                        hln, cn_ = ORDER[qn]
                        hn = grp * 4 + hln
                        sc.wait_ge(s["sGb"].h, g_d[(hl, c, "t1")])
                        sc.dma_start(ocl[:, (qn % 4) * 128:((qn % 4) + 1) * 128],
                                     oc_scr[cn_ * 128:(cn_ + 1) * 128,
                                            hn * 128:(hn + 1) * 128]
                                     ).then_inc(s["sOCld"].h, 16)
                        s["sOCld"].n += 16
                bdone(sc, drain=True)

            @blk.tensor
            def _(t):
                bwait(t)
                t.wait_ge(s["sKVw"].h, 16 * ((2 * HT + 4) * grp + HT + 4))
                for el in range(4):
                    for j in range(NSUP):
                        gq = el * NSUP + j
                        pq = pKV[gq % 2]
                        if gq >= 2:
                            t.wait_ge(s["sKX"].h, kx0 + gq - 1)
                        mm = None
                        for ht in range(HT):
                            mm = t.matmul(pq[:],
                                          wkT[:, ht * 512 + el * 128: ht * 512 + (el + 1) * 128],
                                          xnT[:, ht * S + j * 512: ht * S + (j + 1) * 512],
                                          start=(ht == 0), stop=(ht == HT - 1))
                        mm.then_inc(s["sKP"].h, 1)
                        s["sKP"].n += 1
                t.wait_ge(s["sKVw"].h, 16 * (2 * HT + 4) * (grp + 1))
                for c in range(ST):
                    gq = 4 * NSUP + c
                    pq = pKV[gq % 2]
                    t.wait_ge(s["sKE"].h, ke0 + gq - 1)
                    mm = None
                    for ht in range(HT):
                        mm = t.matmul(pq[:],
                                      xnT[:, ht * S + c * 128: ht * S + (c + 1) * 128],
                                      wvT[:, ht * 512:(ht + 1) * 512],
                                      start=(ht == 0), stop=(ht == HT - 1))
                    mm.then_inc(s["sKP"].h, 1)
                    s["sKP"].n += 1
                t.wait_ge(s["sKE"].h, ke0 + NPRJ)
                for pair in range(2):
                    t.wait_ge(s["sSkp"].h, skp0 + 16 * ST * (2 * pair + 2))
                    for c in range(ST):
                        for a in range(2):
                            hl = 2 * pair + a
                            h = grp * 4 + hl
                            sqt = sqTb[:, hl * S + c * 128:hl * S + (c + 1) * 128]
                            if c >= 1:
                                t.wait_ge(s["sVb"].h, vb_d[(hl, c - 1, "mst")])
                            elif pair == 1:
                                t.wait_ge(s["sVb"].h, vb_d[(hl - 2, ST - 1, "mst")])
                            t.matmul(pSC[a][:], skTg[:, hl * S + c * 128:hl * S + (c + 1) * 128],
                                     sqt, start=True, stop=True).then_inc(s["sPEb"].h, 1)
                            s["sPEb"].n += 1
                            if c >= 1:
                                t.wait_ge(s["sVb"].h, vb_d[(hl, c - 1, "of")])
                            elif pair == 1:
                                t.wait_ge(s["sVb"].h, vb_d[(hl - 2, ST - 1, "of")])
                            mm = None
                            for m in range(NB):
                                mm = t.matmul(pFN[a][:, m * 128:(m + 1) * 128], sqt,
                                              F_bf[:, h * 512 + m * 128: h * 512 + (m + 1) * 128],
                                              start=True, stop=True)
                            mm.then_inc(s["sPEb"].h, 1)
                            s["sPEb"].n += 1
                        for a in range(2):
                            hl = 2 * pair + a
                            sqt = sqTb[:, hl * S + c * 128:hl * S + (c + 1) * 128]
                            t.wait_ge(s["sVb"].h, vb_d[(hl, c, "mst")])
                            if c >= 1:
                                t.wait_ge(s["sVb"].h, vb_d[(hl, c - 1, "lo")])
                            elif pair == 1:
                                t.wait_ge(s["sVb"].h, vb_d[(hl - 2, ST - 1, "lo")])
                            if c == 0:
                                mm = t.matmul(pNM[a][:], mst[a][:], vtile(c, hl),
                                              start=True, stop=True)
                            else:
                                t.matmul(pNM[a][:], mst[a][:], vtile(c, hl),
                                         start=True, stop=False)
                                t.wait_ge(s["sSb"].h, sb_d[(hl, c - 1)])
                                mm = t.matmul(pNM[a][:], sqt, S_bfs[a][:],
                                              start=False, stop=True)
                            mm.then_inc(s["sPEb"].h, 1)
                            s["sPEb"].n += 1
                            if c >= 1:
                                t.wait_ge(s["sVb"].h, vb_d[(hl, c - 1, "sadd")])
                            elif pair == 1:
                                t.wait_ge(s["sVb"].h, vb_d[(hl - 2, ST - 1, "sadd")])
                            t.matmul(pDS[a][:],
                                     skp[:, ((hl % 2) * ST + c) * 128:((hl % 2) * ST + c + 1) * 128],
                                     vtile(c, hl), start=True, stop=True
                                     ).then_inc(s["sPEb"].h, 1)
                            s["sPEb"].n += 1
                bdone(t)

            @blk.vector
            def _(v):
                bwait(v)
                v.memset(vg1.rearrange("p (t e) -> p t e", e=129)[:, :, 128:129], 1.0)
                for el in range(4):
                    for j in range(NSUP):
                        gq = el * NSUP + j
                        v.wait_ge(s["sKX"].h, kx0 + gq + 1)
                        v.scalar_tensor_tensor(
                            skTg[:, (el * S + j * 512):(el * S + (j + 1) * 512)],
                            ebuf2[:], 1.0, rbuf2[:], ALU.min, ALU.add
                        ).then_inc(s["sKE"].h, 1)
                        s["sKE"].n += 1
                for c in range(ST):
                    gq = 4 * NSUP + c
                    v.wait_ge(s["sKP"].h, kp0 + gq + 1)
                    for hl in range(4):
                        ic = v.tensor_copy(vtile_v(c, hl),
                                           pKV[gq % 2][:, hl * 128:(hl + 1) * 128])
                    ic.then_inc(s["sKE"].h, 1)
                    s["sKE"].n += 1
                for pair in range(2):
                    if pair >= 1 and ST >= 2:
                        v.wait_ge(s["sSb"].h, sb_d[(2 * pair - 1, ST - 2)])
                    v.memset(S_sb[0][:], 0.0)
                    v.memset(S_sb[1][:], 0.0)
                    for c in range(ST):
                        for a in range(2):
                            hl = 2 * pair + a
                            v.wait_ge(s["sPEb"].h, pe_d[(hl, c, "sc")])
                            if c >= 1:
                                v.wait_ge(s["sPEb"].h, pe_d[(hl, c - 1, "num")])
                            elif pair == 1:
                                v.wait_ge(s["sPEb"].h, pe_d[(hl - 2, ST - 1, "num")])
                            v.tensor_tensor(mst[a][:], pSC[a][:], maskT[:], ALU.mult
                                            ).then_inc(s["sVb"].h, 1)
                            s["sVb"].n += 1
                        for a in range(2):
                            hl = 2 * pair + a
                            h = grp * 4 + hl
                            base = h * ST * NB + c * NB
                            v.wait_ge(s["sPEb"].h, pe_d[(hl, c, "num")])
                            v.tensor_scalar(dent[a][:], pNM[a][:, 128:129], EPS, None, ALU.max)
                            v.tensor_scalar(of[a][:], pFN[a][:, 0:128],
                                            rcw_all[:, base:base + 1], None, ALU.mult)
                            v.reciprocal(recl[a][:], dent[a][:])
                            v.scalar_tensor_tensor(of[a][:], pFN[a][:, 128:256],
                                                   rcw_all[:, base + 1:base + 2], of[a][:],
                                                   ALU.mult, ALU.add)
                            v.tensor_scalar(lo[a][:], pNM[a][:, 0:128], recl[a][:, 0:1],
                                            None, ALU.mult).then_inc(s["sVb"].h, 1)
                            s["sVb"].n += 1
                            v.scalar_tensor_tensor(of[a][:], pFN[a][:, 256:384],
                                                   rcw_all[:, base + 2:base + 3], of[a][:],
                                                   ALU.mult, ALU.add)
                            v.scalar_tensor_tensor(of[a][:], pFN[a][:, 384:512],
                                                   rcw_all[:, base + 3:base + 4], of[a][:],
                                                   ALU.mult, ALU.add).then_inc(s["sVb"].h, 1)
                            s["sVb"].n += 1
                        for a in range(2):
                            hl = 2 * pair + a
                            v.wait_ge(s["sPEb"].h, pe_d[(hl, c, "ds")])
                            v.tensor_tensor(S_sb[a][:], pDS[a][:], S_sb[a][:], ALU.add
                                            ).then_inc(s["sVb"].h, 1)
                            s["sVb"].n += 1
                        for a in range(2):
                            hl = 2 * pair + a
                            h = grp * 4 + hl
                            q = ORDER.index((hl, c))
                            v.wait_ge(s["sGb"].h, g_d[(hl, c, "t1")])
                            v.scalar_tensor_tensor(memb[a][:], t1b[a][:], p_col[:, c:c + 1],
                                                   ocf[a][:], ALU.mult, ALU.add
                                                   ).then_inc(s["sVb"].h, 1)
                            s["sVb"].n += 1
                            v.wait_ge(s["sGb"].h, g_d[(hl, c, "t2")])
                            if q >= 4:
                                v.wait_ge(s["sYst"].h, yst0 + 16 * (q - 3))
                            v.scalar_tensor_tensor(ybt[q % 4][:], t2b[a][:],
                                                   g_bc[:, h:h + 1], lo[a][:], ALU.mult, ALU.add
                                                   ).then_inc(s["sVb"].h, 1)
                            s["sVb"].n += 1
                bdone(v, drain=True)

            @blk.gpsimd
            def _(g):
                for q, (hl, c) in enumerate(ORDER):
                    a = hl % 2
                    g.wait_ge(s["sOCld"].h, ocld0 + 16 * (q + 1))
                    g.wait_ge(s["sVb"].h, vb_d[(hl, c, "of")])
                    if q >= 2:
                        hp, cp = ORDER[q - 2]
                        g.wait_ge(s["sVb"].h, vb_d[(hp, cp, "memb")])
                    g.tensor_copy(ocf[a][:], ocl[:, (q % 4) * 128:((q % 4) + 1) * 128])
                    g.tensor_sub(t1b[a][:], of[a][:], ocf[a][:]).then_inc(s["sGb"].h, 1)
                    s["sGb"].n += 1
                    g.wait_ge(s["sVb"].h, vb_d[(hl, c, "memb")])
                    g.wait_ge(s["sVb"].h, vb_d[(hl, c, "lo")])
                    if q >= 1:
                        g.wait_ge(s["sVb"].h, vb_d[(ORDER[q - 1][0], ORDER[q - 1][1], "y")])
                    g.tensor_sub(t2b[a][:], memb[a][:], lo[a][:]).then_inc(s["sGb"].h, 1)
                    s["sGb"].n += 1
        ph.close()
        next_phase()
    attn.close()

    ph = ExitStack()
    woT = sb(ph, "woT", [128, HT * 2048], BF16, is_transpose_dest=True)
    yT = [sb(ph, f"yT{i}", [128, HT * 512], BF16, is_transpose_dest=True) for i in range(2)]
    xre = [sb(ph, f"xre{i}", [128, H], F32) for i in range(2)]
    x1t = [sb(ph, f"x1t{i}", [128, H], F32) for i in range(2)]
    xn2t = [sb(ph, f"xn2t{i}", [128, H], BF16) for i in range(2)]
    junk2 = sb(ph, "junk2", [128, H], BF16)
    ssq2 = [sb(ph, f"ssq2{i}", [128, 1], F32) for i in range(2)]
    s1b = [sb(ph, f"s1b{i}", [128, 1], F32) for i in range(2)]
    vtmp2 = sb(ph, "vtmp2", [128, 1], F32)
    mean2b = sb(ph, "mean2b", [128, 1], F32)
    varb = [sb(ph, f"varb{i}", [128, 1], F32) for i in range(2)]
    sqvb = [sb(ph, f"sqvb{i}", [128, 1], F32) for i in range(2)]
    rstdb = [sb(ph, f"rstdb{i}", [128, 1], F32) for i in range(2)]
    nbiasb = [sb(ph, f"nbiasb{i}", [128, 1], F32) for i in range(2)]
    pO = [psum(ph, f"pO{i}", [128, 2048]) for i in range(2)]

    with nc.Block("a3", no_gpsimd_drain=True) as blk:
        @blk.sync
        def _(sy):
            bwait(sy)
            sy.wait_ge(s["sWS"].h, 16 * N_PREP)
            sy.wait_ge(s["sYst"].h, 16 * NH * ST)
            sy.wait_ge(s["sSKst"].h, s["sSKst"].n)
            for et in range(HT):
                for hob in range(4):
                    sy.dma_start(woT[:, et * 2048 + hob * 512: et * 2048 + (hob + 1) * 512],
                                 wo_bf[hob * 512:(hob + 1) * 512, et * 128:(et + 1) * 128],
                                 transpose=True).then_inc(s["sWo"].h, 16)
                    s["sWo"].n += 16
            def load_yT(j):
                if j >= 2:
                    sy.wait_ge(s["sOP"].h, (j - 1) * 4)
                for et in range(HT):
                    sy.dma_start(yT[j % 2][:, et * 512:(et + 1) * 512],
                                 y_scr[j * 512:(j + 1) * 512, et * 128:(et + 1) * 128],
                                 transpose=True).then_inc(s["sYT"].h, 16)
                    s["sYT"].n += 16
            load_yT(0)
            if NSUP > 1:
                load_yT(1)
            for i in range(ST):
                sy.wait_ge(s["sRv"].h, i + 1)
                sy.wait_ge(s["sXN2v"].h, i + 1)
                sy.dma_start(x1_scr[i * 128:(i + 1) * 128, :], x1t[i % 2][:]
                             ).then_inc(s["sX1st"].h, 16)
                s["sX1st"].n += 16
                sy.dma_start(xn2_scr[i * 128:(i + 1) * 128, :], xn2t[i % 2][:]
                             ).then_inc(s["sXN2st"].h, 16)
                s["sXN2st"].n += 16
                if i % 4 == 3 and (i // 4) + 2 < NSUP:
                    load_yT((i // 4) + 2)
            bdone(sy)

        @blk.scalar
        def _(sc):
            bwait(sc)
            for i in range(min(2, ST)):
                sc.dma_start(xre[i % 2][:], x_in[i * 128:(i + 1) * 128, :]
                             ).then_inc(s["sXre"].h, 16)
                s["sXre"].n += 16
            for i in range(ST):
                sc.wait_ge(s["sRv"].h, i + 1)
                if i >= 2:
                    sc.wait_ge(s["sVAR2"].h, i - 1)
                sc.activation(junk2[:], x1t[i % 2][:], AF.Square,
                              accum_out=ssq2[i % 2][:])
                sc.drain()
                sc.sem_inc(s["sSQ2"].h, 1)
                s["sSQ2"].n += 1
                sc.wait_ge(s["sVAR2"].h, i + 1)
                sc.activation(sqvb[i % 2][:], varb[i % 2][:], AF.Sqrt
                              ).then_inc(s["sSQT2"].h, 1)
                s["sSQT2"].n += 1
                if i + 2 < ST:
                    sc.dma_start(xre[i % 2][:], x_in[(i + 2) * 128:(i + 3) * 128, :]
                                 ).then_inc(s["sXre"].h, 16)
                    s["sXre"].n += 16
            bdone(sc, drain=True)

        @blk.tensor
        def _(t):
            bwait(t)
            t.wait_ge(s["sWo"].h, 16 * HT * 4)
            for j in range(NSUP):
                t.wait_ge(s["sYT"].h, 16 * HT * (j + 1))
                for cc in range(4):
                    c = j * 4 + cc
                    if c >= 2:
                        t.wait_ge(s["sRv"].h, c - 1)
                    mm = None
                    for et in range(HT):
                        for hob in range(4):
                            mm = t.matmul(pO[c % 2][:, hob * 512:(hob + 1) * 512],
                                          yT[j % 2][:, et * 512 + cc * 128: et * 512 + (cc + 1) * 128],
                                          woT[:, et * 2048 + hob * 512: et * 2048 + (hob + 1) * 512],
                                          start=(et == 0), stop=(et == HT - 1))
                    mm.then_inc(s["sOP"].h, 1)
                    s["sOP"].n += 1
            bdone(t)

        @blk.vector
        def _(v):
            bwait(v)
            for i in range(ST + 1):
                if i < ST:
                    v.wait_ge(s["sOP"].h, i + 1)
                    v.wait_ge(s["sXre"].h, 16 * (i + 1))
                    if i >= 2:
                        v.wait_ge(s["sX1st"].h, 16 * (i - 1))
                        v.wait_ge(s["sSQ2"].h, i - 1)
                        v.wait_ge(s["sXN2v"].h, i - 1)
                    v.tensor_tensor(x1t[i % 2][:], pO[i % 2][:], xre[i % 2][:], ALU.add
                                    ).then_inc(s["sRv"].h, 1)
                    s["sRv"].n += 1
                    v.reduce_sum(s1b[i % 2][:], x1t[i % 2][:], AX.X)
                if i >= 1:
                    j = i - 1
                    if i == ST:
                        v.drain()
                    v.scalar_tensor_tensor(nbiasb[j % 2][:], s1b[j % 2][:], -1.0 / H,
                                           rstdb[j % 2][:], ALU.mult, ALU.mult)
                if i < ST:
                    v.wait_ge(s["sSQ2"].h, i + 1)
                    v.tensor_scalar(vtmp2[:], ssq2[i % 2][:], 1.0 / H, LN_EPS,
                                    ALU.mult, ALU.add)
                    v.scalar_tensor_tensor(mean2b[:], s1b[i % 2][:], 1.0 / (H * float(H)),
                                           s1b[i % 2][:], ALU.mult, ALU.mult)
                if i >= 1:
                    j = i - 1
                    if j >= 2:
                        v.wait_ge(s["sXN2st"].h, 16 * (j - 1))
                    if i == ST:
                        v.drain()
                    v.tensor_scalar(xn2t[j % 2][:], x1t[j % 2][:], rstdb[j % 2][:, 0:1],
                                    nbiasb[j % 2][:, 0:1], ALU.mult, ALU.add
                                    ).then_inc(s["sXN2v"].h, 1)
                    s["sXN2v"].n += 1
                if i < ST:
                    if i == 0:
                        v.drain()
                    v.tensor_sub(varb[i % 2][:], vtmp2[:], mean2b[:]).then_inc(s["sVAR2"].h, 1)
                    s["sVAR2"].n += 1
                    v.wait_ge(s["sSQT2"].h, i + 1)
                    v.reciprocal(rstdb[i % 2][:], sqvb[i % 2][:])
            bdone(v, drain=True)
    ph.close()
    next_phase()

    ph = ExitStack()
    xn2T = sb(ph, "xn2T", [128, HT * 512], BF16, is_transpose_dest=True)
    hact = sb(ph, "hact", [128, IT * 512], BF16)
    wgT = [sb(ph, f"wgT{i}", [128, HT * 512], BF16, is_transpose_dest=True) for i in range(2)]
    wuT = [sb(ph, f"wuT{i}", [128, HT * 512], BF16, is_transpose_dest=True) for i in range(2)]
    wdT = [sb(ph, f"wdT{i}", [128, 512], BF16, is_transpose_dest=True) for i in range(4)]
    gs_t = [sb(ph, f"gs_t{i}", [128, 512], F32) for i in range(2)]
    x1l = sb(ph, "x1l", [128, 4 * 2048], F32)
    rr = sb(ph, "rr", [128, 4], F32)
    outb = [sb(ph, f"outb{i}", [128, 512], F32) for i in range(2)]
    pGf = [psum(ph, f"pGf{i}", [128, 512]) for i in range(2)]
    pUf = psum(ph, "pUf", [128, 512])
    pDN = [psum(ph, f"pDN{i}", [128, 512]) for i in range(4)]
    pR2 = psum(ph, "pR2", [128, 4])

    with nc.Block("ffn", no_gpsimd_drain=True) as blk:
        @blk.sync
        def _(sy):
            bwait(sy)
            sy.wait_ge(s["sWS"].h, 16 * N_PREP)
            sy.wait_ge(s["sX1st"].h, 16 * ST)
            sy.wait_ge(s["sXN2st"].h, 16 * ST)
            for sbk in range(NFS):
                if sbk >= 1:
                    sy.wait_ge(s["sGU"].h, sbk * 2 * IT)
                for ht in range(HT):
                    sy.dma_start(xn2T[:, ht * 512:(ht + 1) * 512],
                                 xn2_scr[sbk * 512:(sbk + 1) * 512, ht * 128:(ht + 1) * 128],
                                 transpose=True).then_inc(s["sX2T"].h, 16)
                    s["sX2T"].n += 16
                for isup in range(16):
                    gi = sbk * 16 + isup
                    if gi >= 2:
                        sy.wait_ge(s["sGU"].h, (gi - 1) * 8)
                    for ht in range(HT):
                        sy.dma_start(wgT[gi % 2][:, ht * 512:(ht + 1) * 512],
                                     wg_bf[isup * 512:(isup + 1) * 512, ht * 128:(ht + 1) * 128],
                                     transpose=True).then_inc(s["sFW"].h, 16)
                        s["sFW"].n += 16
                    for ht in range(HT):
                        sy.dma_start(wuT[gi % 2][:, ht * 512:(ht + 1) * 512],
                                     wu_bf[isup * 512:(isup + 1) * 512, ht * 128:(ht + 1) * 128],
                                     transpose=True).then_inc(s["sFW"].h, 16)
                        s["sFW"].n += 16
                for hob in range(4):
                    for it in range(IT):
                        k = (sbk * 4 + hob) * IT + it
                        if k >= 4:
                            sy.wait_ge(s["sDP"].h, k - 3)
                        sy.dma_start(wdT[it % 4][:],
                                     wd_bf[hob * 512:(hob + 1) * 512, it * 128:(it + 1) * 128],
                                     transpose=True).then_inc(s["sDW"].h, 16)
                        s["sDW"].n += 16
            bdone(sy)

        @blk.scalar
        def _(sc):
            bwait(sc)
            for sbk in range(NFS):
                sc.wait_ge(s["sX1st"].h, 16 * 4 * (sbk + 1))
                if sbk >= 1:
                    sc.wait_ge(s["sDE"].h, sbk * 16)
                for ss in range(4):
                    sc.dma_start(x1l[:, ss * 2048:(ss + 1) * 2048],
                                 x1_scr[(sbk * 4 + ss) * 128:(sbk * 4 + ss + 1) * 128, :]
                                 ).then_inc(s["sXR1"].h, 16)
                    s["sXR1"].n += 16
                for it in range(IT):
                    gi = sbk * IT + it
                    sc.wait_ge(s["sGU"].h, gi * 2 + 1)
                    if gi >= 2:
                        sc.wait_ge(s["sHA"].h, gi - 1)
                    sc.activation(gs_t[it % 2][:], pGf[gi % 2][:], AF.Silu,
                                  scale=sg_col[:, it:it + 1]).then_inc(s["sSil"].h, 1)
                    s["sSil"].n += 1
                for hob in range(4):
                    for ss in range(4):
                        k = hob * 4 + ss
                        sc.wait_ge(s["sDE"].h, sbk * 16 + k + 1)
                        sc.dma_start(out[(sbk * 4 + ss) * 128:(sbk * 4 + ss + 1) * 128,
                                         hob * 512:(hob + 1) * 512],
                                     outb[k % 2][:]).then_inc(s["sOut"].h, 16)
                        s["sOut"].n += 16
            sc.wait_ge(s["sOut"].h, s["sOut"].n)
            bdone(sc, drain=True)

        @blk.tensor
        def _(t):
            bwait(t)
            for sbk in range(NFS):
                t.wait_ge(s["sX2T"].h, 16 * HT * (sbk + 1))
                for it in range(IT):
                    gi = sbk * IT + it
                    isup, itl = it // 4, it % 4
                    t.wait_ge(s["sFW"].h, 16 * 2 * HT * (sbk * 16 + isup + 1))
                    if gi >= 2:
                        t.wait_ge(s["sSil"].h, gi - 1)
                    mm = None
                    for ht in range(HT):
                        mm = t.matmul(pGf[gi % 2][:],
                                      wgT[(sbk * 16 + isup) % 2][:, ht * 512 + itl * 128: ht * 512 + (itl + 1) * 128],
                                      xn2T[:, ht * 512:(ht + 1) * 512],
                                      start=(ht == 0), stop=(ht == HT - 1))
                    mm.then_inc(s["sGU"].h, 1)
                    s["sGU"].n += 1
                    if gi >= 1:
                        t.wait_ge(s["sHA"].h, gi)
                    mm = None
                    for ht in range(HT):
                        mm = t.matmul(pUf[:],
                                      wuT[(sbk * 16 + isup) % 2][:, ht * 512 + itl * 128: ht * 512 + (itl + 1) * 128],
                                      xn2T[:, ht * 512:(ht + 1) * 512],
                                      start=(ht == 0), stop=(ht == HT - 1))
                    mm.then_inc(s["sGU"].h, 1)
                    s["sGU"].n += 1
                t.wait_ge(s["sHA"].h, (sbk + 1) * IT)
                for ss in range(4):
                    mm = None
                    for it in range(IT):
                        mm = t.matmul(pR2[:, ss:ss + 1],
                                      hact[:, it * 512 + ss * 128: it * 512 + (ss + 1) * 128],
                                      onc_bf[:], start=(it == 0), stop=(it == IT - 1))
                    mm.then_inc(s["sR2"].h, 1)
                    s["sR2"].n += 1
                for hob in range(4):
                    for it in range(IT):
                        k = (sbk * 4 + hob) * IT + it
                        t.wait_ge(s["sDW"].h, 16 * (k + 1))
                        if hob >= 1 and it == 0:
                            t.wait_ge(s["sDE"].h, sbk * 16 + hob * 4)
                        mm = None
                        for ss in range(4):
                            mm = t.matmul(pDN[ss][:],
                                          hact[:, it * 512 + ss * 128: it * 512 + (ss + 1) * 128],
                                          wdT[it % 4][:],
                                          start=(it == 0), stop=(it == IT - 1))
                        mm.then_inc(s["sDP"].h, 1)
                        s["sDP"].n += 1
                if sbk < NFS - 1:
                    t.wait_ge(s["sDE"].h, (sbk + 1) * 16)
            bdone(t)

        @blk.vector
        def _(v):
            bwait(v)
            for sbk in range(NFS):
                for it in range(IT):
                    gi = sbk * IT + it
                    v.wait_ge(s["sSil"].h, gi + 1)
                    v.wait_ge(s["sGU"].h, gi * 2 + 2)
                    v.scalar_tensor_tensor(hact[:, it * 512:(it + 1) * 512],
                                           pUf[:], su_col[:, it:it + 1],
                                           gs_t[it % 2][:], ALU.mult, ALU.mult
                                           ).then_inc(s["sHA"].h, 1)
                    s["sHA"].n += 1
                v.wait_ge(s["sR2"].h, (sbk + 1) * 4)
                v.tensor_scalar(rr[:], pR2[:], 128.0, None, ALU.mult)
                v.drain()
                for hob in range(4):
                    v.wait_ge(s["sDP"].h, (sbk * 4 + hob + 1) * IT)
                    v.wait_ge(s["sXR1"].h, 16 * 4 * (sbk + 1))
                    for ss in range(4):
                        k = hob * 4 + ss
                        if sbk * 16 + k >= 2:
                            v.wait_ge(s["sOut"].h, 16 * (sbk * 16 + k - 1))
                        v.scalar_tensor_tensor(outb[k % 2][:], pDN[ss][:], rr[:, ss:ss + 1],
                                               sd_bc[:, hob * 512:(hob + 1) * 512],
                                               ALU.subtract, ALU.mult)
                        v.tensor_tensor(outb[k % 2][:], outb[k % 2][:],
                                        x1l[:, ss * 2048 + hob * 512: ss * 2048 + (hob + 1) * 512],
                                        ALU.add).then_inc(s["sDE"].h, 1)
                        s["sDE"].n += 1
            bdone(v, drain=True)
    ph.close()
    top.close()
    return nc


_NC_CACHE = {}


def get_nc(S=2048):
    if S not in _NC_CACHE:
        _NC_CACHE[S] = build_nc(S)
    return _NC_CACHE[S]


def kernel(**inputs):
    x = np.ascontiguousarray(inputs["L0_attn_hidden_states"])
    nc = get_nc(2048)
    common = {
        "wq": np.ascontiguousarray(inputs["L0_attn_w_q"]),
        "wk": np.ascontiguousarray(inputs["L0_attn_w_k"]),
        "wv": np.ascontiguousarray(inputs["L0_attn_w_v"]),
        "wo": np.ascontiguousarray(inputs["L0_attn_w_o"]),
        "egw1": np.ascontiguousarray(inputs["L0_attn_eg_w1"]),
        "egb1": np.ascontiguousarray(inputs["L0_attn_eg_b1"]),
        "egw2": np.ascontiguousarray(inputs["L0_attn_eg_w2"]),
        "egb2": np.ascontiguousarray(inputs["L0_attn_eg_b2"]),
        "mgate": np.ascontiguousarray(inputs["L0_attn_memory_gate"]),
        "fmem": np.ascontiguousarray(inputs["L0_attn_fine_memories"]),
        "fnorm": np.ascontiguousarray(inputs["L0_attn_fine_memory_norms"]),
        "gw": np.ascontiguousarray(inputs["L0_ffn_gate_w"]),
        "gs": np.ascontiguousarray(inputs["L0_ffn_gate_scale"]),
        "uw": np.ascontiguousarray(inputs["L0_ffn_up_w"]),
        "us": np.ascontiguousarray(inputs["L0_ffn_up_scale"]),
        "dw": np.ascontiguousarray(inputs["L0_ffn_down_w"]),
        "ds": np.ascontiguousarray(inputs["L0_ffn_down_scale"]),
    }
    in_maps = []
    for b in range(NCORES):
        m = dict(common)
        m["x"] = np.ascontiguousarray(x[b])
        in_maps.append(m)
    res = run_bass_kernel_spmd(nc, in_maps, core_ids=list(range(NCORES)))
    return np.stack([res.results[b]["out"] for b in range(NCORES)], axis=0)
